# version 02
# baseline (speedup 1.0000x reference)
import numpy as np
import ml_dtypes
import concourse.bass as bass
import concourse.mybir as mybir
from concourse.bass_utils import run_bass_kernel_spmd

F32 = mybir.dt.float32
BF16 = mybir.dt.bfloat16
I32 = mybir.dt.int32
U16 = mybir.dt.uint16
U8 = mybir.dt.uint8
AF = mybir.ActivationFunctionType
Alu = mybir.AluOpType
AXX = mybir.AxisListType.X
bf = ml_dtypes.bfloat16

D, S, NB = 2048, 4096, 4
NQH, NKV, HD = 16, 4, 128
MLP, NW, TOWN, NT = 16384, 64, 2048, 16
ISQ = 1.0 / float(np.sqrt(HD))
GA = 0.7978845608028654
GB = GA * 0.044715
EPS = 1e-5
_PERM = np.concatenate([np.arange(0, D, 2), np.arange(1, D, 2)])


def _host_stage(inputs):
    x = np.asarray(inputs["L0_attn_x"], dtype=np.float32)
    Wq = np.asarray(inputs["L0_attn_Wq"], dtype=np.float32)
    Wk = np.asarray(inputs["L0_attn_Wk"], dtype=np.float32)
    Wv = np.asarray(inputs["L0_attn_Wv"], dtype=np.float32)
    Wo = np.asarray(inputs["L0_attn_Wo"], dtype=np.float32)
    gp = np.asarray(inputs["L0_ffn_gate_W_packed"], dtype=np.int32)
    gs = np.asarray(inputs["L0_ffn_gate_scales"], dtype=np.float32)
    up = np.asarray(inputs["L0_ffn_up_W_packed"], dtype=np.int32)
    us = np.asarray(inputs["L0_ffn_up_scales"], dtype=np.float32)
    dp = np.asarray(inputs["L0_ffn_down_W_packed"], dtype=np.int32)
    ds = np.asarray(inputs["L0_ffn_down_scales"], dtype=np.float32)

    dperm = np.concatenate([np.arange(0, HD, 2), np.arange(1, HD, 2)])
    qcol = (np.arange(NQH)[:, None] * HD + dperm[None, :]).reshape(-1)
    kcol = (np.arange(NKV)[:, None] * HD + dperm[None, :]).reshape(-1)
    wq_h = np.ascontiguousarray(Wq.T[_PERM][:, qcol])
    wk_h = np.ascontiguousarray(Wk.T[_PERM][:, kcol])
    wv_h = np.ascontiguousarray(Wv.T[_PERM, :])
    wo_h = np.ascontiguousarray(Wo.T)

    jperm = np.concatenate([np.arange(0, 256, 2), np.arange(1, 256, 2)])

    def block_packed(P):
        PT = P.T
        kp, n = PT.shape
        blk = PT.reshape(8, kp // 8, n // 256, 256).transpose(2, 1, 0, 3)
        blk = blk[:, :, :, jperm]
        return np.ascontiguousarray(blk.reshape(n // 256, kp // 8, 2048))

    pg_h = block_packed(gp)
    pu_h = block_packed(up)
    pd_h = np.ascontiguousarray(dp.T.reshape(64, 128, 2048))

    def block_scales(Sc):
        blk = Sc.T.reshape(8, 8, 64, 256).transpose(2, 1, 0, 3)
        blk = blk[:, :, :, jperm]
        return np.ascontiguousarray(blk.reshape(64, 8, 2048))

    sg_h = block_scales(gs)
    su_h = block_scales(us)
    sd_h = np.ascontiguousarray(ds.T.reshape(64, 8, 2048))

    idn_h = np.eye(128, dtype=bf)
    b8_h = np.zeros((8, 128), dtype=bf)
    for i in range(8):
        b8_h[i, i * 16:(i + 1) * 16] = 1.0
    freqs = (1.0 / 10000.0 ** (np.arange(0, HD, 2) / HD)).astype(np.float64)
    ang = np.arange(S, dtype=np.float64)[:, None] * freqs[None, :]
    cosa = np.cos(ang).astype(np.float32)
    sina = np.sin(ang).astype(np.float32)
    cosk_h = np.ascontiguousarray(
        cosa.reshape(32, 128, 64).transpose(1, 0, 2).reshape(128, 2048))
    sink_h = np.ascontiguousarray(
        sina.reshape(32, 128, 64).transpose(1, 0, 2).reshape(128, 2048))

    maps = []
    for c in range(8):
        b_, par = c // 2, c % 2
        own = np.arange(par, 32, 2)
        xb = x[b_]
        xown = np.ascontiguousarray(xb.reshape(32, 128, D)[own].reshape(TOWN, D))
        cosq_h = np.ascontiguousarray(
            cosa.reshape(32, 128, 64)[own].transpose(1, 0, 2).reshape(128, 1024))
        sinq_h = np.ascontiguousarray(
            sina.reshape(32, 128, 64)[own].transpose(1, 0, 2).reshape(128, 1024))
        dmask_h = np.zeros((NT, 128, 1024), dtype=np.float32)
        for o in range(NT):
            g = 2 * o + par
            nch = (o + 2) // 2
            Wn = min(2, nch)
            qglob = 128 * g + np.arange(128)
            for w2 in range(Wn):
                cidx = nch - Wn + w2
                kglob = 512 * cidx + np.arange(512)
                dmask_h[o, :, 512 * w2:512 * (w2 + 1)] = np.where(
                    kglob[None, :] <= qglob[:, None], 0.0, -1e30)
        maps.append(dict(
            xseq=np.ascontiguousarray(xb), xown=xown,
            wq=wq_h, wk=wk_h, wv=wv_h, wo=wo_h,
            pg=pg_h, pu=pu_h, pd=pd_h, sg=sg_h, su=su_h, sd=sd_h,
            cosk=cosk_h, sink=sink_h, cosq=cosq_h, sinq=sinq_h,
            dmask=dmask_h, idn=idn_h, b8=b8_h))
    return maps


def nch_of(o):
    return (o + 2) // 2


class Sem:
    def __init__(self, h):
        self.h = h
        self.v = 0

    def base(self):
        return self.v


def build_program(taps=False, detect_races=True):
    nc = bass.Bass("TRN2", target_bir_lowering=False,
                   detect_race_conditions=detect_races)
    din = {}
    for nm, shp, dt_ in [
            ("xseq", [S, D], F32), ("xown", [TOWN, D], F32),
            ("wq", [D, D], F32), ("wk", [D, 512], F32), ("wv", [D, 512], F32),
            ("wo", [D, D], F32),
            ("pg", [NW, 128, 2048], I32), ("pu", [NW, 128, 2048], I32),
            ("pd", [NW, 128, 2048], I32),
            ("sg", [NW, 8, 2048], F32), ("su", [NW, 8, 2048], F32),
            ("sd", [NW, 8, 2048], F32),
            ("cosk", [128, 2048], F32), ("sink", [128, 2048], F32),
            ("cosq", [128, 1024], F32), ("sinq", [128, 1024], F32),
            ("dmask", [NT, 128, 1024], F32),
            ("idn", [128, 128], BF16), ("b8", [8, 128], BF16)]:
        din[nm] = nc.dram_tensor(nm, shp, dt_, kind="ExternalInput")
    out = nc.dram_tensor("out", [TOWN, D], F32, kind="ExternalOutput")
    tapd = {}
    if taps:
        for nm in ("t_xnt", "t_kt", "t_v", "t_q", "t_g", "t_at"):
            tapd[nm] = nc.dram_tensor(nm, [128, 16384], BF16,
                                      kind="ExternalOutput")
        for nm in ("t_int", "t_wd", "t_wg"):
            tapd[nm] = nc.dram_tensor(nm, [128, 2048], BF16,
                                      kind="ExternalOutput")
    kvsp = nc.dram_tensor("kvsp", [128, 32768], BF16, kind="Internal")
    intsp = nc.dram_tensor("intsp", [NW, 2, 128, 2048], BF16, kind="Internal")
    wdsp = nc.dram_tensor("wdsp", [NW, 2, 128, 2048], BF16, kind="Internal")

    from contextlib import ExitStack
    ctx = ExitStack()
    with ctx:
        s_idn = ctx.enter_context(nc.sbuf_tensor("s_idn", [128, 128], BF16))
        s_b8 = ctx.enter_context(nc.sbuf_tensor("s_b8", [8, 128], BF16))
        s_cosq = ctx.enter_context(nc.sbuf_tensor("s_cosq", [128, 1024], F32))
        s_sinq = ctx.enter_context(nc.sbuf_tensor("s_sinq", [128, 1024], F32))
        stat = ctx.enter_context(nc.sbuf_tensor("stat", [128, 32], F32))
        mxc = ctx.enter_context(nc.sbuf_tensor("mxc", [128, 8], F32))
        smc = ctx.enter_context(nc.sbuf_tensor("smc", [128, 8], F32))
        nfc = ctx.enter_context(nc.sbuf_tensor("nfc", [128, 8], F32))
        csum = ctx.enter_context(nc.sbuf_tensor("csum", [128, 8], F32))
        gnbt = ctx.enter_context(nc.sbuf_tensor("gnbt", [128, 4], F32))
        jnk = ctx.enter_context(nc.sbuf_tensor("jnk", [128, 8], F32))
        ajk = ctx.enter_context(nc.sbuf_tensor("ajk", [128, 8], F32))
        AR = ctx.enter_context(nc.sbuf_tensor("AR", [128, 188 * 1024], U8))

        def V(off_kb, size_kb, dt_):
            return AR[:, int(off_kb * 1024):int((off_kb + size_kb) * 1024)
                      ].bitcast(dt_)

        def sv(i):
            return stat[:, i:i + 1]

        v_kt = V(0, 32, BF16)
        v_xnt = V(0, 32, BF16)
        v_v = V(32, 32, BF16)
        v_pgb = [V(32, 8, I32), V(40, 8, I32)]
        v_pub = [V(48, 8, I32), V(56, 8, I32)]
        v_wkb = V(64, 16, BF16)
        v_wvb = V(80, 16, BF16)
        v_wge = [V(64, 4, BF16), V(68, 4, BF16)]
        v_wgo = [V(72, 4, BF16), V(76, 4, BF16)]
        v_wue = [V(80, 4, BF16), V(84, 4, BF16)]
        v_wuo = [V(88, 4, BF16), V(92, 4, BF16)]
        v_q = V(64, 32, BF16)
        v_wob = [V(64, 16, BF16), V(80, 16, BF16)]
        v_xs = [V(96, 8, F32), V(104, 8, F32)]
        v_xsq = V(112, 8, F32)
        v_xn = V(120, 4, BF16)
        v_krope = V(124, 1, BF16)
        v_rt1 = V(125, 1, F32)
        v_rt2 = V(126, 1, F32)
        v_scg = [V(96, 4, BF16), V(100, 4, BF16)]
        v_scu = [V(104, 4, BF16), V(108, 4, BF16)]
        v_sgb = [V(112, 4, BF16), V(116, 4, BF16)]
        v_sub = [V(120, 4, BF16), V(124, 4, BF16)]
        v_itb = [V(96, 4, BF16), V(100, 4, BF16)]
        v_wdb = [V(104, 2, BF16), V(106, 2, BF16)]
        v_wqb = [V(96, 16, BF16), V(112, 16, BF16)]
        v_p = [V(96, 8, BF16), V(104, 8, BF16)]
        v_ptb = V(112, 1, BF16)
        v_qt = [V(113, 0.25, BF16), V(113.25, 0.25, BF16)]
        v_mkb = [V(114, 4, F32), V(118, 4, F32)]
        v_art1 = V(160, 1, F32)
        v_art2 = V(161, 1, F32)
        v_xob = [V(96, 8, F32), V(104, 8, F32)]
        v_outb = [V(112, 8, F32), V(120, 8, F32)]
        v_g = V(128, 32, BF16)
        v_pdb = V(128, 8, I32)
        v_wde = V(136, 4, BF16)
        v_wdo = V(140, 4, BF16)
        v_scd = V(144, 4, BF16)
        v_sdb = V(148, 4, BF16)
        v_gsb = V(152, 2, BF16)
        v_upsb = V(154, 2, BF16)
        v_gt1 = V(156, 2, BF16)
        v_gt2 = V(158, 2, BF16)
        v_clo = V(160, 4, U16)
        v_chi = V(164, 4, U16)
        v_tm = V(168, 4, U16)
        v_tA = V(172, 4, U16)
        v_tB = V(176, 4, U16)
        v_cosk = V(160, 8, F32)
        v_sink = V(168, 8, F32)
        v_xntc = V(180, 4, BF16)
        v_intb = [V(180, 4, BF16), V(184, 4, BF16)]

        pA = ctx.enter_context(nc.psum_tensor("pA", [128, 2048], F32))
        pB = ctx.enter_context(nc.psum_tensor("pB", [128, 2048], F32))
        pAb = pA[...].bitcast(BF16)
        pBb = pB[...].bitcast(BF16)

        SM = {}
        for nm in ("ld", "lp", "sp", "pe", "pe2", "pe3", "dv", "dv2", "dv3",
                   "ac", "ac2", "ac3", "os",
                   "xs0", "xs1", "pg0", "pg1", "pu0", "pu1", "pdl",
                   "sg0", "sg1", "su0", "su1", "sdl", "it0", "it1",
                   "wd0", "wd1", "wq0", "wq1", "mk0", "mk1",
                   "at0", "at1", "xo0", "xo1", "wo0", "wo1",
                   "spi0", "spi1", "spw", "spa", "os0", "os1", "ktv"):
            SM[nm] = Sem(ctx.enter_context(nc.semaphore("sm_" + nm)))

        def dma_fence(e, bs):
            for nm in ("sp", "spi0", "spi1", "spw", "spa", "os0", "os1"):
                if bs[nm] > 0:
                    e.wait_ge(SM[nm].h, bs[nm])

        def decode_emit(v, pkap, w_lo, w_hi, scb):
            pv = pkap.bitcast(U16)[:, 0:4096:2]
            v.tensor_scalar(v_clo[...], pv, 15, None, Alu.bitwise_and)
            v.tensor_scalar(v_chi[...], pv, 4, 15, Alu.logical_shift_right,
                            Alu.bitwise_and)
            last = None
            for csrc, wdst in ((v_clo, w_lo), (v_chi, w_hi)):
                v.tensor_scalar(v_tm[...], csrc[...], 7, None, Alu.bitwise_and)
                v.tensor_scalar(v_tA[...], v_tm[...], 64, 16128, Alu.mult,
                                Alu.add)
                v.tensor_scalar(v_tB[...], v_tm[...], 4, 16128, Alu.min,
                                Alu.mult)
                v.tensor_tensor(v_tB[...], v_tA[...], v_tB[...], Alu.min)
                v.tensor_scalar(v_tm[...], csrc[...], 8, 12, Alu.bitwise_and,
                                Alu.logical_shift_left)
                v.tensor_tensor(v_tA[...], v_tB[...], v_tm[...], Alu.bitwise_or)
                last = v.tensor_tensor(wdst[...], v_tA[...].bitcast(BF16),
                                       scb[...], Alu.mult)
            return last

        def rope_emit(v, psum_ap, outb_ap, cs, sn, t1v, t2v):
            csb = cs.unsqueeze(1).broadcast_to([128, 4, 64])
            snb = sn.unsqueeze(1).broadcast_to([128, 4, 64])
            p3 = psum_ap.rearrange("p (h r) -> p h r", h=4)
            x1 = p3[:, :, 0:64]
            x2 = p3[:, :, 64:128]
            o3 = outb_ap.rearrange("p (h r) -> p h r", h=4)
            t1 = t1v[...].rearrange("p (h r) -> p h r", h=4)[:, :, 0:64]
            t2 = t2v[...].rearrange("p (h r) -> p h r", h=4)[:, :, 0:64]
            v.tensor_tensor(t1, x1, csb, Alu.mult)
            v.tensor_tensor(t2, x2, snb, Alu.mult)
            v.tensor_tensor(o3[:, :, 0:64], t1, t2, Alu.subtract)
            v.tensor_tensor(t1, x1, snb, Alu.mult)
            v.tensor_tensor(t2, x2, csb, Alu.mult)
            return v.tensor_tensor(o3[:, :, 64:128], t1, t2, Alu.add)

        def rdv(v, src_ap, w=1):
            v.tensor_copy(jnk[:, 6:7], jnk[:, 7:8])
            return v.tensor_copy(jnk[:, 0:w], src_ap)

        def rac(a, src_ap, w=1):
            a.activation(ajk[:, 6:7], ajk[:, 7:8], AF.Identity)
            return a.activation(ajk[:, 0:w], src_ap, AF.Identity)

        def vgap(v):
            v.tensor_copy(stat[:, 31:32], stat[:, 30:31])

        def ln_dv_stats(v, xsap):
            v.reduce_sum(sv(0), xsap, AXX)
            vgap(v)
            v.tensor_scalar(sv(1), sv(0), 1.0 / D, None, Alu.mult)
            vgap(v)
            v.tensor_scalar(sv(2), sv(1), -1.0, None, Alu.mult)

        def ln_dv_var(v):
            v.tensor_scalar(sv(4), sv(3), 1.0 / D, None, Alu.mult)
            v.tensor_tensor(sv(5), sv(1), sv(1), Alu.mult)
            vgap(v)
            v.scalar_tensor_tensor(sv(6), sv(4), EPS, sv(5), Alu.add,
                                   Alu.subtract)

        def ln_ac_finish(a, xsap):
            a.activation(sv(7), sv(6), AF.Ln)
            a.activation(stat[:, 29:30], stat[:, 28:29], AF.Identity)
            a.activation(sv(8), sv(7), AF.Exp, scale=-0.5)
            a.activation(v_xsq[...], xsap, AF.Identity, bias=sv(2))
            return a.activation(v_xn[...], v_xsq[...], AF.Identity,
                                scale=sv(8))

        def xnt_transposes(t, dst_bf_view):
            last = None
            for ci in range(16):
                off = 256 * (ci % 8) + (ci // 8)
                last = t.transpose(dst_bf_view[:, ci * 128:(ci + 1) * 128],
                                   v_xn[:, off:off + 256 - (ci // 8):2],
                                   s_idn[...])
            return last

        with nc.Block() as blk:
            @blk.sync
            def _(e):
                for dst, src in [(s_idn, din["idn"]), (s_b8, din["b8"]),
                                 (s_cosq, din["cosq"]), (s_sinq, din["sinq"]),
                                 (v_cosk, din["cosk"]), (v_sink, din["sink"])]:
                    e.dma_start(dst[...], src[...]).then_inc(SM["ld"].h, 16)
                SM["ld"].v += 96

            @blk.gpsimd
            def _(e):
                e.dma_start(v_wkb[...].rearrange("p (hc n) -> p hc n", hc=16),
                            din["wk"][...].rearrange("(hc p) n -> p hc n",
                                                     p=128)
                            ).then_inc(SM["lp"].h, 16)
                e.dma_start(v_wvb[...].rearrange("p (hc n) -> p hc n", hc=16),
                            din["wv"][...].rearrange("(hc p) n -> p hc n",
                                                     p=128)
                            ).then_inc(SM["lp"].h, 16)
                SM["lp"].v += 32

            @blk.vector
            def _(e):
                e.wait_ge(SM["ld"].h, SM["ld"].v)
                e.wait_ge(SM["lp"].h, SM["lp"].v)

        NTC = 32
        bs = {k: s.v for k, s in SM.items()}
        with nc.Block() as blk:
            @blk.sync
            def _(e):
                for tc in range(NTC):
                    if tc >= 2:
                        e.wait_ge(SM["ac"].h, bs["ac"] + (tc - 2) * 2 + 2)
                    xsn = "xs" + str(tc % 2)
                    e.dma_start(v_xs[tc % 2][...],
                                din["xseq"][tc * 128:(tc + 1) * 128, :]
                                ).then_inc(SM[xsn].h, 16)
                    SM[xsn].v += 16
                e.wait_ge(SM["dv2"].h, bs["dv2"] + NTC * 2)
                e.wait_ge(SM["ac2"].h, bs["ac2"] + NTC)
                e.dma_start(kvsp[:, 0:16384], v_kt[...]).then_inc(SM["sp"].h, 16)
                e.dma_start(kvsp[:, 16384:32768], v_v[...]).then_inc(
                    SM["sp"].h, 16)
                SM["sp"].v += 32
                if taps:
                    e.wait_ge(SM["sp"].h, SM["sp"].v)
                    e.dma_start(tapd["t_kt"][...], v_kt[...]).then_inc(
                        SM["sp"].h, 16)
                    e.dma_start(tapd["t_v"][...], v_v[...]).then_inc(
                        SM["sp"].h, 16)
                    SM["sp"].v += 32

            @blk.vector
            def _(e):
                for tc in range(NTC):
                    e.wait_ge(SM["xs" + str(tc % 2)].h,
                              bs["xs" + str(tc % 2)] + (tc // 2 + 1) * 16)
                    if tc >= 1:
                        e.wait_ge(SM["ac"].h, bs["ac"] + tc * 2)
                    ln_dv_stats(e, v_xs[tc % 2][...])
                    e.wait_ge(SM["ac"].h, bs["ac"] + tc * 2 + 1)
                    ln_dv_var(e)
                    rdv(e, stat[:, 0:8], 8).then_inc(SM["dv"].h, 1)
                    SM["dv"].v += 1
                    e.wait_ge(SM["pe2"].h, bs["pe2"] + tc * 2 + 1)
                    if tc >= 1:
                        e.wait_ge(SM["pe3"].h, bs["pe3"] + tc)
                    rope_emit(e, pB[:, 0:512],
                              v_krope[...],
                              v_cosk[:, tc * 64:(tc + 1) * 64],
                              v_sink[:, tc * 64:(tc + 1) * 64],
                              v_rt1, v_rt2).then_inc(SM["dv2"].h, 1)
                    SM["dv2"].v += 1
                    e.wait_ge(SM["pe3"].h, bs["pe3"] + tc + 1)
                    dst = v_kt[...].rearrange("p (h t) -> p h t", h=4)[
                        :, :, tc * 128:(tc + 1) * 128]
                    e.tensor_copy(dst, pBb[:, 3072:3584].rearrange(
                        "p (h t) -> p h t", h=4)).then_inc(SM["dv2"].h, 1)
                    SM["dv2"].v += 1

            @blk.scalar
            def _(e):
                for tc in range(NTC):
                    e.wait_ge(SM["xs" + str(tc % 2)].h,
                              bs["xs" + str(tc % 2)] + (tc // 2 + 1) * 16)
                    e.activation(v_xsq[...], v_xs[tc % 2][...], AF.Square,
                                 accum_out=sv(3))
                    rac(e, sv(3)).then_inc(SM["ac"].h, 1)
                    SM["ac"].v += 1
                    e.wait_ge(SM["dv"].h, bs["dv"] + tc + 1)
                    if tc >= 1:
                        e.wait_ge(SM["pe"].h, bs["pe"] + tc)
                    ln_ac_finish(e, v_xs[tc % 2][...]).then_inc(SM["ac"].h, 1)
                    SM["ac"].v += 1
                    e.wait_ge(SM["pe"].h, bs["pe"] + tc + 1)
                    e.activation(v_xntc[...], pAb[:, 0:2048],
                                 AF.Copy).then_inc(SM["ac3"].h, 1)
                    SM["ac3"].v += 1
                    e.wait_ge(SM["pe2"].h, bs["pe2"] + tc * 2 + 2)
                    e.activation(v_v[:, tc * 512:(tc + 1) * 512],
                                 pB[:, 512:1024], AF.Copy
                                 ).then_inc(SM["ac2"].h, 1)
                    SM["ac2"].v += 1

            @blk.tensor
            def _(e):
                for tc in range(NTC):
                    e.wait_ge(SM["ac"].h, bs["ac"] + tc * 2 + 2)
                    if tc >= 1:
                        e.wait_ge(SM["ac3"].h, bs["ac3"] + tc)
                    ins_ = xnt_transposes(e, pAb)
                    ins_.then_inc(SM["pe"].h, 1)
                    SM["pe"].v += 1
                    e.wait_ge(SM["ac3"].h, bs["ac3"] + tc + 1)
                    if tc >= 1:
                        e.wait_ge(SM["dv2"].h, bs["dv2"] + tc * 2)
                        e.wait_ge(SM["ac2"].h, bs["ac2"] + tc)
                    for ci in range(16):
                        ins_ = e.matmul(pB[:, 0:512],
                                        v_xntc[:, ci * 128:(ci + 1) * 128],
                                        v_wkb[:, ci * 512:(ci + 1) * 512],
                                        start=(ci == 0), stop=(ci == 15))
                    ins_.then_inc(SM["pe2"].h, 1)
                    for ci in range(16):
                        ins_ = e.matmul(pB[:, 512:1024],
                                        v_xntc[:, ci * 128:(ci + 1) * 128],
                                        v_wvb[:, ci * 512:(ci + 1) * 512],
                                        start=(ci == 0), stop=(ci == 15))
                    ins_.then_inc(SM["pe2"].h, 1)
                    SM["pe2"].v += 2
                    e.wait_ge(SM["dv2"].h, bs["dv2"] + tc * 2 + 1)
                    for h in range(4):
                        ins_ = e.transpose(
                            pBb[:, 3072 + h * 128:3072 + (h + 1) * 128],
                            v_krope[:, h * 128:(h + 1) * 128], s_idn[...])
                    ins_.then_inc(SM["pe3"].h, 1)
                    SM["pe3"].v += 1

        assert SM["dv2"].v == bs["dv2"] + 64 and SM["ac"].v == bs["ac"] + 64

        atsp = nc.dram_tensor("atsp", [2, 8, 128, 2048], BF16, kind="Internal")
        v_attstg = V(180, 4, BF16)
        v_atb = [V(160, 4, BF16), V(164, 4, BF16)]
        v_xo2 = [V(96, 2, F32), V(98, 2, F32)]
        v_ob2 = [V(100, 2, F32), V(102, 2, F32)]

        def emit_half(h):
            bs = {k: s.v for k, s in SM.items()}
            with nc.Block() as blk:
                @blk.sync
                def _(e):
                    dma_fence(e, bs)
                    for tl in range(8):
                        if tl >= 2:
                            e.wait_ge(SM["ac"].h, bs["ac"] + (tl - 2) * 2 + 2)
                        r0 = h * 1024 + tl * 128
                        xsn = "xs" + str(tl % 2)
                        e.dma_start(v_xs[tl % 2][...],
                                    din["xown"][r0:r0 + 128, :]
                                    ).then_inc(SM[xsn].h, 16)
                        SM[xsn].v += 16

                @blk.vector
                def _(e):
                    for tl in range(8):
                        e.wait_ge(SM["xs" + str(tl % 2)].h,
                                  bs["xs" + str(tl % 2)] + (tl // 2 + 1) * 16)
                        if tl >= 1:
                            e.wait_ge(SM["ac"].h, bs["ac"] + tl * 2)
                        ln_dv_stats(e, v_xs[tl % 2][...])
                        e.wait_ge(SM["ac"].h, bs["ac"] + tl * 2 + 1)
                        ln_dv_var(e)
                        rdv(e, stat[:, 0:8], 8).then_inc(SM["dv"].h, 1)
                        SM["dv"].v += 1

                @blk.scalar
                def _(e):
                    for tl in range(8):
                        e.wait_ge(SM["xs" + str(tl % 2)].h,
                                  bs["xs" + str(tl % 2)] + (tl // 2 + 1) * 16)
                        e.activation(v_xsq[...], v_xs[tl % 2][...], AF.Square,
                                     accum_out=sv(3))
                        rac(e, sv(3)).then_inc(SM["ac"].h, 1)
                        SM["ac"].v += 1
                        e.wait_ge(SM["dv"].h, bs["dv"] + tl + 1)
                        if tl >= 1:
                            e.wait_ge(SM["pe"].h, bs["pe"] + tl)
                        ln_ac_finish(e, v_xs[tl % 2][...]
                                     ).then_inc(SM["ac"].h, 1)
                        SM["ac"].v += 1
                        e.wait_ge(SM["pe"].h, bs["pe"] + tl + 1)
                        dst = v_xnt[...].rearrange("p (ci t) -> p ci t", ci=16)[
                            :, :, tl * 128:(tl + 1) * 128]
                        e.activation(dst, pAb[:, 0:2048].rearrange(
                            "p (ci k) -> p ci k", ci=16), AF.Copy
                            ).then_inc(SM["ac3"].h, 1)
                        SM["ac3"].v += 1

                @blk.tensor
                def _(e):
                    for tl in range(8):
                        e.wait_ge(SM["ac"].h, bs["ac"] + tl * 2 + 2)
                        if tl >= 1:
                            e.wait_ge(SM["ac3"].h, bs["ac3"] + tl)
                        ins_ = xnt_transposes(e, pAb)
                        ins_.then_inc(SM["pe"].h, 1)
                        SM["pe"].v += 1

                @blk.gpsimd
                def _(e):
                    if taps and h == 0:
                        e.wait_ge(SM["ac3"].h, bs["ac3"] + 8)
                        e.dma_start(tapd["t_xnt"][...], v_xnt[...]
                                    ).then_inc(SM["os"].h, 16)
                        SM["os"].v += 16

            bs = {k: s.v for k, s in SM.items()}
            do_down = (h == 0)
            NSC = 3 if do_down else 2
            LPW = 48 if do_down else 32
            SPW = 64 if do_down else 32
            pgld = [16, 48]
            puld = [32, 64]
            pdld = [80] if do_down else []
            _ld = 64 + (16 if do_down else 0)
            for w in range(NW):
                if w + 2 < NW:
                    _ld += 16
                    pgld.append(_ld)
                    _ld += 16
                    puld.append(_ld)
                if do_down and w + 1 < NW:
                    _ld += 16
                    pdld.append(_ld)
            LD_TOT = _ld

            with nc.Block() as blk:
                @blk.sync
                def _(e):
                    dma_fence(e, bs)
                    for w in (0, 1):
                        e.dma_start(v_pgb[w][...], din["pg"][w]
                                    ).then_inc(SM["pg" + str(w)].h, 16)
                        SM["pg" + str(w)].v += 16
                        e.dma_start(v_pub[w][...], din["pu"][w]
                                    ).then_inc(SM["pu" + str(w)].h, 16)
                        SM["pu" + str(w)].v += 16
                    if do_down:
                        e.dma_start(v_pdb[...], din["pd"][0]
                                    ).then_inc(SM["pdl"].h, 16)
                        SM["pdl"].v += 16
                    for w in range(NW):
                        if w + 2 < NW:
                            e.wait_ge(SM["dv3"].h, bs["dv3"] + w * NSC + 1)
                            e.dma_start(v_pgb[w % 2][...], din["pg"][w + 2]
                                        ).then_inc(SM["pg" + str(w % 2)].h, 16)
                            SM["pg" + str(w % 2)].v += 16
                            e.wait_ge(SM["dv3"].h, bs["dv3"] + w * NSC + 2)
                            e.dma_start(v_pub[w % 2][...], din["pu"][w + 2]
                                        ).then_inc(SM["pu" + str(w % 2)].h, 16)
                            SM["pu" + str(w % 2)].v += 16
                        if do_down and w + 1 < NW:
                            e.wait_ge(SM["dv3"].h, bs["dv3"] + w * NSC + 3)
                            e.dma_start(v_pdb[...], din["pd"][w + 1]
                                        ).then_inc(SM["pdl"].h, 16)
                            SM["pdl"].v += 16
                        spn = "spi" + str(w % 2)
                        e.wait_ge(SM["dv2"].h, bs["dv2"] + w * 2 + 1)
                        e.dma_start(intsp[w, 0, :, h * 1024:(h + 1) * 1024],
                                    v_intb[w % 2][:, 0:1024]
                                    ).then_inc(SM[spn].h, 16)
                        e.wait_ge(SM["dv2"].h, bs["dv2"] + w * 2 + 2)
                        e.dma_start(intsp[w, 1, :, h * 1024:(h + 1) * 1024],
                                    v_intb[w % 2][:, 1024:2048]
                                    ).then_inc(SM[spn].h, 16)
                        SM[spn].v += 32
                        if do_down:
                            e.wait_ge(SM["dv3"].h, bs["dv3"] + w * NSC + 3)
                            e.dma_start(wdsp[w, 0], v_wde[...]
                                        ).then_inc(SM["spw"].h, 16)
                            e.dma_start(wdsp[w, 1], v_wdo[...]
                                        ).then_inc(SM["spw"].h, 16)
                            SM["spw"].v += 32

                @blk.gpsimd
                def _(e):
                    dma_fence(e, bs)
                    for w in range(NW):
                        if w >= 2:
                            e.wait_ge(SM["pe2"].h, bs["pe2"] + (w - 1) * NSC)
                        e.dma_start(v_sgb[w % 2][0:8, :], din["sg"][w]
                                    ).then_inc(SM["sg" + str(w % 2)].h, 16)
                        SM["sg" + str(w % 2)].v += 16
                        e.dma_start(v_sub[w % 2][0:8, :], din["su"][w]
                                    ).then_inc(SM["su" + str(w % 2)].h, 16)
                        SM["su" + str(w % 2)].v += 16
                        if do_down:
                            if w >= 1:
                                e.wait_ge(SM["pe2"].h, bs["pe2"] + w * NSC)
                            e.dma_start(v_sdb[0:8, :], din["sd"][w]
                                        ).then_inc(SM["sdl"].h, 16)
                            SM["sdl"].v += 16

                @blk.tensor
                def _(e):
                    def scale_mms(src):
                        for ci in range(8):
                            ins_ = e.matmul(
                                pA[:, ci * 256:(ci + 1) * 256], s_b8[...],
                                src[0:8, ci * 256:(ci + 1) * 256],
                                start=True, stop=True)
                        return ins_

                    e.wait_ge(SM["sg0"].h, bs["sg0"] + 16)
                    scale_mms(v_sgb[0]).then_inc(SM["pe2"].h, 1)
                    e.wait_ge(SM["ac"].h, bs["ac"] + 1)
                    e.wait_ge(SM["su0"].h, bs["su0"] + 16)
                    scale_mms(v_sub[0]).then_inc(SM["pe2"].h, 1)
                    if do_down:
                        e.wait_ge(SM["ac"].h, bs["ac"] + 2)
                        e.wait_ge(SM["sdl"].h, bs["sdl"] + 16)
                        scale_mms(v_sdb).then_inc(SM["pe2"].h, 1)
                    SM["pe2"].v += NSC
                    for w in range(NW):
                        wge, wgo = v_wge[w % 2], v_wgo[w % 2]
                        wue, wuo = v_wue[w % 2], v_wuo[w % 2]
                        for eo in range(2):
                            e.wait_ge(SM["dv3"].h, bs["dv3"] + w * NSC + 2)
                            if w * 2 + eo - 1 >= 0:
                                e.wait_ge(SM["ac2"].h, bs["ac2"] + w * 2 + eo)
                            for tp in range(2):
                                for ci in range(16):
                                    wsrc = (wge if ci < 8 else wgo)
                                    c8 = ci % 8
                                    ins_ = e.matmul(
                                        pB[:, tp * 512:(tp + 1) * 512],
                                        wsrc[:, c8 * 256 + eo * 128:
                                             c8 * 256 + (eo + 1) * 128],
                                        v_xnt[:, ci * 1024 + tp * 512:
                                              ci * 1024 + (tp + 1) * 512],
                                        start=(ci == 0), stop=(ci == 15))
                            for tp in range(2):
                                for ci in range(16):
                                    wsrc = (wue if ci < 8 else wuo)
                                    c8 = ci % 8
                                    ins_ = e.matmul(
                                        pB[:, 1024 + tp * 512:
                                           1024 + (tp + 1) * 512],
                                        wsrc[:, c8 * 256 + eo * 128:
                                             c8 * 256 + (eo + 1) * 128],
                                        v_xnt[:, ci * 1024 + tp * 512:
                                              ci * 1024 + (tp + 1) * 512],
                                        start=(ci == 0), stop=(ci == 15))
                            ins_.then_inc(SM["pe"].h, 1)
                            SM["pe"].v += 1
                            if w + 1 < NW:
                                p1 = (w + 1) % 2
                                if eo == 0:
                                    e.wait_ge(SM["sg" + str(p1)].h,
                                              bs["sg" + str(p1)] +
                                              ((w + 1) // 2 + 1) * 16)
                                    e.wait_ge(SM["ac"].h,
                                              bs["ac"] + w * NSC + NSC)
                                    scale_mms(v_sgb[p1]
                                              ).then_inc(SM["pe2"].h, 1)
                                else:
                                    e.wait_ge(SM["su" + str(p1)].h,
                                              bs["su" + str(p1)] +
                                              ((w + 1) // 2 + 1) * 16)
                                    e.wait_ge(SM["ac"].h,
                                              bs["ac"] + (w + 1) * NSC + 1)
                                    scale_mms(v_sub[p1]
                                              ).then_inc(SM["pe2"].h, 1)
                                SM["pe2"].v += 1
                        if do_down and w + 1 < NW:
                            e.wait_ge(SM["sdl"].h, bs["sdl"] + (w + 2) * 16)
                            e.wait_ge(SM["ac"].h, bs["ac"] + (w + 1) * NSC + 2)
                            scale_mms(v_sdb).then_inc(SM["pe2"].h, 1)
                            SM["pe2"].v += 1

                @blk.scalar
                def _(e):
                    e.wait_ge(SM["pe2"].h, bs["pe2"] + 1)
                    e.activation(v_scg[0][...], pA[...], AF.Copy
                                 ).then_inc(SM["ac"].h, 1)
                    e.wait_ge(SM["pe2"].h, bs["pe2"] + 2)
                    e.activation(v_scu[0][...], pA[...], AF.Copy
                                 ).then_inc(SM["ac"].h, 1)
                    SM["ac"].v += 2
                    if do_down:
                        e.wait_ge(SM["pe2"].h, bs["pe2"] + 3)
                        e.activation(v_scd[...], pA[...], AF.Copy
                                     ).then_inc(SM["ac"].h, 1)
                        SM["ac"].v += 1
                    for w in range(NW):
                        for eo in range(2):
                            e.wait_ge(SM["pe"].h, bs["pe"] + w * 2 + eo + 1)
                            if w * 2 + eo >= 1:
                                e.wait_ge(SM["dv2"].h, bs["dv2"] + w * 2 + eo)
                            e.activation(v_gsb[...], pB[:, 0:1024], AF.Copy)
                            e.activation(v_upsb[...], pB[:, 1024:2048],
                                         AF.Copy, scale=0.5
                                         ).then_inc(SM["ac2"].h, 1)
                            SM["ac2"].v += 1
                            if w + 1 < NW:
                                e.wait_ge(SM["pe2"].h,
                                          bs["pe2"] + (w + 1) * NSC + eo + 1)
                                dstv = (v_scg if eo == 0 else v_scu)[(w + 1) % 2]
                                e.activation(dstv[...], pA[...], AF.Copy
                                             ).then_inc(SM["ac"].h, 1)
                                SM["ac"].v += 1
                            e.wait_ge(SM["dv"].h, bs["dv"] + w * 2 + eo + 1)
                            e.activation(v_gt1[...], v_gt2[...], AF.Tanh
                                         ).then_inc(SM["ac3"].h, 1)
                            SM["ac3"].v += 1
                        if do_down and w + 1 < NW:
                            e.wait_ge(SM["pe2"].h,
                                      bs["pe2"] + (w + 1) * NSC + 3)
                            e.wait_ge(SM["dv3"].h, bs["dv3"] + w * NSC + 3)
                            e.activation(v_scd[...], pA[...], AF.Copy
                                         ).then_inc(SM["ac"].h, 1)
                            SM["ac"].v += 1

                @blk.vector
                def _(e):
                    for w in range(NW):
                        e.wait_ge(SM["pg" + str(w % 2)].h,
                                  bs["pg" + str(w % 2)] + (w // 2 + 1) * 16)
                        e.wait_ge(SM["ac"].h, bs["ac"] + w * NSC + 1)
                        if w >= 2:
                            e.wait_ge(SM["pe"].h, bs["pe"] + (w - 1) * 2)
                        decode_emit(e, v_pgb[w % 2][...], v_wge[w % 2],
                                    v_wgo[w % 2], v_scg[w % 2]
                                    ).then_inc(SM["dv3"].h, 1)
                        e.wait_ge(SM["pu" + str(w % 2)].h,
                                  bs["pu" + str(w % 2)] + (w // 2 + 1) * 16)
                        e.wait_ge(SM["ac"].h, bs["ac"] + w * NSC + 2)
                        decode_emit(e, v_pub[w % 2][...], v_wue[w % 2],
                                    v_wuo[w % 2], v_scu[w % 2]
                                    ).then_inc(SM["dv3"].h, 1)
                        SM["dv3"].v += 2
                        for eo in range(2):
                            e.wait_ge(SM["ac2"].h, bs["ac2"] + w * 2 + eo + 1)
                            if w >= 2:
                                e.wait_ge(SM["spi" + str(w % 2)].h,
                                          bs["spi" + str(w % 2)] +
                                          (w // 2) * 32)
                            e.tensor_tensor(v_gt1[...], v_gsb[...], v_gsb[...],
                                            Alu.mult)
                            e.tensor_scalar(v_gt1[...], v_gt1[...], GB, GA,
                                            Alu.mult, Alu.add)
                            e.tensor_tensor(v_gt2[...], v_gt1[...], v_gsb[...],
                                            Alu.mult).then_inc(SM["dv"].h, 1)
                            SM["dv"].v += 1
                            e.wait_ge(SM["ac3"].h, bs["ac3"] + w * 2 + eo + 1)
                            e.scalar_tensor_tensor(v_gt2[...], v_gt1[...], 1.0,
                                                   v_gsb[...], Alu.add,
                                                   Alu.mult)
                            e.tensor_tensor(
                                v_intb[w % 2][:, eo * 1024:(eo + 1) * 1024],
                                v_gt2[...], v_upsb[...], Alu.mult
                                ).then_inc(SM["dv2"].h, 1)
                            SM["dv2"].v += 1
                        if do_down:
                            e.wait_ge(SM["pdl"].h, bs["pdl"] + (w + 1) * 16)
                            e.wait_ge(SM["ac"].h, bs["ac"] + w * NSC + 3)
                            if w >= 1:
                                e.wait_ge(SM["spw"].h, bs["spw"] + w * 32)
                            decode_emit(e, v_pdb[...], v_wde, v_wdo, v_scd
                                        ).then_inc(SM["dv3"].h, 1)
                            SM["dv3"].v += 1

            bs = {k: s.v for k, s in SM.items()}
            with nc.Block() as blk:
                @blk.sync
                def _(e):
                    dma_fence(e, bs)
                    i = 0
                    for cc in range(4):
                        for w in range(NW):
                            if i >= 2:
                                e.wait_ge(SM["pe"].h, bs["pe"] + i - 1)
                            src = intsp[w, :, :, h * 1024:(h + 1) * 1024
                                        ].transpose([1, 0, 2])
                            e.dma_start(v_itb[i % 2][...].rearrange(
                                "p (eo t) -> p eo t", eo=2), src
                            ).then_inc(SM["it" + str(i % 2)].h, 16)
                            SM["it" + str(i % 2)].v += 16
                            i += 1

                @blk.gpsimd
                def _(e):
                    dma_fence(e, bs)
                    i = 0
                    for cc in range(4):
                        for w in range(NW):
                            if i >= 2:
                                e.wait_ge(SM["pe"].h, bs["pe"] + i - 1)
                            src = wdsp[w, :, :, cc * 512:(cc + 1) * 512
                                       ].transpose([1, 0, 2])
                            e.dma_start(v_wdb[i % 2][...].rearrange(
                                "p (eo c) -> p eo c", eo=2), src
                            ).then_inc(SM["wd" + str(i % 2)].h, 16)
                            SM["wd" + str(i % 2)].v += 16
                            i += 1

                @blk.tensor
                def _(e):
                    i = 0
                    for cc in range(4):
                        for w in range(NW):
                            e.wait_ge(SM["it" + str(i % 2)].h,
                                      bs["it" + str(i % 2)] +
                                      (i // 2 + 1) * 16)
                            e.wait_ge(SM["wd" + str(i % 2)].h,
                                      bs["wd" + str(i % 2)] +
                                      (i // 2 + 1) * 16)
                            if w == 0 and cc > 0:
                                e.wait_ge(SM["ac"].h, bs["ac"] + cc)
                            for eo in range(2):
                                for tt in range(8):
                                    pb = (pA if tt < 4 else pB)
                                    ins_ = e.matmul(
                                        pb[:, (tt % 4) * 512:
                                           (tt % 4 + 1) * 512],
                                        v_itb[i % 2][:, eo * 1024 + tt * 128:
                                                     eo * 1024 + (tt + 1) * 128],
                                        v_wdb[i % 2][:, eo * 512:
                                                     (eo + 1) * 512],
                                        start=(w == 0 and eo == 0),
                                        stop=(w == NW - 1 and eo == 1))
                            ins_.then_inc(SM["pe"].h, 1)
                            SM["pe"].v += 1
                            i += 1

                @blk.scalar
                def _(e):
                    for cc in range(4):
                        e.wait_ge(SM["pe"].h, bs["pe"] + (cc + 1) * NW)
                        for tt in range(8):
                            pb = (pA if tt < 4 else pB)
                            ins_ = e.activation(
                                v_g[:, tt * 2048 + cc * 512:
                                    tt * 2048 + (cc + 1) * 512],
                                pb[:, (tt % 4) * 512:(tt % 4 + 1) * 512],
                                AF.Silu)
                        ins_.then_inc(SM["ac"].h, 1)
                        SM["ac"].v += 1

                @blk.gpsimd
                def _(e):
                    if taps and h == 0:
                        e.wait_ge(SM["ac"].h, bs["ac"] + 4)
                        e.dma_start(tapd["t_g"][...], v_g[...]
                                    ).then_inc(SM["os"].h, 16)
                        e.dma_start(tapd["t_int"][...], intsp[0, 0]
                                    ).then_inc(SM["os"].h, 16)
                        e.dma_start(tapd["t_wd"][...], wdsp[0, 0]
                                    ).then_inc(SM["os"].h, 16)
                        e.dma_start(tapd["t_wg"][...], intsp[0, 1]
                                    ).then_inc(SM["os"].h, 16)
                        SM["os"].v += 64

            bs = {k: s.v for k, s in SM.items()}
            with nc.Block() as blk:
                @blk.gpsimd
                def _(e):
                    dma_fence(e, bs)
                    for dq in range(4):
                        if dq >= 2:
                            e.wait_ge(SM["pe"].h, bs["pe"] + (dq - 1) * 8)
                        e.dma_start(v_wqb[dq % 2][...].rearrange(
                                        "p (hc n) -> p hc n", hc=16),
                                    din["wq"][:, dq * 512:(dq + 1) * 512
                                              ].rearrange(
                                        "(hc p) n -> p hc n", p=128)
                                    ).then_inc(SM["wq" + str(dq % 2)].h, 16)
                        SM["wq" + str(dq % 2)].v += 16

                @blk.tensor
                def _(e):
                    for dq in range(4):
                        e.wait_ge(SM["wq" + str(dq % 2)].h,
                                  bs["wq" + str(dq % 2)] + (dq // 2 + 1) * 16)
                        for tt in range(8):
                            idx = dq * 8 + tt
                            pslot = pA[:, (idx % 2) * 512:(idx % 2 + 1) * 512]
                            if idx >= 2:
                                e.wait_ge(SM["dv"].h, bs["dv"] + idx - 1)
                            for ci in range(16):
                                ins_ = e.matmul(
                                    pslot,
                                    v_xnt[:, ci * 1024 + tt * 128:
                                          ci * 1024 + (tt + 1) * 128],
                                    v_wqb[dq % 2][:, ci * 512:(ci + 1) * 512],
                                    start=(ci == 0), stop=(ci == 15))
                            ins_.then_inc(SM["pe"].h, 1)
                            SM["pe"].v += 1

                @blk.vector
                def _(e):
                    for dq in range(4):
                        for tt in range(8):
                            idx = dq * 8 + tt
                            e.wait_ge(SM["pe"].h, bs["pe"] + idx + 1)
                            o = 8 * h + tt
                            pslot = pA[:, (idx % 2) * 512:(idx % 2 + 1) * 512]
                            rope_emit(
                                e, pslot,
                                v_q[:, tt * 2048 + dq * 512:
                                    tt * 2048 + (dq + 1) * 512],
                                s_cosq[:, o * 64:(o + 1) * 64],
                                s_sinq[:, o * 64:(o + 1) * 64],
                                v_art1, v_art2).then_inc(SM["dv"].h, 1)
                            SM["dv"].v += 1

                @blk.gpsimd
                def _(e):
                    if taps and h == 0:
                        e.wait_ge(SM["dv"].h, bs["dv"] + 32)
                        e.dma_start(tapd["t_q"][...], v_q[...]
                                    ).then_inc(SM["os"].h, 16)
                        SM["os"].v += 16

            bs = {k: s.v for k, s in SM.items()}
            iters = [(j, hd) for j in range(8) for hd in range(NQH)]
            NI = len(iters)
            nchs = [nch_of(8 * h + j) for (j, hd) in iters]
            ngrps = [(n + 1) // 2 for n in nchs]
            PEb = np.cumsum([0] + [1] * NI)
            PE2b = np.cumsum([0] + ngrps)
            PE3b = np.cumsum([0] + [2 * n for n in nchs])
            DVb = np.cumsum([0] + [2] * NI)
            DV2b = np.cumsum([0] + [g + n for g, n in zip(ngrps, nchs)])
            DV3b = np.cumsum([0] + [1] * NI)
            ACb = np.cumsum([0] + ngrps)
            AC2b = np.cumsum([0] + [1] * NI)
            AC3b = np.cumsum([0] + [1] * NI)

            with nc.Block() as blk:
                @blk.sync
                def _(e):
                    dma_fence(e, bs)
                    e.dma_start(v_kt[...], kvsp[:, 0:16384]
                                ).then_inc(SM["ktv"].h, 16)
                    e.dma_start(v_v[...], kvsp[:, 16384:32768]
                                ).then_inc(SM["ktv"].h, 16)
                    SM["ktv"].v += 32
                    for j in (0, 1):
                        e.dma_start(v_mkb[j % 2][...], din["dmask"][8 * h + j]
                                    ).then_inc(SM["mk" + str(j % 2)].h, 16)
                        SM["mk" + str(j % 2)].v += 16
                    for j in range(8):
                        e.wait_ge(SM["ac3"].h,
                                  bs["ac3"] + int(AC3b[j * NQH + NQH]))
                        e.dma_start(atsp[h, j], v_attstg[...]
                                    ).then_inc(SM["spa"].h, 16)
                        SM["spa"].v += 16
                        if j + 2 < 8:
                            e.wait_ge(SM["dv3"].h,
                                      bs["dv3"] + int(DV3b[(j + 1) * NQH]))
                            e.dma_start(v_mkb[j % 2][...],
                                        din["dmask"][8 * h + j + 2]
                                        ).then_inc(SM["mk" + str(j % 2)].h, 16)
                            SM["mk" + str(j % 2)].v += 16
                    if taps and h == 0:
                        e.wait_ge(SM["spa"].h, bs["spa"] + 128)
                        e.dma_start(
                            tapd["t_at"][...].rearrange("p (j n) -> (p j) n",
                                                        j=8),
                            atsp[0]
                        ).then_inc(SM["os"].h, 16)
                        SM["os"].v += 16

                @blk.tensor
                def _(e):
                    e.wait_ge(SM["ktv"].h, bs["ktv"] + 32)
                    for it, (j, hd) in enumerate(iters):
                        nch, ngrp = nchs[it], ngrps[it]
                        kv = hd // 4
                        if it >= 1:
                            e.wait_ge(SM["ac3"].h, bs["ac3"] + int(AC3b[it]))
                        if it >= 1:
                            e.wait_ge(SM["dv2"].h, bs["dv2"] + int(DV2b[it]))
                        ins_ = e.transpose(
                            pBb[:, 3584:3712],
                            v_q[:, j * 2048 + hd * 128:
                                j * 2048 + (hd + 1) * 128],
                            s_idn[...])
                        ins_.then_inc(SM["pe"].h, 1)
                        SM["pe"].v += 1
                        e.wait_ge(SM["dv"].h, bs["dv"] + int(DVb[it]) + 1)
                        for gi in range(ngrp):
                            half = gi % 2
                            if gi >= 2:
                                e.wait_ge(SM["ac"].h,
                                          bs["ac"] + int(ACb[it]) + gi - 1)
                            elif it >= 1:
                                e.wait_ge(SM["ac"].h, bs["ac"] + int(ACb[it]))
                            for c in range(gi * 2, min(nch, gi * 2 + 2)):
                                ins_ = e.matmul(
                                    pA[:, half * 1024 + (c - gi * 2) * 512:
                                       half * 1024 + (c - gi * 2 + 1) * 512],
                                    v_qt[it % 2][...],
                                    v_kt[:, kv * 4096 + c * 512:
                                         kv * 4096 + (c + 1) * 512],
                                    start=True, stop=True)
                            ins_.then_inc(SM["pe2"].h, 1)
                            SM["pe2"].v += 1
                        e.wait_ge(SM["dv3"].h, bs["dv3"] + int(DV3b[it]) + 1)
                        for c in range(nch):
                            for kt in range(4):
                                ins_ = e.transpose(
                                    pBb[:, 3072 + kt * 128:
                                        3072 + (kt + 1) * 128],
                                    v_p[it % 2][:, c * 512 + kt * 128:
                                                c * 512 + (kt + 1) * 128],
                                    s_idn[...])
                            ins_.then_inc(SM["pe3"].h, 1)
                            SM["pe3"].v += 1
                            e.wait_ge(SM["dv2"].h,
                                      bs["dv2"] + int(DV2b[it]) + ngrp + c + 1)
                            for kt in range(4):
                                kg = c * 4 + kt
                                ins_ = e.matmul(
                                    pB[:, 1024:1152],
                                    v_v[:, kg * 512 + kv * 128:
                                        kg * 512 + (kv + 1) * 128],
                                    v_ptb[:, kt * 128:(kt + 1) * 128],
                                    start=(c == 0 and kt == 0),
                                    stop=(c == nch - 1 and kt == 3))
                            ins_.then_inc(SM["pe3"].h, 1)
                            SM["pe3"].v += 1

                @blk.vector
                def _(e):
                    for it, (j, hd) in enumerate(iters):
                        nch, ngrp = nchs[it], ngrps[it]
                        e.wait_ge(SM["pe"].h, bs["pe"] + int(PEb[it]) + 1)
                        if it >= 2:
                            e.wait_ge(SM["pe2"].h, bs["pe2"] + int(PE2b[it - 1]))
                        e.tensor_copy(v_qt[it % 2][...], pBb[:, 3584:3712]
                                      ).then_inc(SM["dv"].h, 1)
                        e.wait_ge(SM["mk" + str(j % 2)].h,
                                  bs["mk" + str(j % 2)] + (j // 2 + 1) * 16)
                        for gi in range(ngrp):
                            half = gi % 2
                            e.wait_ge(SM["pe2"].h,
                                      bs["pe2"] + int(PE2b[it]) + gi + 1)
                            lim = min(nch, gi * 2 + 2)
                            for c in range(gi * 2, lim):
                                ps = pA[:, half * 1024 + (c - gi * 2) * 512:
                                        half * 1024 + (c - gi * 2 + 1) * 512]
                                wn = min(2, nch)
                                if c >= nch - wn:
                                    w2 = c - (nch - wn)
                                    e.scalar_tensor_tensor(
                                        ps, ps, 1.0,
                                        v_mkb[j % 2][:, w2 * 512:
                                                     (w2 + 1) * 512],
                                        Alu.mult, Alu.add)
                                e.reduce_max(mxc[:, c:c + 1], ps, AXX)
                            vgap(e)
                            if lim - gi * 2 == 2:
                                e.tensor_tensor(smc[:, gi:gi + 1],
                                                mxc[:, gi * 2:gi * 2 + 1],
                                                mxc[:, gi * 2 + 1:gi * 2 + 2],
                                                Alu.max)
                            else:
                                e.tensor_copy(smc[:, gi:gi + 1],
                                              mxc[:, gi * 2:gi * 2 + 1])
                            vgap(e)
                            e.tensor_scalar(gnbt[:, gi:gi + 1],
                                            smc[:, gi:gi + 1], -ISQ, None,
                                            Alu.mult)
                            rdv(e, gnbt[:, gi:gi + 1]
                                ).then_inc(SM["dv2"].h, 1)
                            SM["dv2"].v += 1
                        e.reduce_max(sv(9), smc[:, 0:ngrp], AXX)
                        vgap(e)
                        e.tensor_scalar(sv(10), sv(9), -ISQ, None, Alu.mult)
                        rdv(e, sv(10)).then_inc(SM["dv"].h, 1)
                        SM["dv"].v += 2
                        e.wait_ge(SM["ac2"].h, bs["ac2"] + int(AC2b[it]) + 1)
                        for gi in range(ngrp):
                            lim = min(nch, gi * 2 + 2)
                            if lim - gi * 2 == 2:
                                e.tensor_tensor(smc[:, 4 + gi:5 + gi],
                                                csum[:, gi * 2:gi * 2 + 1],
                                                csum[:, gi * 2 + 1:gi * 2 + 2],
                                                Alu.add)
                            else:
                                e.tensor_copy(smc[:, 4 + gi:5 + gi],
                                              csum[:, gi * 2:gi * 2 + 1])
                        if ngrp == 1:
                            vgap(e)
                        for gi in range(ngrp):
                            e.tensor_tensor(smc[:, 4 + gi:5 + gi],
                                            smc[:, 4 + gi:5 + gi],
                                            nfc[:, gi:gi + 1], Alu.mult)
                        vgap(e)
                        e.reduce_sum(sv(11), smc[:, 4:4 + ngrp], AXX)
                        vgap(e)
                        e.reciprocal(sv(12), sv(11))
                        vgap(e)
                        e.tensor_scalar(nfc[:, 0:ngrp], nfc[:, 0:ngrp],
                                        sv(12), None, Alu.mult)
                        vgap(e)
                        for gi in range(ngrp):
                            lim = min(nch, gi * 2 + 2)
                            width = lim - gi * 2
                            ins_ = e.tensor_scalar(
                                v_p[it % 2][:, gi * 1024:
                                            gi * 1024 + width * 512],
                                v_p[it % 2][:, gi * 1024:
                                            gi * 1024 + width * 512],
                                nfc[:, gi:gi + 1], None, Alu.mult)
                        ins_.then_inc(SM["dv3"].h, 1)
                        SM["dv3"].v += 1
                        for c in range(nch):
                            e.wait_ge(SM["pe3"].h,
                                      bs["pe3"] + int(PE3b[it]) + 2 * c + 1)
                            e.tensor_copy(v_ptb[...], pBb[:, 3072:3584]
                                          ).then_inc(SM["dv2"].h, 1)
                            SM["dv2"].v += 1

                @blk.scalar
                def _(e):
                    for it, (j, hd) in enumerate(iters):
                        nch, ngrp = nchs[it], ngrps[it]
                        for gi in range(ngrp):
                            half = gi % 2
                            lim = min(nch, gi * 2 + 2)
                            e.wait_ge(SM["dv2"].h,
                                      bs["dv2"] + int(DV2b[it]) + gi + 1)
                            if it >= 2 and gi == 0:
                                e.wait_ge(SM["pe3"].h,
                                          bs["pe3"] + int(PE3b[it - 1]))
                            for c in range(gi * 2, lim):
                                ps = pA[:, half * 1024 + (c - gi * 2) * 512:
                                        half * 1024 + (c - gi * 2 + 1) * 512]
                                e.activation(
                                    v_p[it % 2][:, c * 512:(c + 1) * 512],
                                    ps, AF.Exp, scale=ISQ,
                                    bias=gnbt[:, gi:gi + 1],
                                    accum_out=csum[:, c:c + 1])
                            rac(e, csum[:, gi * 2:lim], lim - gi * 2
                                ).then_inc(SM["ac"].h, 1)
                            SM["ac"].v += 1
                        e.wait_ge(SM["dv"].h, bs["dv"] + int(DVb[it]) + 2)
                        e.activation(nfc[:, 0:ngrp], smc[:, 0:ngrp], AF.Exp,
                                     scale=ISQ, bias=sv(10))
                        rac(e, nfc[:, 0:ngrp], ngrp).then_inc(SM["ac2"].h, 1)
                        SM["ac2"].v += 1
                        e.wait_ge(SM["pe3"].h,
                                  bs["pe3"] + int(PE3b[it]) + 2 * nch)
                        if hd == 0 and j >= 1:
                            e.wait_ge(SM["spa"].h, bs["spa"] + 16 * j)
                        e.activation(v_attstg[:, hd * 128:(hd + 1) * 128],
                                     pB[:, 1024:1152], AF.Copy
                                     ).then_inc(SM["ac3"].h, 1)
                        SM["ac3"].v += 1

            bs = {k: s.v for k, s in SM.items()}
            with nc.Block() as blk:
                @blk.gpsimd
                def _(e):
                    def wob_load(cc):
                        e.dma_start(v_wob[cc % 2][...].rearrange(
                                        "p (hc n) -> p hc n", hc=16),
                                    din["wo"][:, cc * 512:(cc + 1) * 512
                                              ].rearrange(
                                        "(hc p) n -> p hc n", p=128)
                                    ).then_inc(SM["wo" + str(cc % 2)].h, 16)
                        SM["wo" + str(cc % 2)].v += 16
                    dma_fence(e, bs)
                    wob_load(0)
                    wob_load(1)
                    i = 0
                    for cc in range(4):
                        for j in range(8):
                            e.wait_ge(SM["dv"].h, bs["dv"] + i + 1)
                            r0 = h * 1024 + j * 128
                            e.dma_start(
                                out[r0:r0 + 128, cc * 512:(cc + 1) * 512],
                                v_ob2[i % 2][...]
                            ).then_inc(SM["os" + str(i % 2)].h, 16)
                            SM["os" + str(i % 2)].v += 16
                            i += 1
                        if cc + 2 < 4:
                            e.wait_ge(SM["pe"].h, bs["pe"] + (cc + 1) * 8)
                            wob_load(cc + 2)

                @blk.sync
                def _(e):
                    dma_fence(e, bs)
                    i = 0
                    for cc in range(4):
                        for j in range(8):
                            if i >= 2:
                                e.wait_ge(SM["pe"].h, bs["pe"] + i - 1)
                            e.dma_start(v_atb[i % 2][...], atsp[h, j]
                                        ).then_inc(SM["at" + str(i % 2)].h, 16)
                            SM["at" + str(i % 2)].v += 16
                            if i >= 2:
                                e.wait_ge(SM["dv"].h, bs["dv"] + i - 1)
                            r0 = h * 1024 + j * 128
                            e.dma_start(v_xo2[i % 2][...],
                                        din["xown"][r0:r0 + 128,
                                                    cc * 512:(cc + 1) * 512]
                                        ).then_inc(SM["xo" + str(i % 2)].h, 16)
                            SM["xo" + str(i % 2)].v += 16
                            i += 1

                @blk.tensor
                def _(e):
                    i = 0
                    for cc in range(4):
                        e.wait_ge(SM["wo" + str(cc % 2)].h,
                                  bs["wo" + str(cc % 2)] + (cc // 2 + 1) * 16)
                        for j in range(8):
                            pslot = pA[:, (i % 4) * 512:(i % 4 + 1) * 512]
                            e.wait_ge(SM["at" + str(i % 2)].h,
                                      bs["at" + str(i % 2)] +
                                      (i // 2 + 1) * 16)
                            if i >= 4:
                                e.wait_ge(SM["dv"].h, bs["dv"] + i - 3)
                            for ci in range(16):
                                ins_ = e.matmul(
                                    pslot,
                                    v_atb[i % 2][:, ci * 128:(ci + 1) * 128],
                                    v_wob[cc % 2][:, ci * 512:(ci + 1) * 512],
                                    start=(ci == 0), stop=(ci == 15))
                            ins_.then_inc(SM["pe"].h, 1)
                            SM["pe"].v += 1
                            i += 1

                @blk.vector
                def _(e):
                    i = 0
                    for cc in range(4):
                        for j in range(8):
                            e.wait_ge(SM["pe"].h, bs["pe"] + i + 1)
                            e.wait_ge(SM["xo" + str(i % 2)].h,
                                      bs["xo" + str(i % 2)] +
                                      (i // 2 + 1) * 16)
                            if i >= 2:
                                e.wait_ge(SM["os" + str(i % 2)].h,
                                          bs["os" + str(i % 2)] +
                                          (i // 2) * 16)
                            pslot = pA[:, (i % 4) * 512:(i % 4 + 1) * 512]
                            e.tensor_tensor(
                                v_ob2[i % 2][...], pslot,
                                v_g[:, j * 2048 + cc * 512:
                                    j * 2048 + (cc + 1) * 512], Alu.mult)
                            e.tensor_tensor(v_ob2[i % 2][...], v_ob2[i % 2][...],
                                            v_xo2[i % 2][...], Alu.add
                                            ).then_inc(SM["dv"].h, 1)
                            SM["dv"].v += 1
                            i += 1


        emit_half(0)
        emit_half(1)

        with nc.Block() as blk:
            @blk.gpsimd
            def _(e):
                e.wait_ge(SM["os0"].h, SM["os0"].v)
                e.wait_ge(SM["os1"].h, SM["os1"].v)
                e.wait_ge(SM["os"].h, SM["os"].v)
    return nc


def kernel(**inputs):
    import os as _os
    taps = bool(_os.environ.get("MK_TAPS"))
    maps = _host_stage(inputs)
    nc = build_program(taps=taps)
    res = run_bass_kernel_spmd(nc, maps, core_ids=list(range(8)))
    full = np.empty((NB, S, D), np.float32)
    for c in range(8):
        par = c % 2
        full[c // 2].reshape(32, 128, D)[par::2] = \
            np.asarray(res.results[c]["out"]).reshape(NT, 128, D)
    if taps:
        kernel.last_taps = res.results
    return full


# version 03
# speedup vs baseline: 1.4611x; 1.4611x over previous
import numpy as np
import ml_dtypes
import concourse.bass as bass
import concourse.mybir as mybir
from concourse.bass_utils import run_bass_kernel_spmd

F32 = mybir.dt.float32
BF16 = mybir.dt.bfloat16
I32 = mybir.dt.int32
U16 = mybir.dt.uint16
U8 = mybir.dt.uint8
AF = mybir.ActivationFunctionType
Alu = mybir.AluOpType
AXX = mybir.AxisListType.X
bf = ml_dtypes.bfloat16

D, S, NB = 2048, 4096, 4
NQH, NKV, HD = 16, 4, 128
MLP, NW, TOWN, NT = 16384, 64, 2048, 16
ISQ = 1.0 / float(np.sqrt(HD))
GA = 0.7978845608028654
GB = GA * 0.044715
EPS = 1e-5
_PERM = np.concatenate([np.arange(0, D, 2), np.arange(1, D, 2)])


def _host_stage(inputs):
    x = np.asarray(inputs["L0_attn_x"], dtype=np.float32)
    Wq = np.asarray(inputs["L0_attn_Wq"], dtype=np.float32)
    Wk = np.asarray(inputs["L0_attn_Wk"], dtype=np.float32)
    Wv = np.asarray(inputs["L0_attn_Wv"], dtype=np.float32)
    Wo = np.asarray(inputs["L0_attn_Wo"], dtype=np.float32)
    gp = np.asarray(inputs["L0_ffn_gate_W_packed"], dtype=np.int32)
    gs = np.asarray(inputs["L0_ffn_gate_scales"], dtype=np.float32)
    up = np.asarray(inputs["L0_ffn_up_W_packed"], dtype=np.int32)
    us = np.asarray(inputs["L0_ffn_up_scales"], dtype=np.float32)
    dp = np.asarray(inputs["L0_ffn_down_W_packed"], dtype=np.int32)
    ds = np.asarray(inputs["L0_ffn_down_scales"], dtype=np.float32)

    dperm = np.concatenate([np.arange(0, HD, 2), np.arange(1, HD, 2)])
    qcol = (np.arange(NQH)[:, None] * HD + dperm[None, :]).reshape(-1)
    kcol = (np.arange(NKV)[:, None] * HD + dperm[None, :]).reshape(-1)
    wq_h = np.ascontiguousarray(Wq.T[_PERM][:, qcol])
    wk_h = np.ascontiguousarray(Wk.T[_PERM][:, kcol])
    wv_h = np.ascontiguousarray(Wv.T[_PERM, :])
    wo_h = np.ascontiguousarray(Wo.T)

    jperm = np.concatenate([np.arange(0, 256, 2), np.arange(1, 256, 2)])

    def block_packed(P):
        PT = P.T
        kp, n = PT.shape
        blk = PT.reshape(8, kp // 8, n // 256, 256).transpose(2, 1, 0, 3)
        blk = blk[:, :, :, jperm]
        return np.ascontiguousarray(blk.reshape(n // 256, kp // 8, 2048))

    def iterize(Ph):
        t = Ph.reshape(64, 128, 8, 2, 128).transpose(0, 3, 1, 2, 4)
        return np.ascontiguousarray(t.reshape(128, 128, 1024))

    pg2_h = iterize(block_packed(gp))
    pu2_h = iterize(block_packed(up))
    pd_h = np.ascontiguousarray(dp.T.reshape(64, 128, 2048))

    def block_scales(Sc):
        blk = Sc.T.reshape(8, 8, 64, 256).transpose(2, 1, 0, 3)
        blk = blk[:, :, :, jperm]
        return np.ascontiguousarray(blk.reshape(64, 8, 2048))

    def scale_iterize(Sh):
        t = Sh.reshape(64, 8, 8, 2, 128).transpose(0, 3, 1, 2, 4)
        t = np.repeat(t, 16, axis=2)
        return np.ascontiguousarray(t.reshape(128, 128, 1024))

    sg2_h = scale_iterize(block_scales(gs))
    su2_h = scale_iterize(block_scales(us))
    sd2_h = np.ascontiguousarray(
        np.repeat(ds.T.reshape(64, 8, 1, 2048), 16, axis=2
                  ).reshape(64, 128, 2048))

    idn_h = np.eye(128, dtype=bf)
    freqs = (1.0 / 10000.0 ** (np.arange(0, HD, 2) / HD)).astype(np.float64)
    ang = np.arange(S, dtype=np.float64)[:, None] * freqs[None, :]
    cosa = np.cos(ang).astype(np.float32)
    sina = np.sin(ang).astype(np.float32)
    cosk_h = np.ascontiguousarray(
        cosa.reshape(32, 128, 64).transpose(1, 0, 2).reshape(128, 2048))
    sink_h = np.ascontiguousarray(
        sina.reshape(32, 128, 64).transpose(1, 0, 2).reshape(128, 2048))

    maps = []
    for c in range(8):
        b_, par = c // 2, c % 2
        own = np.arange(par, 32, 2)
        xb = x[b_]
        xown = np.ascontiguousarray(xb.reshape(32, 128, D)[own].reshape(TOWN, D))
        cosq_h = np.ascontiguousarray(
            cosa.reshape(32, 128, 64)[own].transpose(1, 0, 2).reshape(128, 1024))
        sinq_h = np.ascontiguousarray(
            sina.reshape(32, 128, 64)[own].transpose(1, 0, 2).reshape(128, 1024))
        dmask_h = np.zeros((NT, 128, 1024), dtype=np.float32)
        for o in range(NT):
            g = 2 * o + par
            nch = (o + 2) // 2
            Wn = min(2, nch)
            qglob = 128 * g + np.arange(128)
            for w2 in range(Wn):
                cidx = nch - Wn + w2
                kglob = 512 * cidx + np.arange(512)
                dmask_h[o, :, 512 * w2:512 * (w2 + 1)] = np.where(
                    kglob[None, :] <= qglob[:, None], 0.0, -1e30)
        maps.append(dict(
            xseq=np.ascontiguousarray(xb), xown=xown,
            wq=wq_h, wk=wk_h, wv=wv_h, wo=wo_h,
            pg2=pg2_h, pu2=pu2_h, pd=pd_h, sg2=sg2_h, su2=su2_h, sd2=sd2_h,
            cosk=cosk_h, sink=sink_h, cosq=cosq_h, sinq=sinq_h,
            dmask=dmask_h, idn=idn_h))
    return maps


def nch_of(o):
    return (o + 2) // 2


class Sem:
    def __init__(self, h):
        self.h = h
        self.v = 0

    def base(self):
        return self.v


def build_program(taps=False, detect_races=True):
    nc = bass.Bass("TRN2", target_bir_lowering=False,
                   detect_race_conditions=detect_races)
    din = {}
    for nm, shp, dt_ in [
            ("xseq", [S, D], F32), ("xown", [TOWN, D], F32),
            ("wq", [D, D], F32), ("wk", [D, 512], F32), ("wv", [D, 512], F32),
            ("wo", [D, D], F32),
            ("pg2", [128, 128, 1024], I32), ("pu2", [128, 128, 1024], I32),
            ("pd", [NW, 128, 2048], I32),
            ("sg2", [128, 128, 1024], F32), ("su2", [128, 128, 1024], F32),
            ("sd2", [NW, 128, 2048], F32),
            ("cosk", [128, 2048], F32), ("sink", [128, 2048], F32),
            ("cosq", [128, 1024], F32), ("sinq", [128, 1024], F32),
            ("dmask", [NT, 128, 1024], F32),
            ("idn", [128, 128], BF16)]:
        din[nm] = nc.dram_tensor(nm, shp, dt_, kind="ExternalInput")
    out = nc.dram_tensor("out", [TOWN, D], F32, kind="ExternalOutput")
    tapd = {}
    if taps:
        for nm in ("t_xnt", "t_kt", "t_v", "t_q", "t_g", "t_at"):
            tapd[nm] = nc.dram_tensor(nm, [128, 16384], BF16,
                                      kind="ExternalOutput")
        for nm in ("t_int", "t_wd", "t_wg"):
            tapd[nm] = nc.dram_tensor(nm, [128, 2048], BF16,
                                      kind="ExternalOutput")
    kvsp = nc.dram_tensor("kvsp", [128, 32768], BF16, kind="Internal")
    intsp = nc.dram_tensor("intsp", [NW, 2, 128, 2048], BF16, kind="Internal")
    wdsp = nc.dram_tensor("wdsp", [NW, 2, 128, 2048], BF16, kind="Internal")

    from contextlib import ExitStack
    ctx = ExitStack()
    with ctx:
        s_idn = ctx.enter_context(nc.sbuf_tensor("s_idn", [128, 128], BF16))
        s_cosq = ctx.enter_context(nc.sbuf_tensor("s_cosq", [128, 1024], F32))
        s_sinq = ctx.enter_context(nc.sbuf_tensor("s_sinq", [128, 1024], F32))
        stat = ctx.enter_context(nc.sbuf_tensor("stat", [128, 32], F32))
        mxc = ctx.enter_context(nc.sbuf_tensor("mxc", [128, 8], F32))
        smc = ctx.enter_context(nc.sbuf_tensor("smc", [128, 8], F32))
        nfc = ctx.enter_context(nc.sbuf_tensor("nfc", [128, 8], F32))
        csum = ctx.enter_context(nc.sbuf_tensor("csum", [128, 8], F32))
        gnbt = ctx.enter_context(nc.sbuf_tensor("gnbt", [128, 4], F32))
        jnk = ctx.enter_context(nc.sbuf_tensor("jnk", [128, 8], F32))
        ajk = ctx.enter_context(nc.sbuf_tensor("ajk", [128, 8], F32))
        AR = ctx.enter_context(nc.sbuf_tensor("AR", [128, 188 * 1024], U8))

        def V(off_kb, size_kb, dt_):
            return AR[:, int(off_kb * 1024):int((off_kb + size_kb) * 1024)
                      ].bitcast(dt_)

        def sv(i):
            return stat[:, i:i + 1]

        v_kt = V(0, 32, BF16)
        v_xnt = V(0, 32, BF16)
        v_v = V(32, 32, BF16)
        v_pgb = [V(32, 8, I32), V(40, 8, I32)]
        v_pub = [V(48, 8, I32), V(56, 8, I32)]
        v_wkb = V(64, 16, BF16)
        v_wvb = V(80, 16, BF16)
        v_wge = [V(64, 4, BF16), V(68, 4, BF16)]
        v_wgo = [V(72, 4, BF16), V(76, 4, BF16)]
        v_wue = [V(80, 4, BF16), V(84, 4, BF16)]
        v_wuo = [V(88, 4, BF16), V(92, 4, BF16)]
        v_q = V(64, 32, BF16)
        v_wob = [V(64, 16, BF16), V(80, 16, BF16)]
        v_xs = [V(96, 8, F32), V(104, 8, F32)]
        v_xsq = V(112, 8, F32)
        v_xn = V(120, 4, BF16)
        v_krope = V(124, 1, BF16)
        v_rt1 = V(125, 1, F32)
        v_rt2 = V(126, 1, F32)
        v_scg = [V(96, 4, BF16), V(100, 4, BF16)]
        v_scu = [V(104, 4, BF16), V(108, 4, BF16)]
        v_sgb = [V(112, 4, BF16), V(116, 4, BF16)]
        v_sub = [V(120, 4, BF16), V(124, 4, BF16)]
        v_itb = [V(96, 4, BF16), V(100, 4, BF16)]
        v_wdb = [V(104, 2, BF16), V(106, 2, BF16)]
        v_wqb = [V(96, 16, BF16), V(112, 16, BF16)]
        v_p = [V(96, 8, BF16), V(104, 8, BF16)]
        v_ptb = V(112, 1, BF16)
        v_qt = [V(113, 0.25, BF16), V(113.25, 0.25, BF16)]
        v_mkb = [V(114, 4, F32), V(118, 4, F32)]
        v_art1 = V(160, 1, F32)
        v_art2 = V(161, 1, F32)
        v_xob = [V(96, 8, F32), V(104, 8, F32)]
        v_outb = [V(112, 8, F32), V(120, 8, F32)]
        v_g = V(128, 32, BF16)
        v_pdb = V(128, 8, I32)
        v_wde = V(136, 4, BF16)
        v_wdo = V(140, 4, BF16)
        v_scd = V(144, 4, BF16)
        v_sdb = V(148, 4, BF16)
        v_gsb = V(152, 2, BF16)
        v_upsb = V(154, 2, BF16)
        v_gt1 = V(156, 2, BF16)
        v_gt2 = V(158, 2, BF16)
        v_clo = V(160, 4, U16)
        v_chi = V(164, 4, U16)
        v_tm = V(168, 4, U16)
        v_tA = V(172, 4, U16)
        v_tB = V(176, 4, U16)
        v_cosk = V(160, 8, F32)
        v_sink = V(168, 8, F32)
        v_xntc = V(180, 4, BF16)
        v_intb = [V(180, 4, BF16), V(184, 4, BF16)]
        v_xnt2 = V(0, 64, BF16)
        v_pgb2 = [V(64, 4, I32), V(68, 4, I32)]
        v_pub2 = [V(72, 4, I32), V(76, 4, I32)]
        v_scg2 = [V(80, 2, BF16), V(82, 2, BF16)]
        v_scu2 = [V(84, 2, BF16), V(86, 2, BF16)]
        v_wge2 = [V(88, 2, BF16), V(90, 2, BF16)]
        v_wgo2 = [V(92, 2, BF16), V(94, 2, BF16)]
        v_wue2 = [V(96, 2, BF16), V(98, 2, BF16)]
        v_wuo2 = [V(100, 2, BF16), V(102, 2, BF16)]
        v_clo2 = V(104, 2, U16)
        v_chi2 = V(106, 2, U16)
        v_tm2 = V(108, 2, U16)
        v_tA2 = V(110, 2, U16)
        v_tB2 = V(112, 2, U16)
        v_gsb2 = V(114, 4, BF16)
        v_upsb2 = V(118, 4, BF16)
        v_intb2 = [V(122, 4, BF16), V(126, 4, BF16)]
        v_pdb2 = V(130, 8, I32)
        v_scd2 = V(138, 4, BF16)
        v_wde2 = V(142, 4, BF16)
        v_wdo2 = V(146, 4, BF16)

        pA = ctx.enter_context(nc.psum_tensor("pA", [128, 2048], F32))
        pB = ctx.enter_context(nc.psum_tensor("pB", [128, 2048], F32))
        pAb = pA[...].bitcast(BF16)
        pBb = pB[...].bitcast(BF16)

        SM = {}
        for nm in ("ld", "lp", "sp", "pe", "pe2", "pe3", "dv", "dv2", "dv3",
                   "ac", "ac2", "ac3", "os",
                   "xs0", "xs1", "pg0", "pg1", "pu0", "pu1", "pdl",
                   "sg0", "sg1", "su0", "su1", "sdl", "it0", "it1",
                   "wd0", "wd1", "wq0", "wq1", "mk0", "mk1",
                   "at0", "at1", "xo0", "xo1", "wo0", "wo1",
                   "spi0", "spi1", "spw", "spa", "os0", "os1", "ktv"):
            SM[nm] = Sem(ctx.enter_context(nc.semaphore("sm_" + nm)))

        def dma_fence(e, bs):
            for nm in ("sp", "spi0", "spi1", "spw", "spa", "os0", "os1"):
                if bs[nm] > 0:
                    e.wait_ge(SM[nm].h, bs[nm])

        def decode_emit(v, pkap, w_lo, w_hi, scb):
            pv = pkap.bitcast(U16)[:, 0:4096:2]
            v.tensor_scalar(v_clo[...], pv, 15, None, Alu.bitwise_and)
            v.tensor_scalar(v_chi[...], pv, 4, 15, Alu.logical_shift_right,
                            Alu.bitwise_and)
            last = None
            for csrc, wdst in ((v_clo, w_lo), (v_chi, w_hi)):
                v.tensor_scalar(v_tm[...], csrc[...], 7, None, Alu.bitwise_and)
                v.tensor_scalar(v_tA[...], v_tm[...], 64, 16128, Alu.mult,
                                Alu.add)
                v.tensor_scalar(v_tB[...], v_tm[...], 4, 16128, Alu.min,
                                Alu.mult)
                v.tensor_tensor(v_tB[...], v_tA[...], v_tB[...], Alu.min)
                v.tensor_scalar(v_tm[...], csrc[...], 8, 12, Alu.bitwise_and,
                                Alu.logical_shift_left)
                v.tensor_tensor(v_tA[...], v_tB[...], v_tm[...], Alu.bitwise_or)
                last = v.tensor_tensor(wdst[...], v_tA[...].bitcast(BF16),
                                       scb[...], Alu.mult)
            return last

        def decode2(v, pkap, w_lo, w_hi, scb):
            pv = pkap.bitcast(U16)[:, 0:2048:2]
            v.tensor_scalar(v_clo2[...], pv, 15, None, Alu.bitwise_and)
            v.tensor_scalar(v_chi2[...], pv, 4, 15, Alu.logical_shift_right,
                            Alu.bitwise_and)
            last = None
            for csrc, wdst in ((v_clo2, w_lo), (v_chi2, w_hi)):
                v.tensor_scalar(v_tm2[...], csrc[...], 7, None,
                                Alu.bitwise_and)
                v.tensor_scalar(v_tA2[...], v_tm2[...], 64, 16128, Alu.mult,
                                Alu.add)
                v.tensor_scalar(v_tB2[...], v_tm2[...], 4, 16128, Alu.min,
                                Alu.mult)
                v.tensor_tensor(v_tB2[...], v_tA2[...], v_tB2[...], Alu.min)
                v.tensor_scalar(v_tm2[...], csrc[...], 8, 12, Alu.bitwise_and,
                                Alu.logical_shift_left)
                v.tensor_tensor(v_tA2[...], v_tB2[...], v_tm2[...],
                                Alu.bitwise_or)
                last = v.tensor_tensor(wdst, v_tA2[...].bitcast(BF16),
                                       scb, Alu.mult)
            return last

        def rope_emit(v, psum_ap, outb_ap, cs, sn, t1v, t2v):
            csb = cs.unsqueeze(1).broadcast_to([128, 4, 64])
            snb = sn.unsqueeze(1).broadcast_to([128, 4, 64])
            p3 = psum_ap.rearrange("p (h r) -> p h r", h=4)
            x1 = p3[:, :, 0:64]
            x2 = p3[:, :, 64:128]
            o3 = outb_ap.rearrange("p (h r) -> p h r", h=4)
            t1 = t1v[...].rearrange("p (h r) -> p h r", h=4)[:, :, 0:64]
            t2 = t2v[...].rearrange("p (h r) -> p h r", h=4)[:, :, 0:64]
            v.tensor_tensor(t1, x1, csb, Alu.mult)
            v.tensor_tensor(t2, x2, snb, Alu.mult)
            v.tensor_tensor(o3[:, :, 0:64], t1, t2, Alu.subtract)
            v.tensor_tensor(t1, x1, snb, Alu.mult)
            v.tensor_tensor(t2, x2, csb, Alu.mult)
            return v.tensor_tensor(o3[:, :, 64:128], t1, t2, Alu.add)

        def rdv(v, src_ap, w=1):
            v.tensor_copy(jnk[:, 6:7], jnk[:, 7:8])
            return v.tensor_copy(jnk[:, 0:w], src_ap)

        def rac(a, src_ap, w=1):
            a.activation(ajk[:, 6:7], ajk[:, 7:8], AF.Identity)
            return a.activation(ajk[:, 0:w], src_ap, AF.Identity)

        def vgap(v):
            v.tensor_copy(stat[:, 31:32], stat[:, 30:31])

        def ln_dv_stats(v, xsap):
            v.reduce_sum(sv(0), xsap, AXX)
            vgap(v)
            v.tensor_scalar(sv(1), sv(0), 1.0 / D, None, Alu.mult)
            vgap(v)
            v.tensor_scalar(sv(2), sv(1), -1.0, None, Alu.mult)

        def ln_dv_var(v):
            v.tensor_scalar(sv(4), sv(3), 1.0 / D, None, Alu.mult)
            v.tensor_tensor(sv(5), sv(1), sv(1), Alu.mult)
            vgap(v)
            v.scalar_tensor_tensor(sv(6), sv(4), EPS, sv(5), Alu.add,
                                   Alu.subtract)

        def ln_ac_finish(a, xsap):
            a.activation(sv(7), sv(6), AF.Ln)
            a.activation(stat[:, 29:30], stat[:, 28:29], AF.Identity)
            a.activation(sv(8), sv(7), AF.Exp, scale=-0.5)
            a.activation(v_xsq[...], xsap, AF.Identity, bias=sv(2))
            return a.activation(v_xn[...], v_xsq[...], AF.Identity,
                                scale=sv(8))

        def xnt_transposes(t, dst_bf_view):
            last = None
            for ci in range(16):
                off = 256 * (ci % 8) + (ci // 8)
                last = t.transpose(dst_bf_view[:, ci * 128:(ci + 1) * 128],
                                   v_xn[:, off:off + 256 - (ci // 8):2],
                                   s_idn[...])
            return last

        with nc.Block() as blk:
            @blk.sync
            def _(e):
                for dst, src in [(s_idn, din["idn"]),
                                 (s_cosq, din["cosq"]), (s_sinq, din["sinq"]),
                                 (v_cosk, din["cosk"]), (v_sink, din["sink"])]:
                    e.dma_start(dst[...], src[...]).then_inc(SM["ld"].h, 16)
                SM["ld"].v += 80

            @blk.gpsimd
            def _(e):
                e.dma_start(v_wkb[...].rearrange("p (hc n) -> p hc n", hc=16),
                            din["wk"][...].rearrange("(hc p) n -> p hc n",
                                                     p=128)
                            ).then_inc(SM["lp"].h, 16)
                e.dma_start(v_wvb[...].rearrange("p (hc n) -> p hc n", hc=16),
                            din["wv"][...].rearrange("(hc p) n -> p hc n",
                                                     p=128)
                            ).then_inc(SM["lp"].h, 16)
                SM["lp"].v += 32

            @blk.vector
            def _(e):
                e.wait_ge(SM["ld"].h, SM["ld"].v)
                e.wait_ge(SM["lp"].h, SM["lp"].v)

        NTC = 32
        bs = {k: s.v for k, s in SM.items()}
        with nc.Block() as blk:
            @blk.sync
            def _(e):
                for tc in range(NTC):
                    if tc >= 2:
                        e.wait_ge(SM["ac"].h, bs["ac"] + (tc - 2) * 2 + 2)
                    xsn = "xs" + str(tc % 2)
                    e.dma_start(v_xs[tc % 2][...],
                                din["xseq"][tc * 128:(tc + 1) * 128, :]
                                ).then_inc(SM[xsn].h, 16)
                    SM[xsn].v += 16
                e.wait_ge(SM["dv2"].h, bs["dv2"] + NTC * 2)
                e.wait_ge(SM["ac2"].h, bs["ac2"] + NTC)
                e.dma_start(kvsp[:, 0:16384], v_kt[...]).then_inc(SM["sp"].h, 16)
                e.dma_start(kvsp[:, 16384:32768], v_v[...]).then_inc(
                    SM["sp"].h, 16)
                SM["sp"].v += 32
                if taps:
                    e.wait_ge(SM["sp"].h, SM["sp"].v)
                    e.dma_start(tapd["t_kt"][...], v_kt[...]).then_inc(
                        SM["sp"].h, 16)
                    e.dma_start(tapd["t_v"][...], v_v[...]).then_inc(
                        SM["sp"].h, 16)
                    SM["sp"].v += 32

            @blk.vector
            def _(e):
                for tc in range(NTC):
                    e.wait_ge(SM["xs" + str(tc % 2)].h,
                              bs["xs" + str(tc % 2)] + (tc // 2 + 1) * 16)
                    if tc >= 1:
                        e.wait_ge(SM["ac"].h, bs["ac"] + tc * 2)
                    ln_dv_stats(e, v_xs[tc % 2][...])
                    e.wait_ge(SM["ac"].h, bs["ac"] + tc * 2 + 1)
                    ln_dv_var(e)
                    rdv(e, stat[:, 0:8], 8).then_inc(SM["dv"].h, 1)
                    SM["dv"].v += 1
                    e.wait_ge(SM["pe2"].h, bs["pe2"] + tc * 2 + 1)
                    if tc >= 1:
                        e.wait_ge(SM["pe3"].h, bs["pe3"] + tc)
                    rope_emit(e, pB[:, 0:512],
                              v_krope[...],
                              v_cosk[:, tc * 64:(tc + 1) * 64],
                              v_sink[:, tc * 64:(tc + 1) * 64],
                              v_rt1, v_rt2).then_inc(SM["dv2"].h, 1)
                    SM["dv2"].v += 1
                    e.wait_ge(SM["pe3"].h, bs["pe3"] + tc + 1)
                    dst = v_kt[...].rearrange("p (h t) -> p h t", h=4)[
                        :, :, tc * 128:(tc + 1) * 128]
                    e.tensor_copy(dst, pBb[:, 3072:3584].rearrange(
                        "p (h t) -> p h t", h=4)).then_inc(SM["dv2"].h, 1)
                    SM["dv2"].v += 1

            @blk.scalar
            def _(e):
                for tc in range(NTC):
                    e.wait_ge(SM["xs" + str(tc % 2)].h,
                              bs["xs" + str(tc % 2)] + (tc // 2 + 1) * 16)
                    e.activation(v_xsq[...], v_xs[tc % 2][...], AF.Square,
                                 accum_out=sv(3))
                    rac(e, sv(3)).then_inc(SM["ac"].h, 1)
                    SM["ac"].v += 1
                    e.wait_ge(SM["dv"].h, bs["dv"] + tc + 1)
                    if tc >= 1:
                        e.wait_ge(SM["pe"].h, bs["pe"] + tc)
                    ln_ac_finish(e, v_xs[tc % 2][...]).then_inc(SM["ac"].h, 1)
                    SM["ac"].v += 1
                    e.wait_ge(SM["pe"].h, bs["pe"] + tc + 1)
                    e.activation(v_xntc[...], pAb[:, 0:2048],
                                 AF.Copy).then_inc(SM["ac3"].h, 1)
                    SM["ac3"].v += 1
                    e.wait_ge(SM["pe2"].h, bs["pe2"] + tc * 2 + 2)
                    e.activation(v_v[:, tc * 512:(tc + 1) * 512],
                                 pB[:, 512:1024], AF.Copy
                                 ).then_inc(SM["ac2"].h, 1)
                    SM["ac2"].v += 1

            @blk.tensor
            def _(e):
                for tc in range(NTC):
                    e.wait_ge(SM["ac"].h, bs["ac"] + tc * 2 + 2)
                    if tc >= 1:
                        e.wait_ge(SM["ac3"].h, bs["ac3"] + tc)
                    ins_ = xnt_transposes(e, pAb)
                    ins_.then_inc(SM["pe"].h, 1)
                    SM["pe"].v += 1
                    e.wait_ge(SM["ac3"].h, bs["ac3"] + tc + 1)
                    if tc >= 1:
                        e.wait_ge(SM["dv2"].h, bs["dv2"] + tc * 2)
                        e.wait_ge(SM["ac2"].h, bs["ac2"] + tc)
                    for ci in range(16):
                        ins_ = e.matmul(pB[:, 0:512],
                                        v_xntc[:, ci * 128:(ci + 1) * 128],
                                        v_wkb[:, ci * 512:(ci + 1) * 512],
                                        start=(ci == 0), stop=(ci == 15))
                    ins_.then_inc(SM["pe2"].h, 1)
                    for ci in range(16):
                        ins_ = e.matmul(pB[:, 512:1024],
                                        v_xntc[:, ci * 128:(ci + 1) * 128],
                                        v_wvb[:, ci * 512:(ci + 1) * 512],
                                        start=(ci == 0), stop=(ci == 15))
                    ins_.then_inc(SM["pe2"].h, 1)
                    SM["pe2"].v += 2
                    e.wait_ge(SM["dv2"].h, bs["dv2"] + tc * 2 + 1)
                    for h in range(4):
                        ins_ = e.transpose(
                            pBb[:, 3072 + h * 128:3072 + (h + 1) * 128],
                            v_krope[:, h * 128:(h + 1) * 128], s_idn[...])
                    ins_.then_inc(SM["pe3"].h, 1)
                    SM["pe3"].v += 1

        assert SM["dv2"].v == bs["dv2"] + 64 and SM["ac"].v == bs["ac"] + 64

        atsp = nc.dram_tensor("atsp", [2, 8, 128, 2048], BF16, kind="Internal")
        v_attstg = V(180, 4, BF16)
        v_atb = [V(160, 4, BF16), V(164, 4, BF16)]
        v_xo2 = [V(96, 2, F32), V(98, 2, F32)]
        v_ob2 = [V(100, 2, F32), V(102, 2, F32)]

        bs = {k: s.v for k, s in SM.items()}
        with nc.Block() as blk:
            @blk.sync
            def _(e):
                dma_fence(e, bs)
                for tl in range(16):
                    if tl >= 2:
                        e.wait_ge(SM["ac"].h, bs["ac"] + (tl - 2) * 2 + 2)
                    xsn = "xs" + str(tl % 2)
                    e.dma_start(v_xs[tl % 2][...],
                                din["xown"][tl * 128:(tl + 1) * 128, :]
                                ).then_inc(SM[xsn].h, 16)
                    SM[xsn].v += 16

            @blk.vector
            def _(e):
                for tl in range(16):
                    e.wait_ge(SM["xs" + str(tl % 2)].h,
                              bs["xs" + str(tl % 2)] + (tl // 2 + 1) * 16)
                    if tl >= 1:
                        e.wait_ge(SM["ac"].h, bs["ac"] + tl * 2)
                    ln_dv_stats(e, v_xs[tl % 2][...])
                    e.wait_ge(SM["ac"].h, bs["ac"] + tl * 2 + 1)
                    ln_dv_var(e)
                    rdv(e, stat[:, 0:8], 8).then_inc(SM["dv"].h, 1)
                    SM["dv"].v += 1

            @blk.scalar
            def _(e):
                for tl in range(16):
                    e.wait_ge(SM["xs" + str(tl % 2)].h,
                              bs["xs" + str(tl % 2)] + (tl // 2 + 1) * 16)
                    e.activation(v_xsq[...], v_xs[tl % 2][...], AF.Square,
                                 accum_out=sv(3))
                    rac(e, sv(3)).then_inc(SM["ac"].h, 1)
                    SM["ac"].v += 1
                    e.wait_ge(SM["dv"].h, bs["dv"] + tl + 1)
                    if tl >= 1:
                        e.wait_ge(SM["pe"].h, bs["pe"] + tl)
                    ln_ac_finish(e, v_xs[tl % 2][...]).then_inc(SM["ac"].h, 1)
                    SM["ac"].v += 1
                    e.wait_ge(SM["pe"].h, bs["pe"] + tl + 1)
                    dst = v_xnt2[...].rearrange("p (ci t) -> p ci t", ci=16)[
                        :, :, tl * 128:(tl + 1) * 128]
                    e.activation(dst, pAb[:, 0:2048].rearrange(
                        "p (ci k) -> p ci k", ci=16), AF.Copy
                        ).then_inc(SM["ac3"].h, 1)
                    SM["ac3"].v += 1

            @blk.tensor
            def _(e):
                for tl in range(16):
                    e.wait_ge(SM["ac"].h, bs["ac"] + tl * 2 + 2)
                    if tl >= 1:
                        e.wait_ge(SM["ac3"].h, bs["ac3"] + tl)
                    xnt_transposes(e, pAb).then_inc(SM["pe"].h, 1)
                    SM["pe"].v += 1

        bs = {k: s.v for k, s in SM.items()}
        NI1 = 128
        dv3c = 0
        DEC_G, DEC_U, DEC_D = [], [], {}
        for i in range(NI1):
            dv3c += 1
            DEC_G.append(dv3c)
            dv3c += 1
            DEC_U.append(dv3c)
            if i % 2 == 1:
                dv3c += 1
                DEC_D[i] = dv3c

        with nc.Block() as blk:
            @blk.sync
            def _(e):
                dma_fence(e, bs)
                for j in (0, 1):
                    e.dma_start(v_pgb2[j][...], din["pg2"][j]
                                ).then_inc(SM["pg" + str(j)].h, 16)
                    SM["pg" + str(j)].v += 16
                    e.dma_start(v_pub2[j][...], din["pu2"][j]
                                ).then_inc(SM["pu" + str(j)].h, 16)
                    SM["pu" + str(j)].v += 16
                for i in range(NI1):
                    if i + 2 < NI1:
                        e.wait_ge(SM["dv3"].h, bs["dv3"] + DEC_G[i])
                        e.dma_start(v_pgb2[i % 2][...], din["pg2"][i + 2]
                                    ).then_inc(SM["pg" + str(i % 2)].h, 16)
                        SM["pg" + str(i % 2)].v += 16
                        e.wait_ge(SM["dv3"].h, bs["dv3"] + DEC_U[i])
                        e.dma_start(v_pub2[i % 2][...], din["pu2"][i + 2]
                                    ).then_inc(SM["pu" + str(i % 2)].h, 16)
                        SM["pu" + str(i % 2)].v += 16
                    if i >= 1:
                        e.wait_ge(SM["dv"].h, bs["dv"] + i)
                        spn = "spi" + str((i - 1) % 2)
                        e.dma_start(intsp[(i - 1) // 2, (i - 1) % 2],
                                    v_intb2[(i - 1) % 2][...]
                                    ).then_inc(SM[spn].h, 16)
                        SM[spn].v += 16
                    if i % 2 == 1:
                        e.wait_ge(SM["dv3"].h, bs["dv3"] + DEC_D[i])
                        e.dma_start(wdsp[i // 2, 0], v_wde2[...]
                                    ).then_inc(SM["spw"].h, 16)
                        e.dma_start(wdsp[i // 2, 1], v_wdo2[...]
                                    ).then_inc(SM["spw"].h, 16)
                        SM["spw"].v += 32
                e.wait_ge(SM["dv"].h, bs["dv"] + NI1)
                e.dma_start(intsp[63, 1], v_intb2[1][...]
                            ).then_inc(SM["spi1"].h, 16)
                SM["spi1"].v += 16

            @blk.gpsimd
            def _(e):
                dma_fence(e, bs)
                for j in (0, 1):
                    e.dma_start(v_scg2[j][...], din["sg2"][j]
                                ).then_inc(SM["sg" + str(j)].h, 16)
                    SM["sg" + str(j)].v += 16
                    e.dma_start(v_scu2[j][...], din["su2"][j]
                                ).then_inc(SM["su" + str(j)].h, 16)
                    SM["su" + str(j)].v += 16
                e.dma_start(v_pdb2[...], din["pd"][0]
                            ).then_inc(SM["pdl"].h, 16)
                SM["pdl"].v += 16
                e.dma_start(v_scd2[...], din["sd2"][0]
                            ).then_inc(SM["sdl"].h, 16)
                SM["sdl"].v += 16
                for i in range(NI1):
                    if i + 2 < NI1:
                        e.wait_ge(SM["dv3"].h, bs["dv3"] + DEC_G[i])
                        e.dma_start(v_scg2[i % 2][...], din["sg2"][i + 2]
                                    ).then_inc(SM["sg" + str(i % 2)].h, 16)
                        SM["sg" + str(i % 2)].v += 16
                        e.wait_ge(SM["dv3"].h, bs["dv3"] + DEC_U[i])
                        e.dma_start(v_scu2[i % 2][...], din["su2"][i + 2]
                                    ).then_inc(SM["su" + str(i % 2)].h, 16)
                        SM["su" + str(i % 2)].v += 16
                    if i % 2 == 1 and i // 2 + 1 < NW:
                        e.wait_ge(SM["dv3"].h, bs["dv3"] + DEC_D[i])
                        e.dma_start(v_pdb2[...], din["pd"][i // 2 + 1]
                                    ).then_inc(SM["pdl"].h, 16)
                        SM["pdl"].v += 16
                        e.dma_start(v_scd2[...], din["sd2"][i // 2 + 1]
                                    ).then_inc(SM["sdl"].h, 16)
                        SM["sdl"].v += 16

            @blk.tensor
            def _(e):
                for i in range(NI1):
                    wge, wgo = v_wge2[i % 2], v_wgo2[i % 2]
                    wue, wuo = v_wue2[i % 2], v_wuo2[i % 2]
                    e.wait_ge(SM["dv3"].h, bs["dv3"] + DEC_G[i])
                    if i >= 1:
                        e.wait_ge(SM["ac2"].h, bs["ac2"] + 2 * (i - 1) + 1)
                    for tp in range(4):
                        for ci in range(16):
                            wsrc = (wge if ci < 8 else wgo)
                            c8 = ci % 8
                            ins_ = e.matmul(
                                pB[:, tp * 512:(tp + 1) * 512],
                                wsrc[:, c8 * 128:(c8 + 1) * 128],
                                v_xnt2[:, ci * 2048 + tp * 512:
                                       ci * 2048 + (tp + 1) * 512],
                                start=(ci == 0), stop=(ci == 15))
                    ins_.then_inc(SM["pe"].h, 1)
                    SM["pe"].v += 1
                    e.wait_ge(SM["dv3"].h, bs["dv3"] + DEC_U[i])
                    if i >= 1:
                        e.wait_ge(SM["ac2"].h, bs["ac2"] + 2 * (i - 1) + 2)
                    for tp in range(4):
                        for ci in range(16):
                            wsrc = (wue if ci < 8 else wuo)
                            c8 = ci % 8
                            ins_ = e.matmul(
                                pA[:, tp * 512:(tp + 1) * 512],
                                wsrc[:, c8 * 128:(c8 + 1) * 128],
                                v_xnt2[:, ci * 2048 + tp * 512:
                                       ci * 2048 + (tp + 1) * 512],
                                start=(ci == 0), stop=(ci == 15))
                    ins_.then_inc(SM["pe"].h, 1)
                    SM["pe"].v += 1

            @blk.scalar
            def _(e):
                for i in range(NI1):
                    e.wait_ge(SM["pe"].h, bs["pe"] + 2 * i + 1)
                    if i >= 1:
                        e.wait_ge(SM["dv"].h, bs["dv"] + i)
                    e.activation(v_gsb2[...], pB[...], AF.Gelu_apprx_tanh
                                 ).then_inc(SM["ac2"].h, 1)
                    SM["ac2"].v += 1
                    e.wait_ge(SM["pe"].h, bs["pe"] + 2 * i + 2)
                    e.activation(v_upsb2[...], pA[...], AF.Copy
                                 ).then_inc(SM["ac2"].h, 1)
                    SM["ac2"].v += 1

            @blk.vector
            def _(e):
                for i in range(NI1):
                    p = i % 2
                    e.wait_ge(SM["pg" + str(p)].h,
                              bs["pg" + str(p)] + (i // 2 + 1) * 16)
                    e.wait_ge(SM["sg" + str(p)].h,
                              bs["sg" + str(p)] + (i // 2 + 1) * 16)
                    if i >= 2:
                        e.wait_ge(SM["pe"].h, bs["pe"] + 2 * (i - 2) + 1)
                    decode2(e, v_pgb2[p][...], v_wge2[p][...],
                            v_wgo2[p][...], v_scg2[p][...]
                            ).then_inc(SM["dv3"].h, 1)
                    SM["dv3"].v += 1
                    e.wait_ge(SM["pu" + str(p)].h,
                              bs["pu" + str(p)] + (i // 2 + 1) * 16)
                    e.wait_ge(SM["su" + str(p)].h,
                              bs["su" + str(p)] + (i // 2 + 1) * 16)
                    if i >= 2:
                        e.wait_ge(SM["pe"].h, bs["pe"] + 2 * (i - 2) + 2)
                    decode2(e, v_pub2[p][...], v_wue2[p][...],
                            v_wuo2[p][...], v_scu2[p][...]
                            ).then_inc(SM["dv3"].h, 1)
                    SM["dv3"].v += 1
                    if i % 2 == 1:
                        wd = i // 2
                        e.wait_ge(SM["pdl"].h, bs["pdl"] + (wd + 1) * 16)
                        e.wait_ge(SM["sdl"].h, bs["sdl"] + (wd + 1) * 16)
                        if wd >= 1:
                            e.wait_ge(SM["spw"].h, bs["spw"] + wd * 32)
                        for hf in range(2):
                            sl = slice(hf * 1024, (hf + 1) * 1024)
                            ins_ = decode2(e, v_pdb2[:, sl], v_wde2[:, sl],
                                           v_wdo2[:, sl], v_scd2[:, sl])
                        ins_.then_inc(SM["dv3"].h, 1)
                        SM["dv3"].v += 1
                    if i >= 1:
                        e.wait_ge(SM["ac2"].h, bs["ac2"] + 2 * (i - 1) + 2)
                        if i - 1 >= 2:
                            e.wait_ge(SM["spi" + str((i - 1) % 2)].h,
                                      bs["spi" + str((i - 1) % 2)] +
                                      ((i - 1) // 2) * 16)
                        e.tensor_tensor(v_intb2[(i - 1) % 2][...],
                                        v_gsb2[...], v_upsb2[...], Alu.mult
                                        ).then_inc(SM["dv"].h, 1)
                        SM["dv"].v += 1
                e.wait_ge(SM["ac2"].h, bs["ac2"] + 2 * (NI1 - 1) + 2)
                e.wait_ge(SM["spi1"].h, bs["spi1"] + ((NI1 - 1) // 2) * 16)
                e.tensor_tensor(v_intb2[(NI1 - 1) % 2][...],
                                v_gsb2[...], v_upsb2[...], Alu.mult
                                ).then_inc(SM["dv"].h, 1)
                SM["dv"].v += 1

        def emit_half(h):
            bs = {k: s.v for k, s in SM.items()}
            with nc.Block() as blk:
                @blk.sync
                def _(e):
                    dma_fence(e, bs)
                    for tl in range(8):
                        if tl >= 2:
                            e.wait_ge(SM["ac"].h, bs["ac"] + (tl - 2) * 2 + 2)
                        r0 = h * 1024 + tl * 128
                        xsn = "xs" + str(tl % 2)
                        e.dma_start(v_xs[tl % 2][...],
                                    din["xown"][r0:r0 + 128, :]
                                    ).then_inc(SM[xsn].h, 16)
                        SM[xsn].v += 16

                @blk.vector
                def _(e):
                    for tl in range(8):
                        e.wait_ge(SM["xs" + str(tl % 2)].h,
                                  bs["xs" + str(tl % 2)] + (tl // 2 + 1) * 16)
                        if tl >= 1:
                            e.wait_ge(SM["ac"].h, bs["ac"] + tl * 2)
                        ln_dv_stats(e, v_xs[tl % 2][...])
                        e.wait_ge(SM["ac"].h, bs["ac"] + tl * 2 + 1)
                        ln_dv_var(e)
                        rdv(e, stat[:, 0:8], 8).then_inc(SM["dv"].h, 1)
                        SM["dv"].v += 1

                @blk.scalar
                def _(e):
                    for tl in range(8):
                        e.wait_ge(SM["xs" + str(tl % 2)].h,
                                  bs["xs" + str(tl % 2)] + (tl // 2 + 1) * 16)
                        e.activation(v_xsq[...], v_xs[tl % 2][...], AF.Square,
                                     accum_out=sv(3))
                        rac(e, sv(3)).then_inc(SM["ac"].h, 1)
                        SM["ac"].v += 1
                        e.wait_ge(SM["dv"].h, bs["dv"] + tl + 1)
                        if tl >= 1:
                            e.wait_ge(SM["pe"].h, bs["pe"] + tl)
                        ln_ac_finish(e, v_xs[tl % 2][...]
                                     ).then_inc(SM["ac"].h, 1)
                        SM["ac"].v += 1
                        e.wait_ge(SM["pe"].h, bs["pe"] + tl + 1)
                        dst = v_xnt[...].rearrange("p (ci t) -> p ci t", ci=16)[
                            :, :, tl * 128:(tl + 1) * 128]
                        e.activation(dst, pAb[:, 0:2048].rearrange(
                            "p (ci k) -> p ci k", ci=16), AF.Copy
                            ).then_inc(SM["ac3"].h, 1)
                        SM["ac3"].v += 1

                @blk.tensor
                def _(e):
                    for tl in range(8):
                        e.wait_ge(SM["ac"].h, bs["ac"] + tl * 2 + 2)
                        if tl >= 1:
                            e.wait_ge(SM["ac3"].h, bs["ac3"] + tl)
                        ins_ = xnt_transposes(e, pAb)
                        ins_.then_inc(SM["pe"].h, 1)
                        SM["pe"].v += 1

                @blk.gpsimd
                def _(e):
                    if taps and h == 0:
                        e.wait_ge(SM["ac3"].h, bs["ac3"] + 8)
                        e.dma_start(tapd["t_xnt"][...], v_xnt[...]
                                    ).then_inc(SM["os"].h, 16)
                        SM["os"].v += 16

            bs = {k: s.v for k, s in SM.items()}
            with nc.Block() as blk:
                @blk.sync
                def _(e):
                    dma_fence(e, bs)
                    i = 0
                    for cc in range(4):
                        for w in range(NW):
                            if i >= 2:
                                e.wait_ge(SM["pe"].h, bs["pe"] + i - 1)
                            src = intsp[w, :, :, h * 1024:(h + 1) * 1024
                                        ].transpose([1, 0, 2])
                            e.dma_start(v_itb[i % 2][...].rearrange(
                                "p (eo t) -> p eo t", eo=2), src
                            ).then_inc(SM["it" + str(i % 2)].h, 16)
                            SM["it" + str(i % 2)].v += 16
                            i += 1

                @blk.gpsimd
                def _(e):
                    dma_fence(e, bs)
                    i = 0
                    for cc in range(4):
                        for w in range(NW):
                            if i >= 2:
                                e.wait_ge(SM["pe"].h, bs["pe"] + i - 1)
                            src = wdsp[w, :, :, cc * 512:(cc + 1) * 512
                                       ].transpose([1, 0, 2])
                            e.dma_start(v_wdb[i % 2][...].rearrange(
                                "p (eo c) -> p eo c", eo=2), src
                            ).then_inc(SM["wd" + str(i % 2)].h, 16)
                            SM["wd" + str(i % 2)].v += 16
                            i += 1

                @blk.tensor
                def _(e):
                    i = 0
                    for cc in range(4):
                        for w in range(NW):
                            e.wait_ge(SM["it" + str(i % 2)].h,
                                      bs["it" + str(i % 2)] +
                                      (i // 2 + 1) * 16)
                            e.wait_ge(SM["wd" + str(i % 2)].h,
                                      bs["wd" + str(i % 2)] +
                                      (i // 2 + 1) * 16)
                            if w == 0 and cc > 0:
                                e.wait_ge(SM["ac"].h, bs["ac"] + cc)
                            for eo in range(2):
                                for tt in range(8):
                                    pb = (pA if tt < 4 else pB)
                                    ins_ = e.matmul(
                                        pb[:, (tt % 4) * 512:
                                           (tt % 4 + 1) * 512],
                                        v_itb[i % 2][:, eo * 1024 + tt * 128:
                                                     eo * 1024 + (tt + 1) * 128],
                                        v_wdb[i % 2][:, eo * 512:
                                                     (eo + 1) * 512],
                                        start=(w == 0 and eo == 0),
                                        stop=(w == NW - 1 and eo == 1))
                            ins_.then_inc(SM["pe"].h, 1)
                            SM["pe"].v += 1
                            i += 1

                @blk.scalar
                def _(e):
                    for cc in range(4):
                        e.wait_ge(SM["pe"].h, bs["pe"] + (cc + 1) * NW)
                        for tt in range(8):
                            pb = (pA if tt < 4 else pB)
                            ins_ = e.activation(
                                v_g[:, tt * 2048 + cc * 512:
                                    tt * 2048 + (cc + 1) * 512],
                                pb[:, (tt % 4) * 512:(tt % 4 + 1) * 512],
                                AF.Silu)
                        ins_.then_inc(SM["ac"].h, 1)
                        SM["ac"].v += 1

                @blk.gpsimd
                def _(e):
                    if taps and h == 0:
                        e.wait_ge(SM["ac"].h, bs["ac"] + 4)
                        e.dma_start(tapd["t_g"][...], v_g[...]
                                    ).then_inc(SM["os"].h, 16)
                        e.dma_start(tapd["t_int"][...], intsp[0, 0]
                                    ).then_inc(SM["os"].h, 16)
                        e.dma_start(tapd["t_wd"][...], wdsp[0, 0]
                                    ).then_inc(SM["os"].h, 16)
                        e.dma_start(tapd["t_wg"][...], intsp[0, 1]
                                    ).then_inc(SM["os"].h, 16)
                        SM["os"].v += 64

            bs = {k: s.v for k, s in SM.items()}
            with nc.Block() as blk:
                @blk.gpsimd
                def _(e):
                    dma_fence(e, bs)
                    for dq in range(4):
                        if dq >= 2:
                            e.wait_ge(SM["pe"].h, bs["pe"] + (dq - 1) * 8)
                        e.dma_start(v_wqb[dq % 2][...].rearrange(
                                        "p (hc n) -> p hc n", hc=16),
                                    din["wq"][:, dq * 512:(dq + 1) * 512
                                              ].rearrange(
                                        "(hc p) n -> p hc n", p=128)
                                    ).then_inc(SM["wq" + str(dq % 2)].h, 16)
                        SM["wq" + str(dq % 2)].v += 16

                @blk.tensor
                def _(e):
                    for dq in range(4):
                        e.wait_ge(SM["wq" + str(dq % 2)].h,
                                  bs["wq" + str(dq % 2)] + (dq // 2 + 1) * 16)
                        for tt in range(8):
                            idx = dq * 8 + tt
                            pslot = pA[:, (idx % 2) * 512:(idx % 2 + 1) * 512]
                            if idx >= 2:
                                e.wait_ge(SM["dv"].h, bs["dv"] + idx - 1)
                            for ci in range(16):
                                ins_ = e.matmul(
                                    pslot,
                                    v_xnt[:, ci * 1024 + tt * 128:
                                          ci * 1024 + (tt + 1) * 128],
                                    v_wqb[dq % 2][:, ci * 512:(ci + 1) * 512],
                                    start=(ci == 0), stop=(ci == 15))
                            ins_.then_inc(SM["pe"].h, 1)
                            SM["pe"].v += 1

                @blk.vector
                def _(e):
                    for dq in range(4):
                        for tt in range(8):
                            idx = dq * 8 + tt
                            e.wait_ge(SM["pe"].h, bs["pe"] + idx + 1)
                            o = 8 * h + tt
                            pslot = pA[:, (idx % 2) * 512:(idx % 2 + 1) * 512]
                            rope_emit(
                                e, pslot,
                                v_q[:, tt * 2048 + dq * 512:
                                    tt * 2048 + (dq + 1) * 512],
                                s_cosq[:, o * 64:(o + 1) * 64],
                                s_sinq[:, o * 64:(o + 1) * 64],
                                v_art1, v_art2).then_inc(SM["dv"].h, 1)
                            SM["dv"].v += 1

                @blk.gpsimd
                def _(e):
                    if taps and h == 0:
                        e.wait_ge(SM["dv"].h, bs["dv"] + 32)
                        e.dma_start(tapd["t_q"][...], v_q[...]
                                    ).then_inc(SM["os"].h, 16)
                        SM["os"].v += 16

            bs = {k: s.v for k, s in SM.items()}
            iters = [(j, hd) for j in range(8) for hd in range(NQH)]
            NI = len(iters)
            nchs = [nch_of(8 * h + j) for (j, hd) in iters]
            ngrps = [(n + 1) // 2 for n in nchs]
            PEb = np.cumsum([0] + [1] * NI)
            PE2b = np.cumsum([0] + ngrps)
            PE3b = np.cumsum([0] + [2 * n for n in nchs])
            DVb = np.cumsum([0] + [2] * NI)
            DV2b = np.cumsum([0] + [g + n for g, n in zip(ngrps, nchs)])
            DV3b = np.cumsum([0] + [1] * NI)
            ACb = np.cumsum([0] + ngrps)
            AC2b = np.cumsum([0] + [1] * NI)
            AC3b = np.cumsum([0] + [1] * NI)

            with nc.Block() as blk:
                @blk.sync
                def _(e):
                    dma_fence(e, bs)
                    e.dma_start(v_kt[...], kvsp[:, 0:16384]
                                ).then_inc(SM["ktv"].h, 16)
                    e.dma_start(v_v[...], kvsp[:, 16384:32768]
                                ).then_inc(SM["ktv"].h, 16)
                    SM["ktv"].v += 32
                    for j in (0, 1):
                        e.dma_start(v_mkb[j % 2][...], din["dmask"][8 * h + j]
                                    ).then_inc(SM["mk" + str(j % 2)].h, 16)
                        SM["mk" + str(j % 2)].v += 16
                    for j in range(8):
                        e.wait_ge(SM["ac3"].h,
                                  bs["ac3"] + int(AC3b[j * NQH + NQH]))
                        e.dma_start(atsp[h, j], v_attstg[...]
                                    ).then_inc(SM["spa"].h, 16)
                        SM["spa"].v += 16
                        if j + 2 < 8:
                            e.wait_ge(SM["dv3"].h,
                                      bs["dv3"] + int(DV3b[(j + 1) * NQH]))
                            e.dma_start(v_mkb[j % 2][...],
                                        din["dmask"][8 * h + j + 2]
                                        ).then_inc(SM["mk" + str(j % 2)].h, 16)
                            SM["mk" + str(j % 2)].v += 16
                    if taps and h == 0:
                        e.wait_ge(SM["spa"].h, bs["spa"] + 128)
                        e.dma_start(
                            tapd["t_at"][...].rearrange("p (j n) -> (p j) n",
                                                        j=8),
                            atsp[0]
                        ).then_inc(SM["os"].h, 16)
                        SM["os"].v += 16

                @blk.tensor
                def _(e):
                    e.wait_ge(SM["ktv"].h, bs["ktv"] + 32)
                    for it, (j, hd) in enumerate(iters):
                        nch, ngrp = nchs[it], ngrps[it]
                        kv = hd // 4
                        if it >= 1:
                            e.wait_ge(SM["ac3"].h, bs["ac3"] + int(AC3b[it]))
                        if it >= 1:
                            e.wait_ge(SM["dv2"].h, bs["dv2"] + int(DV2b[it]))
                        ins_ = e.transpose(
                            pBb[:, 3584:3712],
                            v_q[:, j * 2048 + hd * 128:
                                j * 2048 + (hd + 1) * 128],
                            s_idn[...])
                        ins_.then_inc(SM["pe"].h, 1)
                        SM["pe"].v += 1
                        e.wait_ge(SM["dv"].h, bs["dv"] + int(DVb[it]) + 1)
                        for gi in range(ngrp):
                            half = gi % 2
                            if gi >= 2:
                                e.wait_ge(SM["ac"].h,
                                          bs["ac"] + int(ACb[it]) + gi - 1)
                            elif it >= 1:
                                e.wait_ge(SM["ac"].h, bs["ac"] + int(ACb[it]))
                            for c in range(gi * 2, min(nch, gi * 2 + 2)):
                                ins_ = e.matmul(
                                    pA[:, half * 1024 + (c - gi * 2) * 512:
                                       half * 1024 + (c - gi * 2 + 1) * 512],
                                    v_qt[it % 2][...],
                                    v_kt[:, kv * 4096 + c * 512:
                                         kv * 4096 + (c + 1) * 512],
                                    start=True, stop=True)
                            ins_.then_inc(SM["pe2"].h, 1)
                            SM["pe2"].v += 1
                        e.wait_ge(SM["dv3"].h, bs["dv3"] + int(DV3b[it]) + 1)
                        for c in range(nch):
                            for kt in range(4):
                                ins_ = e.transpose(
                                    pBb[:, 3072 + kt * 128:
                                        3072 + (kt + 1) * 128],
                                    v_p[it % 2][:, c * 512 + kt * 128:
                                                c * 512 + (kt + 1) * 128],
                                    s_idn[...])
                            ins_.then_inc(SM["pe3"].h, 1)
                            SM["pe3"].v += 1
                            e.wait_ge(SM["dv2"].h,
                                      bs["dv2"] + int(DV2b[it]) + ngrp + c + 1)
                            for kt in range(4):
                                kg = c * 4 + kt
                                ins_ = e.matmul(
                                    pB[:, 1024:1152],
                                    v_v[:, kg * 512 + kv * 128:
                                        kg * 512 + (kv + 1) * 128],
                                    v_ptb[:, kt * 128:(kt + 1) * 128],
                                    start=(c == 0 and kt == 0),
                                    stop=(c == nch - 1 and kt == 3))
                            ins_.then_inc(SM["pe3"].h, 1)
                            SM["pe3"].v += 1

                @blk.vector
                def _(e):
                    for it, (j, hd) in enumerate(iters):
                        nch, ngrp = nchs[it], ngrps[it]
                        e.wait_ge(SM["pe"].h, bs["pe"] + int(PEb[it]) + 1)
                        if it >= 2:
                            e.wait_ge(SM["pe2"].h, bs["pe2"] + int(PE2b[it - 1]))
                        e.tensor_copy(v_qt[it % 2][...], pBb[:, 3584:3712]
                                      ).then_inc(SM["dv"].h, 1)
                        e.wait_ge(SM["mk" + str(j % 2)].h,
                                  bs["mk" + str(j % 2)] + (j // 2 + 1) * 16)
                        for gi in range(ngrp):
                            half = gi % 2
                            e.wait_ge(SM["pe2"].h,
                                      bs["pe2"] + int(PE2b[it]) + gi + 1)
                            lim = min(nch, gi * 2 + 2)
                            for c in range(gi * 2, lim):
                                ps = pA[:, half * 1024 + (c - gi * 2) * 512:
                                        half * 1024 + (c - gi * 2 + 1) * 512]
                                wn = min(2, nch)
                                if c >= nch - wn:
                                    w2 = c - (nch - wn)
                                    e.scalar_tensor_tensor(
                                        ps, ps, 1.0,
                                        v_mkb[j % 2][:, w2 * 512:
                                                     (w2 + 1) * 512],
                                        Alu.mult, Alu.add)
                                e.reduce_max(mxc[:, c:c + 1], ps, AXX)
                            vgap(e)
                            if lim - gi * 2 == 2:
                                e.tensor_tensor(smc[:, gi:gi + 1],
                                                mxc[:, gi * 2:gi * 2 + 1],
                                                mxc[:, gi * 2 + 1:gi * 2 + 2],
                                                Alu.max)
                            else:
                                e.tensor_copy(smc[:, gi:gi + 1],
                                              mxc[:, gi * 2:gi * 2 + 1])
                            vgap(e)
                            e.tensor_scalar(gnbt[:, gi:gi + 1],
                                            smc[:, gi:gi + 1], -ISQ, None,
                                            Alu.mult)
                            rdv(e, gnbt[:, gi:gi + 1]
                                ).then_inc(SM["dv2"].h, 1)
                            SM["dv2"].v += 1
                        e.reduce_max(sv(9), smc[:, 0:ngrp], AXX)
                        vgap(e)
                        e.tensor_scalar(sv(10), sv(9), -ISQ, None, Alu.mult)
                        rdv(e, sv(10)).then_inc(SM["dv"].h, 1)
                        SM["dv"].v += 2
                        e.wait_ge(SM["ac2"].h, bs["ac2"] + int(AC2b[it]) + 1)
                        for gi in range(ngrp):
                            lim = min(nch, gi * 2 + 2)
                            if lim - gi * 2 == 2:
                                e.tensor_tensor(smc[:, 4 + gi:5 + gi],
                                                csum[:, gi * 2:gi * 2 + 1],
                                                csum[:, gi * 2 + 1:gi * 2 + 2],
                                                Alu.add)
                            else:
                                e.tensor_copy(smc[:, 4 + gi:5 + gi],
                                              csum[:, gi * 2:gi * 2 + 1])
                        if ngrp == 1:
                            vgap(e)
                        for gi in range(ngrp):
                            e.tensor_tensor(smc[:, 4 + gi:5 + gi],
                                            smc[:, 4 + gi:5 + gi],
                                            nfc[:, gi:gi + 1], Alu.mult)
                        vgap(e)
                        e.reduce_sum(sv(11), smc[:, 4:4 + ngrp], AXX)
                        vgap(e)
                        e.reciprocal(sv(12), sv(11))
                        vgap(e)
                        e.tensor_scalar(nfc[:, 0:ngrp], nfc[:, 0:ngrp],
                                        sv(12), None, Alu.mult)
                        vgap(e)
                        for gi in range(ngrp):
                            lim = min(nch, gi * 2 + 2)
                            width = lim - gi * 2
                            ins_ = e.tensor_scalar(
                                v_p[it % 2][:, gi * 1024:
                                            gi * 1024 + width * 512],
                                v_p[it % 2][:, gi * 1024:
                                            gi * 1024 + width * 512],
                                nfc[:, gi:gi + 1], None, Alu.mult)
                        ins_.then_inc(SM["dv3"].h, 1)
                        SM["dv3"].v += 1
                        for c in range(nch):
                            e.wait_ge(SM["pe3"].h,
                                      bs["pe3"] + int(PE3b[it]) + 2 * c + 1)
                            e.tensor_copy(v_ptb[...], pBb[:, 3072:3584]
                                          ).then_inc(SM["dv2"].h, 1)
                            SM["dv2"].v += 1

                @blk.scalar
                def _(e):
                    for it, (j, hd) in enumerate(iters):
                        nch, ngrp = nchs[it], ngrps[it]
                        for gi in range(ngrp):
                            half = gi % 2
                            lim = min(nch, gi * 2 + 2)
                            e.wait_ge(SM["dv2"].h,
                                      bs["dv2"] + int(DV2b[it]) + gi + 1)
                            if it >= 2 and gi == 0:
                                e.wait_ge(SM["pe3"].h,
                                          bs["pe3"] + int(PE3b[it - 1]))
                            for c in range(gi * 2, lim):
                                ps = pA[:, half * 1024 + (c - gi * 2) * 512:
                                        half * 1024 + (c - gi * 2 + 1) * 512]
                                e.activation(
                                    v_p[it % 2][:, c * 512:(c + 1) * 512],
                                    ps, AF.Exp, scale=ISQ,
                                    bias=gnbt[:, gi:gi + 1],
                                    accum_out=csum[:, c:c + 1])
                            rac(e, csum[:, gi * 2:lim], lim - gi * 2
                                ).then_inc(SM["ac"].h, 1)
                            SM["ac"].v += 1
                        e.wait_ge(SM["dv"].h, bs["dv"] + int(DVb[it]) + 2)
                        e.activation(nfc[:, 0:ngrp], smc[:, 0:ngrp], AF.Exp,
                                     scale=ISQ, bias=sv(10))
                        rac(e, nfc[:, 0:ngrp], ngrp).then_inc(SM["ac2"].h, 1)
                        SM["ac2"].v += 1
                        e.wait_ge(SM["pe3"].h,
                                  bs["pe3"] + int(PE3b[it]) + 2 * nch)
                        if hd == 0 and j >= 1:
                            e.wait_ge(SM["spa"].h, bs["spa"] + 16 * j)
                        e.activation(v_attstg[:, hd * 128:(hd + 1) * 128],
                                     pB[:, 1024:1152], AF.Copy
                                     ).then_inc(SM["ac3"].h, 1)
                        SM["ac3"].v += 1

            bs = {k: s.v for k, s in SM.items()}
            with nc.Block() as blk:
                @blk.gpsimd
                def _(e):
                    def wob_load(cc):
                        e.dma_start(v_wob[cc % 2][...].rearrange(
                                        "p (hc n) -> p hc n", hc=16),
                                    din["wo"][:, cc * 512:(cc + 1) * 512
                                              ].rearrange(
                                        "(hc p) n -> p hc n", p=128)
                                    ).then_inc(SM["wo" + str(cc % 2)].h, 16)
                        SM["wo" + str(cc % 2)].v += 16
                    dma_fence(e, bs)
                    wob_load(0)
                    wob_load(1)
                    i = 0
                    for cc in range(4):
                        for j in range(8):
                            e.wait_ge(SM["dv"].h, bs["dv"] + i + 1)
                            r0 = h * 1024 + j * 128
                            e.dma_start(
                                out[r0:r0 + 128, cc * 512:(cc + 1) * 512],
                                v_ob2[i % 2][...]
                            ).then_inc(SM["os" + str(i % 2)].h, 16)
                            SM["os" + str(i % 2)].v += 16
                            i += 1
                        if cc + 2 < 4:
                            e.wait_ge(SM["pe"].h, bs["pe"] + (cc + 1) * 8)
                            wob_load(cc + 2)

                @blk.sync
                def _(e):
                    dma_fence(e, bs)
                    i = 0
                    for cc in range(4):
                        for j in range(8):
                            if i >= 2:
                                e.wait_ge(SM["pe"].h, bs["pe"] + i - 1)
                            e.dma_start(v_atb[i % 2][...], atsp[h, j]
                                        ).then_inc(SM["at" + str(i % 2)].h, 16)
                            SM["at" + str(i % 2)].v += 16
                            if i >= 2:
                                e.wait_ge(SM["dv"].h, bs["dv"] + i - 1)
                            r0 = h * 1024 + j * 128
                            e.dma_start(v_xo2[i % 2][...],
                                        din["xown"][r0:r0 + 128,
                                                    cc * 512:(cc + 1) * 512]
                                        ).then_inc(SM["xo" + str(i % 2)].h, 16)
                            SM["xo" + str(i % 2)].v += 16
                            i += 1

                @blk.tensor
                def _(e):
                    i = 0
                    for cc in range(4):
                        e.wait_ge(SM["wo" + str(cc % 2)].h,
                                  bs["wo" + str(cc % 2)] + (cc // 2 + 1) * 16)
                        for j in range(8):
                            pslot = pA[:, (i % 4) * 512:(i % 4 + 1) * 512]
                            e.wait_ge(SM["at" + str(i % 2)].h,
                                      bs["at" + str(i % 2)] +
                                      (i // 2 + 1) * 16)
                            if i >= 4:
                                e.wait_ge(SM["dv"].h, bs["dv"] + i - 3)
                            for ci in range(16):
                                ins_ = e.matmul(
                                    pslot,
                                    v_atb[i % 2][:, ci * 128:(ci + 1) * 128],
                                    v_wob[cc % 2][:, ci * 512:(ci + 1) * 512],
                                    start=(ci == 0), stop=(ci == 15))
                            ins_.then_inc(SM["pe"].h, 1)
                            SM["pe"].v += 1
                            i += 1

                @blk.vector
                def _(e):
                    i = 0
                    for cc in range(4):
                        for j in range(8):
                            e.wait_ge(SM["pe"].h, bs["pe"] + i + 1)
                            e.wait_ge(SM["xo" + str(i % 2)].h,
                                      bs["xo" + str(i % 2)] +
                                      (i // 2 + 1) * 16)
                            if i >= 2:
                                e.wait_ge(SM["os" + str(i % 2)].h,
                                          bs["os" + str(i % 2)] +
                                          (i // 2) * 16)
                            pslot = pA[:, (i % 4) * 512:(i % 4 + 1) * 512]
                            e.tensor_tensor(
                                v_ob2[i % 2][...], pslot,
                                v_g[:, j * 2048 + cc * 512:
                                    j * 2048 + (cc + 1) * 512], Alu.mult)
                            e.tensor_tensor(v_ob2[i % 2][...], v_ob2[i % 2][...],
                                            v_xo2[i % 2][...], Alu.add
                                            ).then_inc(SM["dv"].h, 1)
                            SM["dv"].v += 1
                            i += 1


        emit_half(0)
        emit_half(1)

        with nc.Block() as blk:
            @blk.gpsimd
            def _(e):
                e.wait_ge(SM["os0"].h, SM["os0"].v)
                e.wait_ge(SM["os1"].h, SM["os1"].v)
                e.wait_ge(SM["os"].h, SM["os"].v)
    return nc


def kernel(**inputs):
    import os as _os
    taps = bool(_os.environ.get("MK_TAPS"))
    maps = _host_stage(inputs)
    nc = build_program(taps=taps)
    res = run_bass_kernel_spmd(nc, maps, core_ids=list(range(8)))
    full = np.empty((NB, S, D), np.float32)
    for c in range(8):
        par = c % 2
        full[c // 2].reshape(32, 128, D)[par::2] = \
            np.asarray(res.results[c]["out"]).reshape(NT, 128, D)
    if taps:
        kernel.last_taps = res.results
    return full


# version 05
# speedup vs baseline: 1.4644x; 1.0023x over previous
import numpy as np
import ml_dtypes
import concourse.bass as bass
import concourse.mybir as mybir
from concourse.bass_utils import run_bass_kernel_spmd

F32 = mybir.dt.float32
BF16 = mybir.dt.bfloat16
I32 = mybir.dt.int32
U16 = mybir.dt.uint16
U8 = mybir.dt.uint8
AF = mybir.ActivationFunctionType
Alu = mybir.AluOpType
AXX = mybir.AxisListType.X
bf = ml_dtypes.bfloat16

D, S, NB = 2048, 4096, 4
NQH, NKV, HD = 16, 4, 128
MLP, NW, TOWN, NT = 16384, 64, 2048, 16
ISQ = 1.0 / float(np.sqrt(HD))
GA = 0.7978845608028654
GB = GA * 0.044715
EPS = 1e-5
_PERM = np.concatenate([np.arange(0, D, 2), np.arange(1, D, 2)])
F2_SCALAR_DMA = False


def _host_stage(inputs):
    x = np.asarray(inputs["L0_attn_x"], dtype=np.float32)
    Wq = np.asarray(inputs["L0_attn_Wq"], dtype=np.float32)
    Wk = np.asarray(inputs["L0_attn_Wk"], dtype=np.float32)
    Wv = np.asarray(inputs["L0_attn_Wv"], dtype=np.float32)
    Wo = np.asarray(inputs["L0_attn_Wo"], dtype=np.float32)
    gp = np.asarray(inputs["L0_ffn_gate_W_packed"], dtype=np.int32)
    gs = np.asarray(inputs["L0_ffn_gate_scales"], dtype=np.float32)
    up = np.asarray(inputs["L0_ffn_up_W_packed"], dtype=np.int32)
    us = np.asarray(inputs["L0_ffn_up_scales"], dtype=np.float32)
    dp = np.asarray(inputs["L0_ffn_down_W_packed"], dtype=np.int32)
    ds = np.asarray(inputs["L0_ffn_down_scales"], dtype=np.float32)

    dperm = np.concatenate([np.arange(0, HD, 2), np.arange(1, HD, 2)])
    qcol = (np.arange(NQH)[:, None] * HD + dperm[None, :]).reshape(-1)
    kcol = (np.arange(NKV)[:, None] * HD + dperm[None, :]).reshape(-1)
    wq_h = np.ascontiguousarray(Wq.T[_PERM][:, qcol])
    wk_h = np.ascontiguousarray(Wk.T[_PERM][:, kcol])
    wv_h = np.ascontiguousarray(Wv.T[_PERM, :])
    wo_h = np.ascontiguousarray(Wo.T)

    jperm = np.concatenate([np.arange(0, 256, 2), np.arange(1, 256, 2)])

    def block_packed(P):
        PT = P.T
        kp, n = PT.shape
        blk = PT.reshape(8, kp // 8, n // 256, 256).transpose(2, 1, 0, 3)
        blk = blk[:, :, :, jperm]
        return np.ascontiguousarray(blk.reshape(n // 256, kp // 8, 2048))

    def iterize(Ph):
        t = Ph.reshape(64, 128, 8, 2, 128).transpose(0, 3, 1, 2, 4)
        return np.ascontiguousarray(t.reshape(128, 128, 1024))

    pg2_h = iterize(block_packed(gp))
    pu2_h = iterize(block_packed(up))
    pd_h = np.ascontiguousarray(dp.T.reshape(64, 128, 2048))

    def block_scales(Sc):
        blk = Sc.T.reshape(8, 8, 64, 256).transpose(2, 1, 0, 3)
        blk = blk[:, :, :, jperm]
        return np.ascontiguousarray(blk.reshape(64, 8, 2048))

    def scale_iterize(Sh):
        t = Sh.reshape(64, 8, 8, 2, 128).transpose(0, 3, 1, 2, 4)
        t = np.repeat(t, 16, axis=2)
        return np.ascontiguousarray(t.reshape(128, 128, 1024))

    sg2_h = scale_iterize(block_scales(gs))
    su2_h = scale_iterize(block_scales(us))
    sd2_h = np.ascontiguousarray(
        np.repeat(ds.T.reshape(64, 8, 1, 2048), 16, axis=2
                  ).reshape(64, 128, 2048))

    idn_h = np.eye(128, dtype=bf)
    freqs = (1.0 / 10000.0 ** (np.arange(0, HD, 2) / HD)).astype(np.float64)
    ang = np.arange(S, dtype=np.float64)[:, None] * freqs[None, :]
    cosa = np.cos(ang).astype(np.float32)
    sina = np.sin(ang).astype(np.float32)
    cosk_h = np.ascontiguousarray(
        cosa.reshape(32, 128, 64).transpose(1, 0, 2).reshape(128, 2048))
    sink_h = np.ascontiguousarray(
        sina.reshape(32, 128, 64).transpose(1, 0, 2).reshape(128, 2048))

    maps = []
    for c in range(8):
        b_, par = c // 2, c % 2
        own = np.arange(par, 32, 2)
        xb = x[b_]
        xown = np.ascontiguousarray(xb.reshape(32, 128, D)[own].reshape(TOWN, D))
        cosq_h = np.ascontiguousarray(
            cosa.reshape(32, 128, 64)[own].transpose(1, 0, 2).reshape(128, 1024))
        sinq_h = np.ascontiguousarray(
            sina.reshape(32, 128, 64)[own].transpose(1, 0, 2).reshape(128, 1024))
        dmask_h = np.zeros((NT, 128, 1024), dtype=np.float32)
        for o in range(NT):
            g = 2 * o + par
            nch = (o + 2) // 2
            Wn = min(2, nch)
            qglob = 128 * g + np.arange(128)
            for w2 in range(Wn):
                cidx = nch - Wn + w2
                kglob = 512 * cidx + np.arange(512)
                dmask_h[o, :, 512 * w2:512 * (w2 + 1)] = np.where(
                    kglob[None, :] <= qglob[:, None], 0.0, -1e30)
        maps.append(dict(
            xseq=np.ascontiguousarray(xb), xown=xown,
            wq=wq_h, wk=wk_h, wv=wv_h, wo=wo_h,
            pg2=pg2_h, pu2=pu2_h, pd=pd_h, sg2=sg2_h, su2=su2_h, sd2=sd2_h,
            cosk=cosk_h, sink=sink_h, cosq=cosq_h, sinq=sinq_h,
            dmask=dmask_h, idn=idn_h))
    return maps


def nch_of(o):
    return (o + 2) // 2


class Sem:
    def __init__(self, h):
        self.h = h
        self.v = 0

    def base(self):
        return self.v


def build_program(taps=False, detect_races=True):
    nc = bass.Bass("TRN2", target_bir_lowering=False,
                   detect_race_conditions=detect_races)
    din = {}
    for nm, shp, dt_ in [
            ("xseq", [S, D], F32), ("xown", [TOWN, D], F32),
            ("wq", [D, D], F32), ("wk", [D, 512], F32), ("wv", [D, 512], F32),
            ("wo", [D, D], F32),
            ("pg2", [128, 128, 1024], I32), ("pu2", [128, 128, 1024], I32),
            ("pd", [NW, 128, 2048], I32),
            ("sg2", [128, 128, 1024], F32), ("su2", [128, 128, 1024], F32),
            ("sd2", [NW, 128, 2048], F32),
            ("cosk", [128, 2048], F32), ("sink", [128, 2048], F32),
            ("cosq", [128, 1024], F32), ("sinq", [128, 1024], F32),
            ("dmask", [NT, 128, 1024], F32),
            ("idn", [128, 128], BF16)]:
        din[nm] = nc.dram_tensor(nm, shp, dt_, kind="ExternalInput")
    out = nc.dram_tensor("out", [TOWN, D], F32, kind="ExternalOutput")
    tapd = {}
    if taps:
        for nm in ("t_xnt", "t_kt", "t_v", "t_q", "t_g", "t_at"):
            tapd[nm] = nc.dram_tensor(nm, [128, 16384], BF16,
                                      kind="ExternalOutput")
        for nm in ("t_int", "t_wd", "t_wg"):
            tapd[nm] = nc.dram_tensor(nm, [128, 2048], BF16,
                                      kind="ExternalOutput")
    kvsp = nc.dram_tensor("kvsp", [128, 32768], BF16, kind="Internal")
    intsp = nc.dram_tensor("intsp", [NW, 2, 128, 2048], BF16, kind="Internal")
    wdsp = nc.dram_tensor("wdsp", [NW, 2, 128, 2048], BF16, kind="Internal")

    from contextlib import ExitStack
    ctx = ExitStack()
    with ctx:
        s_idn = ctx.enter_context(nc.sbuf_tensor("s_idn", [128, 128], BF16))
        s_cosq = ctx.enter_context(nc.sbuf_tensor("s_cosq", [128, 1024], F32))
        s_sinq = ctx.enter_context(nc.sbuf_tensor("s_sinq", [128, 1024], F32))
        stat = ctx.enter_context(nc.sbuf_tensor("stat", [128, 32], F32))
        mxc = ctx.enter_context(nc.sbuf_tensor("mxc", [128, 8], F32))
        smc = ctx.enter_context(nc.sbuf_tensor("smc", [128, 8], F32))
        nfc = ctx.enter_context(nc.sbuf_tensor("nfc", [128, 8], F32))
        csum = ctx.enter_context(nc.sbuf_tensor("csum", [128, 8], F32))
        gnbt = ctx.enter_context(nc.sbuf_tensor("gnbt", [128, 4], F32))
        jnk = ctx.enter_context(nc.sbuf_tensor("jnk", [128, 8], F32))
        ajk = ctx.enter_context(nc.sbuf_tensor("ajk", [128, 8], F32))
        AR = ctx.enter_context(nc.sbuf_tensor("AR", [128, 188 * 1024], U8))

        def V(off_kb, size_kb, dt_):
            return AR[:, int(off_kb * 1024):int((off_kb + size_kb) * 1024)
                      ].bitcast(dt_)

        def sv(i):
            return stat[:, i:i + 1]

        v_kt = V(0, 32, BF16)
        v_xnt = V(0, 32, BF16)
        v_v = V(32, 32, BF16)
        v_pgb = [V(32, 8, I32), V(40, 8, I32)]
        v_pub = [V(48, 8, I32), V(56, 8, I32)]
        v_wkb = V(64, 16, BF16)
        v_wvb = V(80, 16, BF16)
        v_wge = [V(64, 4, BF16), V(68, 4, BF16)]
        v_wgo = [V(72, 4, BF16), V(76, 4, BF16)]
        v_wue = [V(80, 4, BF16), V(84, 4, BF16)]
        v_wuo = [V(88, 4, BF16), V(92, 4, BF16)]
        v_q = V(64, 32, BF16)
        v_wob = [V(64, 16, BF16), V(80, 16, BF16)]
        v_xs = [V(96, 8, F32), V(104, 8, F32)]
        v_xsq = V(112, 8, F32)
        v_xn = V(120, 4, BF16)
        v_krope = V(124, 1, BF16)
        v_rt1 = V(125, 1, F32)
        v_rt2 = V(126, 1, F32)
        v_scg = [V(96, 4, BF16), V(100, 4, BF16)]
        v_scu = [V(104, 4, BF16), V(108, 4, BF16)]
        v_sgb = [V(112, 4, BF16), V(116, 4, BF16)]
        v_sub = [V(120, 4, BF16), V(124, 4, BF16)]
        v_itb = [V(96, 4, BF16), V(100, 4, BF16)]
        v_wdb = [V(104, 2, BF16), V(106, 2, BF16)]
        v_wqb = [V(96, 16, BF16), V(112, 16, BF16)]
        v_p = [V(96, 8, BF16), V(104, 8, BF16)]
        v_ptb = V(112, 1, BF16)
        v_qt = [V(113, 0.25, BF16), V(113.25, 0.25, BF16)]
        v_mkb = [V(114, 4, F32), V(118, 4, F32)]
        v_art1 = V(160, 1, F32)
        v_art2 = V(161, 1, F32)
        v_xob = [V(96, 8, F32), V(104, 8, F32)]
        v_outb = [V(112, 8, F32), V(120, 8, F32)]
        v_g = V(128, 32, BF16)
        v_pdb = V(128, 8, I32)
        v_wde = V(136, 4, BF16)
        v_wdo = V(140, 4, BF16)
        v_scd = V(144, 4, BF16)
        v_sdb = V(148, 4, BF16)
        v_gsb = V(152, 2, BF16)
        v_upsb = V(154, 2, BF16)
        v_gt1 = V(156, 2, BF16)
        v_gt2 = V(158, 2, BF16)
        v_clo = V(160, 4, U16)
        v_chi = V(164, 4, U16)
        v_tm = V(168, 4, U16)
        v_tA = V(172, 4, U16)
        v_tB = V(176, 4, U16)
        v_cosk = V(160, 8, F32)
        v_sink = V(168, 8, F32)
        v_xntc = V(180, 4, BF16)
        v_intb = [V(180, 4, BF16), V(184, 4, BF16)]
        v_xnt2 = V(0, 64, BF16)
        v_pgb2 = [V(64, 4, I32), V(68, 4, I32)]
        v_pub2 = [V(72, 4, I32), V(76, 4, I32)]
        v_scg2 = [V(80, 2, BF16), V(82, 2, BF16)]
        v_scu2 = [V(84, 2, BF16), V(86, 2, BF16)]
        v_wge2 = [V(88, 2, BF16), V(90, 2, BF16)]
        v_wgo2 = [V(92, 2, BF16), V(94, 2, BF16)]
        v_wue2 = [V(96, 2, BF16), V(98, 2, BF16)]
        v_wuo2 = [V(100, 2, BF16), V(102, 2, BF16)]
        v_clo2 = V(104, 2, U16)
        v_chi2 = V(106, 2, U16)
        v_tm2 = V(108, 2, U16)
        v_tA2 = V(110, 2, U16)
        v_tB2 = V(112, 2, U16)
        v_gsb2 = V(114, 4, BF16)
        v_upsb2 = V(118, 4, BF16)
        v_intb2 = [V(122, 4, BF16), V(126, 4, BF16)]
        v_pdb2 = V(130, 8, I32)
        v_scd2 = V(138, 4, BF16)
        v_wde2 = V(142, 4, BF16)
        v_wdo2 = V(146, 4, BF16)

        pA = ctx.enter_context(nc.psum_tensor("pA", [128, 2048], F32))
        pB = ctx.enter_context(nc.psum_tensor("pB", [128, 2048], F32))
        pAb = pA[...].bitcast(BF16)
        pBb = pB[...].bitcast(BF16)

        SM = {}
        for nm in ("ld", "lp", "sp", "pe", "pe2", "pe3", "dv", "dv2", "dv3",
                   "ac", "ac2", "ac3", "os",
                   "xs0", "xs1", "pg0", "pg1", "pu0", "pu1", "pdl",
                   "sg0", "sg1", "su0", "su1", "sdl", "it0", "it1",
                   "wd0", "wd1", "wq0", "wq1", "mk0", "mk1",
                   "at0", "at1", "xo0", "xo1", "wo0", "wo1",
                   "spi0", "spi1", "spw", "spa", "os0", "os1", "ktv",
                   "acp", "qte", "dve"):
            SM[nm] = Sem(ctx.enter_context(nc.semaphore("sm_" + nm)))

        def dma_fence(e, bs):
            for nm in ("sp", "spi0", "spi1", "spw", "spa", "os0", "os1"):
                if bs[nm] > 0:
                    e.wait_ge(SM[nm].h, bs[nm])

        def decode_emit(v, pkap, w_lo, w_hi, scb):
            pv = pkap.bitcast(U16)[:, 0:4096:2]
            v.tensor_scalar(v_clo[...], pv, 15, None, Alu.bitwise_and)
            v.tensor_scalar(v_chi[...], pv, 4, 15, Alu.logical_shift_right,
                            Alu.bitwise_and)
            last = None
            for csrc, wdst in ((v_clo, w_lo), (v_chi, w_hi)):
                v.tensor_scalar(v_tm[...], csrc[...], 7, None, Alu.bitwise_and)
                v.tensor_scalar(v_tA[...], v_tm[...], 64, 16128, Alu.mult,
                                Alu.add)
                v.tensor_scalar(v_tB[...], v_tm[...], 4, 16128, Alu.min,
                                Alu.mult)
                v.tensor_tensor(v_tB[...], v_tA[...], v_tB[...], Alu.min)
                v.tensor_scalar(v_tm[...], csrc[...], 8, 12, Alu.bitwise_and,
                                Alu.logical_shift_left)
                v.tensor_tensor(v_tA[...], v_tB[...], v_tm[...], Alu.bitwise_or)
                last = v.tensor_tensor(wdst[...], v_tA[...].bitcast(BF16),
                                       scb[...], Alu.mult)
            return last

        def decode2(v, pkap, w_lo, w_hi, scb):
            pv = pkap.bitcast(U16)[:, 0:2048:2]
            v.tensor_scalar(v_clo2[...], pv, 15, None, Alu.bitwise_and)
            v.tensor_scalar(v_chi2[...], pv, 4, 15, Alu.logical_shift_right,
                            Alu.bitwise_and)
            last = None
            for csrc, wdst in ((v_clo2, w_lo), (v_chi2, w_hi)):
                v.tensor_scalar(v_tm2[...], csrc[...], 7, None,
                                Alu.bitwise_and)
                v.tensor_scalar(v_tA2[...], v_tm2[...], 64, 16128, Alu.mult,
                                Alu.add)
                v.tensor_scalar(v_tB2[...], v_tm2[...], 4, 16128, Alu.min,
                                Alu.mult)
                v.tensor_tensor(v_tB2[...], v_tA2[...], v_tB2[...], Alu.min)
                v.tensor_scalar(v_tm2[...], csrc[...], 8, 12, Alu.bitwise_and,
                                Alu.logical_shift_left)
                v.tensor_tensor(v_tA2[...], v_tB2[...], v_tm2[...],
                                Alu.bitwise_or)
                last = v.tensor_tensor(wdst, v_tA2[...].bitcast(BF16),
                                       scb, Alu.mult)
            return last

        def rope_emit(v, psum_ap, outb_ap, cs, sn, t1v, t2v):
            csb = cs.unsqueeze(1).broadcast_to([128, 4, 64])
            snb = sn.unsqueeze(1).broadcast_to([128, 4, 64])
            p3 = psum_ap.rearrange("p (h r) -> p h r", h=4)
            x1 = p3[:, :, 0:64]
            x2 = p3[:, :, 64:128]
            o3 = outb_ap.rearrange("p (h r) -> p h r", h=4)
            t1 = t1v[...].rearrange("p (h r) -> p h r", h=4)[:, :, 0:64]
            t2 = t2v[...].rearrange("p (h r) -> p h r", h=4)[:, :, 0:64]
            v.tensor_tensor(t1, x1, csb, Alu.mult)
            v.tensor_tensor(t2, x2, snb, Alu.mult)
            v.tensor_tensor(o3[:, :, 0:64], t1, t2, Alu.subtract)
            v.tensor_tensor(t1, x1, snb, Alu.mult)
            v.tensor_tensor(t2, x2, csb, Alu.mult)
            return v.tensor_tensor(o3[:, :, 64:128], t1, t2, Alu.add)

        def rdv(v, src_ap, w=1):
            v.tensor_copy(jnk[:, 6:7], jnk[:, 7:8])
            return v.tensor_copy(jnk[:, 0:w], src_ap)

        def rac(a, src_ap, w=1):
            a.activation(ajk[:, 6:7], ajk[:, 7:8], AF.Identity)
            return a.activation(ajk[:, 0:w], src_ap, AF.Identity)

        def vgap(v):
            v.tensor_copy(stat[:, 31:32], stat[:, 30:31])

        def ln_dv_stats(v, xsap):
            v.reduce_sum(sv(0), xsap, AXX)
            vgap(v)
            v.tensor_scalar(sv(1), sv(0), 1.0 / D, None, Alu.mult)
            vgap(v)
            v.tensor_scalar(sv(2), sv(1), -1.0, None, Alu.mult)

        def ln_dv_var(v):
            v.tensor_scalar(sv(4), sv(3), 1.0 / D, None, Alu.mult)
            v.tensor_tensor(sv(5), sv(1), sv(1), Alu.mult)
            vgap(v)
            v.scalar_tensor_tensor(sv(6), sv(4), EPS, sv(5), Alu.add,
                                   Alu.subtract)

        def ln_ac_finish(a, xsap):
            a.activation(sv(7), sv(6), AF.Ln)
            a.activation(stat[:, 29:30], stat[:, 28:29], AF.Identity)
            a.activation(sv(8), sv(7), AF.Exp, scale=-0.5)
            a.activation(v_xsq[...], xsap, AF.Identity, bias=sv(2))
            return a.activation(v_xn[...], v_xsq[...], AF.Identity,
                                scale=sv(8))

        def xnt_transposes(t, dst_bf_view):
            last = None
            for ci in range(16):
                off = 256 * (ci % 8) + (ci // 8)
                last = t.transpose(dst_bf_view[:, ci * 128:(ci + 1) * 128],
                                   v_xn[:, off:off + 256 - (ci // 8):2],
                                   s_idn[...])
            return last

        with nc.Block() as blk:
            @blk.sync
            def _(e):
                for dst, src in [(s_idn, din["idn"]),
                                 (s_cosq, din["cosq"]), (s_sinq, din["sinq"]),
                                 (v_cosk, din["cosk"]), (v_sink, din["sink"])]:
                    e.dma_start(dst[...], src[...]).then_inc(SM["ld"].h, 16)
                SM["ld"].v += 80

            @blk.gpsimd
            def _(e):
                e.dma_start(v_wkb[...].rearrange("p (hc n) -> p hc n", hc=16),
                            din["wk"][...].rearrange("(hc p) n -> p hc n",
                                                     p=128)
                            ).then_inc(SM["lp"].h, 16)
                e.dma_start(v_wvb[...].rearrange("p (hc n) -> p hc n", hc=16),
                            din["wv"][...].rearrange("(hc p) n -> p hc n",
                                                     p=128)
                            ).then_inc(SM["lp"].h, 16)
                SM["lp"].v += 32

            @blk.vector
            def _(e):
                for buf in (stat, mxc, smc, nfc, csum, gnbt, jnk, ajk):
                    e.memset(buf[...], 0.0)
                e.wait_ge(SM["ld"].h, SM["ld"].v)
                e.wait_ge(SM["lp"].h, SM["lp"].v)

        NTC = 32
        bs = {k: s.v for k, s in SM.items()}
        with nc.Block() as blk:
            @blk.sync
            def _(e):
                for tc in range(NTC):
                    if tc >= 2:
                        e.wait_ge(SM["ac"].h, bs["ac"] + (tc - 2) * 2 + 2)
                    xsn = "xs" + str(tc % 2)
                    e.dma_start(v_xs[tc % 2][...],
                                din["xseq"][tc * 128:(tc + 1) * 128, :]
                                ).then_inc(SM[xsn].h, 16)
                    SM[xsn].v += 16
                e.wait_ge(SM["dv2"].h, bs["dv2"] + NTC * 2)
                e.wait_ge(SM["ac2"].h, bs["ac2"] + NTC)
                e.dma_start(kvsp[:, 0:16384], v_kt[...]).then_inc(SM["sp"].h, 16)
                e.dma_start(kvsp[:, 16384:32768], v_v[...]).then_inc(
                    SM["sp"].h, 16)
                SM["sp"].v += 32
                if taps:
                    e.wait_ge(SM["sp"].h, SM["sp"].v)
                    e.dma_start(tapd["t_kt"][...], v_kt[...]).then_inc(
                        SM["sp"].h, 16)
                    e.dma_start(tapd["t_v"][...], v_v[...]).then_inc(
                        SM["sp"].h, 16)
                    SM["sp"].v += 32

            @blk.vector
            def _(e):
                for tc in range(NTC):
                    e.wait_ge(SM["xs" + str(tc % 2)].h,
                              bs["xs" + str(tc % 2)] + (tc // 2 + 1) * 16)
                    if tc >= 1:
                        e.wait_ge(SM["ac"].h, bs["ac"] + tc * 2)
                    ln_dv_stats(e, v_xs[tc % 2][...])
                    e.wait_ge(SM["ac"].h, bs["ac"] + tc * 2 + 1)
                    ln_dv_var(e)
                    rdv(e, stat[:, 0:8], 8).then_inc(SM["dv"].h, 1)
                    SM["dv"].v += 1
                    e.wait_ge(SM["pe2"].h, bs["pe2"] + tc * 2 + 1)
                    if tc >= 1:
                        e.wait_ge(SM["pe3"].h, bs["pe3"] + tc)
                    rope_emit(e, pB[:, 0:512],
                              v_krope[...],
                              v_cosk[:, tc * 64:(tc + 1) * 64],
                              v_sink[:, tc * 64:(tc + 1) * 64],
                              v_rt1, v_rt2).then_inc(SM["dv2"].h, 1)
                    SM["dv2"].v += 1
                    e.wait_ge(SM["pe3"].h, bs["pe3"] + tc + 1)
                    dst = v_kt[...].rearrange("p (h t) -> p h t", h=4)[
                        :, :, tc * 128:(tc + 1) * 128]
                    e.tensor_copy(dst, pBb[:, 3072:3584].rearrange(
                        "p (h t) -> p h t", h=4)).then_inc(SM["dv2"].h, 1)
                    SM["dv2"].v += 1

            @blk.scalar
            def _(e):
                for tc in range(NTC):
                    e.wait_ge(SM["xs" + str(tc % 2)].h,
                              bs["xs" + str(tc % 2)] + (tc // 2 + 1) * 16)
                    e.activation(v_xsq[...], v_xs[tc % 2][...], AF.Square,
                                 accum_out=sv(3))
                    rac(e, sv(3)).then_inc(SM["ac"].h, 1)
                    SM["ac"].v += 1
                    e.wait_ge(SM["dv"].h, bs["dv"] + tc + 1)
                    if tc >= 1:
                        e.wait_ge(SM["pe"].h, bs["pe"] + tc)
                    ln_ac_finish(e, v_xs[tc % 2][...]).then_inc(SM["ac"].h, 1)
                    SM["ac"].v += 1
                    e.wait_ge(SM["pe"].h, bs["pe"] + tc + 1)
                    e.activation(v_xntc[...], pAb[:, 0:2048],
                                 AF.Copy).then_inc(SM["ac3"].h, 1)
                    SM["ac3"].v += 1
                    e.wait_ge(SM["pe2"].h, bs["pe2"] + tc * 2 + 2)
                    e.activation(v_v[:, tc * 512:(tc + 1) * 512],
                                 pB[:, 512:1024], AF.Copy
                                 ).then_inc(SM["ac2"].h, 1)
                    SM["ac2"].v += 1

            @blk.tensor
            def _(e):
                for tc in range(NTC):
                    e.wait_ge(SM["ac"].h, bs["ac"] + tc * 2 + 2)
                    if tc >= 1:
                        e.wait_ge(SM["ac3"].h, bs["ac3"] + tc)
                    ins_ = xnt_transposes(e, pAb)
                    ins_.then_inc(SM["pe"].h, 1)
                    SM["pe"].v += 1
                    e.wait_ge(SM["ac3"].h, bs["ac3"] + tc + 1)
                    if tc >= 1:
                        e.wait_ge(SM["dv2"].h, bs["dv2"] + tc * 2)
                        e.wait_ge(SM["ac2"].h, bs["ac2"] + tc)
                    for ci in range(16):
                        ins_ = e.matmul(pB[:, 0:512],
                                        v_xntc[:, ci * 128:(ci + 1) * 128],
                                        v_wkb[:, ci * 512:(ci + 1) * 512],
                                        start=(ci == 0), stop=(ci == 15))
                    ins_.then_inc(SM["pe2"].h, 1)
                    for ci in range(16):
                        ins_ = e.matmul(pB[:, 512:1024],
                                        v_xntc[:, ci * 128:(ci + 1) * 128],
                                        v_wvb[:, ci * 512:(ci + 1) * 512],
                                        start=(ci == 0), stop=(ci == 15))
                    ins_.then_inc(SM["pe2"].h, 1)
                    SM["pe2"].v += 2
                    e.wait_ge(SM["dv2"].h, bs["dv2"] + tc * 2 + 1)
                    for h in range(4):
                        ins_ = e.transpose(
                            pBb[:, 3072 + h * 128:3072 + (h + 1) * 128],
                            v_krope[:, h * 128:(h + 1) * 128], s_idn[...])
                    ins_.then_inc(SM["pe3"].h, 1)
                    SM["pe3"].v += 1

        assert SM["dv2"].v == bs["dv2"] + 64 and SM["ac"].v == bs["ac"] + 64

        atsp = nc.dram_tensor("atsp", [2, 8, 128, 2048], BF16, kind="Internal")
        v_attstg = V(180, 4, BF16)
        v_atb = [V(160, 4, BF16), V(164, 4, BF16)]
        v_xo2 = [V(96, 2, F32), V(98, 2, F32)]
        v_ob2 = [V(100, 2, F32), V(102, 2, F32)]
        v_diag = [V(122, 1, BF16), V(123, 1, BF16)]
        v_ptb2 = [V(112, 1, BF16), V(124, 1, BF16)]

        bs = {k: s.v for k, s in SM.items()}
        with nc.Block() as blk:
            @blk.sync
            def _(e):
                dma_fence(e, bs)
                for tl in range(16):
                    if tl >= 2:
                        e.wait_ge(SM["ac"].h, bs["ac"] + (tl - 2) * 2 + 2)
                    xsn = "xs" + str(tl % 2)
                    e.dma_start(v_xs[tl % 2][...],
                                din["xown"][tl * 128:(tl + 1) * 128, :]
                                ).then_inc(SM[xsn].h, 16)
                    SM[xsn].v += 16

            @blk.vector
            def _(e):
                for tl in range(16):
                    e.wait_ge(SM["xs" + str(tl % 2)].h,
                              bs["xs" + str(tl % 2)] + (tl // 2 + 1) * 16)
                    if tl >= 1:
                        e.wait_ge(SM["ac"].h, bs["ac"] + tl * 2)
                    ln_dv_stats(e, v_xs[tl % 2][...])
                    e.wait_ge(SM["ac"].h, bs["ac"] + tl * 2 + 1)
                    ln_dv_var(e)
                    rdv(e, stat[:, 0:8], 8).then_inc(SM["dv"].h, 1)
                    SM["dv"].v += 1

            @blk.scalar
            def _(e):
                for tl in range(16):
                    e.wait_ge(SM["xs" + str(tl % 2)].h,
                              bs["xs" + str(tl % 2)] + (tl // 2 + 1) * 16)
                    e.activation(v_xsq[...], v_xs[tl % 2][...], AF.Square,
                                 accum_out=sv(3))
                    rac(e, sv(3)).then_inc(SM["ac"].h, 1)
                    SM["ac"].v += 1
                    e.wait_ge(SM["dv"].h, bs["dv"] + tl + 1)
                    if tl >= 1:
                        e.wait_ge(SM["pe"].h, bs["pe"] + tl)
                    ln_ac_finish(e, v_xs[tl % 2][...]).then_inc(SM["ac"].h, 1)
                    SM["ac"].v += 1
                    e.wait_ge(SM["pe"].h, bs["pe"] + tl + 1)
                    dst = v_xnt2[...].rearrange("p (ci t) -> p ci t", ci=16)[
                        :, :, tl * 128:(tl + 1) * 128]
                    e.activation(dst, pAb[:, 0:2048].rearrange(
                        "p (ci k) -> p ci k", ci=16), AF.Copy
                        ).then_inc(SM["ac3"].h, 1)
                    SM["ac3"].v += 1

            @blk.tensor
            def _(e):
                for tl in range(16):
                    e.wait_ge(SM["ac"].h, bs["ac"] + tl * 2 + 2)
                    if tl >= 1:
                        e.wait_ge(SM["ac3"].h, bs["ac3"] + tl)
                    xnt_transposes(e, pAb).then_inc(SM["pe"].h, 1)
                    SM["pe"].v += 1

        bs = {k: s.v for k, s in SM.items()}
        NI1 = 128
        dv3c = 0
        DEC_G, DEC_U, DEC_D = [], [], {}
        for i in range(NI1):
            dv3c += 1
            DEC_G.append(dv3c)
            dv3c += 1
            DEC_U.append(dv3c)
            if i % 2 == 1:
                dv3c += 1
                DEC_D[i] = dv3c

        with nc.Block() as blk:
            @blk.sync
            def _(e):
                dma_fence(e, bs)
                for j in (0, 1):
                    e.dma_start(v_pgb2[j][...], din["pg2"][j]
                                ).then_inc(SM["pg" + str(j)].h, 16)
                    SM["pg" + str(j)].v += 16
                    e.dma_start(v_pub2[j][...], din["pu2"][j]
                                ).then_inc(SM["pu" + str(j)].h, 16)
                    SM["pu" + str(j)].v += 16
                for i in range(NI1):
                    if i + 2 < NI1:
                        e.wait_ge(SM["dv3"].h, bs["dv3"] + DEC_G[i])
                        e.dma_start(v_pgb2[i % 2][...], din["pg2"][i + 2]
                                    ).then_inc(SM["pg" + str(i % 2)].h, 16)
                        SM["pg" + str(i % 2)].v += 16
                        e.wait_ge(SM["dv3"].h, bs["dv3"] + DEC_U[i])
                        e.dma_start(v_pub2[i % 2][...], din["pu2"][i + 2]
                                    ).then_inc(SM["pu" + str(i % 2)].h, 16)
                        SM["pu" + str(i % 2)].v += 16
                    if i >= 1:
                        e.wait_ge(SM["dv"].h, bs["dv"] + i)
                        spn = "spi" + str((i - 1) % 2)
                        e.dma_start(intsp[(i - 1) // 2, (i - 1) % 2],
                                    v_intb2[(i - 1) % 2][...]
                                    ).then_inc(SM[spn].h, 16)
                        SM[spn].v += 16
                    if i % 2 == 1:
                        e.wait_ge(SM["dv3"].h, bs["dv3"] + DEC_D[i])
                        e.dma_start(wdsp[i // 2, 0], v_wde2[...]
                                    ).then_inc(SM["spw"].h, 16)
                        e.dma_start(wdsp[i // 2, 1], v_wdo2[...]
                                    ).then_inc(SM["spw"].h, 16)
                        SM["spw"].v += 32
                e.wait_ge(SM["dv"].h, bs["dv"] + NI1)
                e.dma_start(intsp[63, 1], v_intb2[1][...]
                            ).then_inc(SM["spi1"].h, 16)
                SM["spi1"].v += 16

            @blk.gpsimd
            def _(e):
                dma_fence(e, bs)
                for j in (0, 1):
                    e.dma_start(v_scg2[j][...], din["sg2"][j]
                                ).then_inc(SM["sg" + str(j)].h, 16)
                    SM["sg" + str(j)].v += 16
                    e.dma_start(v_scu2[j][...], din["su2"][j]
                                ).then_inc(SM["su" + str(j)].h, 16)
                    SM["su" + str(j)].v += 16
                e.dma_start(v_pdb2[...], din["pd"][0]
                            ).then_inc(SM["pdl"].h, 16)
                SM["pdl"].v += 16
                e.dma_start(v_scd2[...], din["sd2"][0]
                            ).then_inc(SM["sdl"].h, 16)
                SM["sdl"].v += 16
                for i in range(NI1):
                    if i + 2 < NI1:
                        e.wait_ge(SM["dv3"].h, bs["dv3"] + DEC_G[i])
                        e.dma_start(v_scg2[i % 2][...], din["sg2"][i + 2]
                                    ).then_inc(SM["sg" + str(i % 2)].h, 16)
                        SM["sg" + str(i % 2)].v += 16
                        e.wait_ge(SM["dv3"].h, bs["dv3"] + DEC_U[i])
                        e.dma_start(v_scu2[i % 2][...], din["su2"][i + 2]
                                    ).then_inc(SM["su" + str(i % 2)].h, 16)
                        SM["su" + str(i % 2)].v += 16
                    if i % 2 == 1 and i // 2 + 1 < NW:
                        e.wait_ge(SM["dv3"].h, bs["dv3"] + DEC_D[i])
                        e.dma_start(v_pdb2[...], din["pd"][i // 2 + 1]
                                    ).then_inc(SM["pdl"].h, 16)
                        SM["pdl"].v += 16
                        e.dma_start(v_scd2[...], din["sd2"][i // 2 + 1]
                                    ).then_inc(SM["sdl"].h, 16)
                        SM["sdl"].v += 16

            @blk.tensor
            def _(e):
                for i in range(NI1):
                    wge, wgo = v_wge2[i % 2], v_wgo2[i % 2]
                    wue, wuo = v_wue2[i % 2], v_wuo2[i % 2]
                    e.wait_ge(SM["dv3"].h, bs["dv3"] + DEC_G[i])
                    if i >= 1:
                        e.wait_ge(SM["ac2"].h, bs["ac2"] + 2 * (i - 1) + 1)
                    for tp in range(4):
                        for ci in range(16):
                            wsrc = (wge if ci < 8 else wgo)
                            c8 = ci % 8
                            ins_ = e.matmul(
                                pB[:, tp * 512:(tp + 1) * 512],
                                wsrc[:, c8 * 128:(c8 + 1) * 128],
                                v_xnt2[:, ci * 2048 + tp * 512:
                                       ci * 2048 + (tp + 1) * 512],
                                start=(ci == 0), stop=(ci == 15))
                    ins_.then_inc(SM["pe"].h, 1)
                    SM["pe"].v += 1
                    e.wait_ge(SM["dv3"].h, bs["dv3"] + DEC_U[i])
                    if i >= 1:
                        e.wait_ge(SM["ac2"].h, bs["ac2"] + 2 * (i - 1) + 2)
                    for tp in range(4):
                        for ci in range(16):
                            wsrc = (wue if ci < 8 else wuo)
                            c8 = ci % 8
                            ins_ = e.matmul(
                                pA[:, tp * 512:(tp + 1) * 512],
                                wsrc[:, c8 * 128:(c8 + 1) * 128],
                                v_xnt2[:, ci * 2048 + tp * 512:
                                       ci * 2048 + (tp + 1) * 512],
                                start=(ci == 0), stop=(ci == 15))
                    ins_.then_inc(SM["pe"].h, 1)
                    SM["pe"].v += 1

            @blk.scalar
            def _(e):
                for i in range(NI1):
                    e.wait_ge(SM["pe"].h, bs["pe"] + 2 * i + 1)
                    if i >= 1:
                        e.wait_ge(SM["dv"].h, bs["dv"] + i)
                    e.activation(v_gsb2[...], pB[...], AF.Gelu_apprx_tanh
                                 ).then_inc(SM["ac2"].h, 1)
                    SM["ac2"].v += 1
                    e.wait_ge(SM["pe"].h, bs["pe"] + 2 * i + 2)
                    e.activation(v_upsb2[...], pA[...], AF.Copy
                                 ).then_inc(SM["ac2"].h, 1)
                    SM["ac2"].v += 1

            @blk.vector
            def _(e):
                for i in range(NI1):
                    p = i % 2
                    e.wait_ge(SM["pg" + str(p)].h,
                              bs["pg" + str(p)] + (i // 2 + 1) * 16)
                    e.wait_ge(SM["sg" + str(p)].h,
                              bs["sg" + str(p)] + (i // 2 + 1) * 16)
                    if i >= 2:
                        e.wait_ge(SM["pe"].h, bs["pe"] + 2 * (i - 2) + 1)
                    decode2(e, v_pgb2[p][...], v_wge2[p][...],
                            v_wgo2[p][...], v_scg2[p][...]
                            ).then_inc(SM["dv3"].h, 1)
                    SM["dv3"].v += 1
                    e.wait_ge(SM["pu" + str(p)].h,
                              bs["pu" + str(p)] + (i // 2 + 1) * 16)
                    e.wait_ge(SM["su" + str(p)].h,
                              bs["su" + str(p)] + (i // 2 + 1) * 16)
                    if i >= 2:
                        e.wait_ge(SM["pe"].h, bs["pe"] + 2 * (i - 2) + 2)
                    decode2(e, v_pub2[p][...], v_wue2[p][...],
                            v_wuo2[p][...], v_scu2[p][...]
                            ).then_inc(SM["dv3"].h, 1)
                    SM["dv3"].v += 1
                    if i % 2 == 1:
                        wd = i // 2
                        e.wait_ge(SM["pdl"].h, bs["pdl"] + (wd + 1) * 16)
                        e.wait_ge(SM["sdl"].h, bs["sdl"] + (wd + 1) * 16)
                        if wd >= 1:
                            e.wait_ge(SM["spw"].h, bs["spw"] + wd * 32)
                        for hf in range(2):
                            sl = slice(hf * 1024, (hf + 1) * 1024)
                            ins_ = decode2(e, v_pdb2[:, sl], v_wde2[:, sl],
                                           v_wdo2[:, sl], v_scd2[:, sl])
                        ins_.then_inc(SM["dv3"].h, 1)
                        SM["dv3"].v += 1
                    if i >= 1:
                        e.wait_ge(SM["ac2"].h, bs["ac2"] + 2 * (i - 1) + 2)
                        if i - 1 >= 2:
                            e.wait_ge(SM["spi" + str((i - 1) % 2)].h,
                                      bs["spi" + str((i - 1) % 2)] +
                                      ((i - 1) // 2) * 16)
                        e.tensor_tensor(v_intb2[(i - 1) % 2][...],
                                        v_gsb2[...], v_upsb2[...], Alu.mult
                                        ).then_inc(SM["dv"].h, 1)
                        SM["dv"].v += 1
                e.wait_ge(SM["ac2"].h, bs["ac2"] + 2 * (NI1 - 1) + 2)
                e.wait_ge(SM["spi1"].h, bs["spi1"] + ((NI1 - 1) // 2) * 16)
                e.tensor_tensor(v_intb2[(NI1 - 1) % 2][...],
                                v_gsb2[...], v_upsb2[...], Alu.mult
                                ).then_inc(SM["dv"].h, 1)
                SM["dv"].v += 1

        def emit_half(h):
            bs = {k: s.v for k, s in SM.items()}
            with nc.Block() as blk:
                @blk.sync
                def _(e):
                    dma_fence(e, bs)
                    for tl in range(8):
                        if tl >= 2:
                            e.wait_ge(SM["ac"].h, bs["ac"] + (tl - 2) * 2 + 2)
                        r0 = h * 1024 + tl * 128
                        xsn = "xs" + str(tl % 2)
                        e.dma_start(v_xs[tl % 2][...],
                                    din["xown"][r0:r0 + 128, :]
                                    ).then_inc(SM[xsn].h, 16)
                        SM[xsn].v += 16

                @blk.vector
                def _(e):
                    for tl in range(8):
                        e.wait_ge(SM["xs" + str(tl % 2)].h,
                                  bs["xs" + str(tl % 2)] + (tl // 2 + 1) * 16)
                        if tl >= 1:
                            e.wait_ge(SM["ac"].h, bs["ac"] + tl * 2)
                        ln_dv_stats(e, v_xs[tl % 2][...])
                        e.wait_ge(SM["ac"].h, bs["ac"] + tl * 2 + 1)
                        ln_dv_var(e)
                        rdv(e, stat[:, 0:8], 8).then_inc(SM["dv"].h, 1)
                        SM["dv"].v += 1

                @blk.scalar
                def _(e):
                    for tl in range(8):
                        e.wait_ge(SM["xs" + str(tl % 2)].h,
                                  bs["xs" + str(tl % 2)] + (tl // 2 + 1) * 16)
                        e.activation(v_xsq[...], v_xs[tl % 2][...], AF.Square,
                                     accum_out=sv(3))
                        rac(e, sv(3)).then_inc(SM["ac"].h, 1)
                        SM["ac"].v += 1
                        e.wait_ge(SM["dv"].h, bs["dv"] + tl + 1)
                        if tl >= 1:
                            e.wait_ge(SM["pe"].h, bs["pe"] + tl)
                        ln_ac_finish(e, v_xs[tl % 2][...]
                                     ).then_inc(SM["ac"].h, 1)
                        SM["ac"].v += 1
                        e.wait_ge(SM["pe"].h, bs["pe"] + tl + 1)
                        dst = v_xnt[...].rearrange("p (ci t) -> p ci t", ci=16)[
                            :, :, tl * 128:(tl + 1) * 128]
                        e.activation(dst, pAb[:, 0:2048].rearrange(
                            "p (ci k) -> p ci k", ci=16), AF.Copy
                            ).then_inc(SM["ac3"].h, 1)
                        SM["ac3"].v += 1

                @blk.tensor
                def _(e):
                    for tl in range(8):
                        e.wait_ge(SM["ac"].h, bs["ac"] + tl * 2 + 2)
                        if tl >= 1:
                            e.wait_ge(SM["ac3"].h, bs["ac3"] + tl)
                        ins_ = xnt_transposes(e, pAb)
                        ins_.then_inc(SM["pe"].h, 1)
                        SM["pe"].v += 1

                @blk.gpsimd
                def _(e):
                    if taps and h == 0:
                        e.wait_ge(SM["ac3"].h, bs["ac3"] + 8)
                        e.dma_start(tapd["t_xnt"][...], v_xnt[...]
                                    ).then_inc(SM["os"].h, 16)
                        SM["os"].v += 16

            bs = {k: s.v for k, s in SM.items()}

            def f2_it_load(e, i, cc, w):
                if i >= 2:
                    e.wait_ge(SM["pe"].h, bs["pe"] + i - 1)
                src = intsp[w, :, :, h * 1024:(h + 1) * 1024
                            ].transpose([1, 0, 2])
                e.dma_start(v_itb[i % 2][...].rearrange(
                    "p (eo t) -> p eo t", eo=2), src
                ).then_inc(SM["it" + str(i % 2)].h, 16)
                SM["it" + str(i % 2)].v += 16

            def f2_wd_load(e, i, cc, w):
                if i >= 2:
                    e.wait_ge(SM["pe"].h, bs["pe"] + i - 1)
                src = wdsp[w, :, :, cc * 512:(cc + 1) * 512
                           ].transpose([1, 0, 2])
                e.dma_start(v_wdb[i % 2][...].rearrange(
                    "p (eo c) -> p eo c", eo=2), src
                ).then_inc(SM["wd" + str(i % 2)].h, 16)
                SM["wd" + str(i % 2)].v += 16

            with nc.Block() as blk:
                @blk.sync
                def _(e):
                    dma_fence(e, bs)
                    i = 0
                    for cc in range(4):
                        for w in range(NW):
                            if i % 2 == 0 or not F2_SCALAR_DMA:
                                f2_it_load(e, i, cc, w)
                            i += 1

                @blk.gpsimd
                def _(e):
                    dma_fence(e, bs)
                    i = 0
                    for cc in range(4):
                        for w in range(NW):
                            f2_wd_load(e, i, cc, w)
                            i += 1

                @blk.tensor
                def _(e):
                    i = 0
                    for cc in range(4):
                        for w in range(NW):
                            e.wait_ge(SM["it" + str(i % 2)].h,
                                      bs["it" + str(i % 2)] +
                                      (i // 2 + 1) * 16)
                            e.wait_ge(SM["wd" + str(i % 2)].h,
                                      bs["wd" + str(i % 2)] +
                                      (i // 2 + 1) * 16)
                            if w == 0 and cc > 0:
                                e.wait_ge(SM["ac"].h, bs["ac"] + cc)
                            for eo in range(2):
                                for tt in range(8):
                                    pb = (pA if tt < 4 else pB)
                                    ins_ = e.matmul(
                                        pb[:, (tt % 4) * 512:
                                           (tt % 4 + 1) * 512],
                                        v_itb[i % 2][:, eo * 1024 + tt * 128:
                                                     eo * 1024 + (tt + 1) * 128],
                                        v_wdb[i % 2][:, eo * 512:
                                                     (eo + 1) * 512],
                                        start=(w == 0 and eo == 0),
                                        stop=(w == NW - 1 and eo == 1))
                            ins_.then_inc(SM["pe"].h, 1)
                            SM["pe"].v += 1
                            i += 1

                @blk.scalar
                def _(e):
                    dma_fence(e, bs)
                    i = 0
                    for cc in range(4):
                        for w in range(NW):
                            if i % 2 == 1 and F2_SCALAR_DMA:
                                f2_it_load(e, i, cc, w)
                            i += 1
                        e.wait_ge(SM["pe"].h, bs["pe"] + (cc + 1) * NW)
                        for tt in range(8):
                            pb = (pA if tt < 4 else pB)
                            ins_ = e.activation(
                                v_g[:, tt * 2048 + cc * 512:
                                    tt * 2048 + (cc + 1) * 512],
                                pb[:, (tt % 4) * 512:(tt % 4 + 1) * 512],
                                AF.Silu)
                        ins_.then_inc(SM["ac"].h, 1)
                        SM["ac"].v += 1

                @blk.gpsimd
                def _(e):
                    if taps and h == 0:
                        e.wait_ge(SM["ac"].h, bs["ac"] + 4)
                        e.dma_start(tapd["t_g"][...], v_g[...]
                                    ).then_inc(SM["os"].h, 16)
                        e.dma_start(tapd["t_int"][...], intsp[0, 0]
                                    ).then_inc(SM["os"].h, 16)
                        e.dma_start(tapd["t_wd"][...], wdsp[0, 0]
                                    ).then_inc(SM["os"].h, 16)
                        e.dma_start(tapd["t_wg"][...], intsp[0, 1]
                                    ).then_inc(SM["os"].h, 16)
                        SM["os"].v += 64

            bs = {k: s.v for k, s in SM.items()}
            with nc.Block() as blk:
                @blk.gpsimd
                def _(e):
                    dma_fence(e, bs)
                    for dq in range(4):
                        if dq >= 2:
                            e.wait_ge(SM["pe"].h, bs["pe"] + (dq - 1) * 8)
                        e.dma_start(v_wqb[dq % 2][...].rearrange(
                                        "p (hc n) -> p hc n", hc=16),
                                    din["wq"][:, dq * 512:(dq + 1) * 512
                                              ].rearrange(
                                        "(hc p) n -> p hc n", p=128)
                                    ).then_inc(SM["wq" + str(dq % 2)].h, 16)
                        SM["wq" + str(dq % 2)].v += 16

                @blk.tensor
                def _(e):
                    for dq in range(4):
                        e.wait_ge(SM["wq" + str(dq % 2)].h,
                                  bs["wq" + str(dq % 2)] + (dq // 2 + 1) * 16)
                        for tt in range(8):
                            idx = dq * 8 + tt
                            pslot = pA[:, (idx % 2) * 512:(idx % 2 + 1) * 512]
                            if idx >= 2:
                                e.wait_ge(SM["dv"].h, bs["dv"] + idx - 1)
                            for ci in range(16):
                                ins_ = e.matmul(
                                    pslot,
                                    v_xnt[:, ci * 1024 + tt * 128:
                                          ci * 1024 + (tt + 1) * 128],
                                    v_wqb[dq % 2][:, ci * 512:(ci + 1) * 512],
                                    start=(ci == 0), stop=(ci == 15))
                            ins_.then_inc(SM["pe"].h, 1)
                            SM["pe"].v += 1

                @blk.vector
                def _(e):
                    for dq in range(4):
                        for tt in range(8):
                            idx = dq * 8 + tt
                            e.wait_ge(SM["pe"].h, bs["pe"] + idx + 1)
                            o = 8 * h + tt
                            pslot = pA[:, (idx % 2) * 512:(idx % 2 + 1) * 512]
                            rope_emit(
                                e, pslot,
                                v_q[:, tt * 2048 + dq * 512:
                                    tt * 2048 + (dq + 1) * 512],
                                s_cosq[:, o * 64:(o + 1) * 64],
                                s_sinq[:, o * 64:(o + 1) * 64],
                                v_art1, v_art2).then_inc(SM["dv"].h, 1)
                            SM["dv"].v += 1

                @blk.gpsimd
                def _(e):
                    if taps and h == 0:
                        e.wait_ge(SM["dv"].h, bs["dv"] + 32)
                        e.dma_start(tapd["t_q"][...], v_q[...]
                                    ).then_inc(SM["os"].h, 16)
                        SM["os"].v += 16

            bs = {k: s.v for k, s in SM.items()}
            iters = [(j, hd) for j in range(8) for hd in range(NQH)]
            NI = len(iters)
            nchs = [nch_of(8 * h + j) for (j, hd) in iters]
            ngrps = [(n + 1) // 2 for n in nchs]
            nde = [(n + 1) // 2 for n in nchs]
            nae = [n // 2 for n in nchs]
            PEb = np.cumsum([0] + [1] * NI)
            PE2b = np.cumsum([0] + ngrps)
            PE3b = np.cumsum([0] + [2 * n for n in nchs])
            DVb = np.cumsum([0] + [1] * NI)
            DV2b = np.cumsum([0] + ngrps)
            DV3b = np.cumsum([0] + [1] * NI)
            ACb = np.cumsum([0] + ngrps)
            AC2b = np.cumsum([0] + [1] * NI)
            AC3b = np.cumsum([0] + [1] * NI)
            ACPb = np.cumsum([0] + nae)
            DVEb = np.cumsum([0] + nde)
            stg = [pB[:, 0:512], pB[:, 512:1024]]

            with nc.Block() as blk:
                @blk.sync
                def _(e):
                    dma_fence(e, bs)
                    e.dma_start(v_kt[...], kvsp[:, 0:16384]
                                ).then_inc(SM["ktv"].h, 16)
                    e.dma_start(v_v[...], kvsp[:, 16384:32768]
                                ).then_inc(SM["ktv"].h, 16)
                    SM["ktv"].v += 32
                    for j in (0, 1):
                        e.dma_start(v_mkb[j % 2][...], din["dmask"][8 * h + j]
                                    ).then_inc(SM["mk" + str(j % 2)].h, 16)
                        SM["mk" + str(j % 2)].v += 16
                    for j in range(8):
                        e.wait_ge(SM["ac3"].h,
                                  bs["ac3"] + int(AC3b[j * NQH + NQH]))
                        e.dma_start(atsp[h, j], v_attstg[...]
                                    ).then_inc(SM["spa"].h, 16)
                        SM["spa"].v += 16
                        if j + 2 < 8:
                            e.wait_ge(SM["dv3"].h,
                                      bs["dv3"] + int(DV3b[(j + 1) * NQH]))
                            e.dma_start(v_mkb[j % 2][...],
                                        din["dmask"][8 * h + j + 2]
                                        ).then_inc(SM["mk" + str(j % 2)].h, 16)
                            SM["mk" + str(j % 2)].v += 16
                    if taps and h == 0:
                        e.wait_ge(SM["spa"].h, bs["spa"] + 128)
                        e.dma_start(
                            tapd["t_at"][...].rearrange("p (j n) -> (p j) n",
                                                        j=8),
                            atsp[0]
                        ).then_inc(SM["os"].h, 16)
                        SM["os"].v += 16

                @blk.tensor
                def _(e):
                    def evac_wait(o, c):
                        if c % 2 == 0:
                            e.wait_ge(SM["dve"].h, bs["dve"] + int(DVEb[o]) +
                                      c // 2 + 1)
                        else:
                            e.wait_ge(SM["acp"].h,
                                      bs["acp"] + int(ACPb[o]) + (c + 1) // 2)

                    def pt_emit(o, c):
                        g = c // 2
                        stgb = stg[c % 2]
                        for kt in range(4):
                            ins2 = e.matmul(
                                stgb[:, kt * 128:(kt + 1) * 128],
                                v_p[o % 2][:, c * 512 + kt * 128:
                                           c * 512 + (kt + 1) * 128],
                                v_diag[o % 2][:, g * 128:(g + 1) * 128],
                                start=True, stop=True)
                        return ins2

                    e.wait_ge(SM["ktv"].h, bs["ktv"] + 32)
                    for it in range(NI + 1):
                        if it < NI:
                            j, hd = iters[it]
                            nch, ngrp = nchs[it], ngrps[it]
                            kv = hd // 4
                            if it >= 2:
                                e.wait_ge(SM["qte"].h, bs["qte"] + it - 1)
                            qs = 3072 + 256 * (it % 2)
                            ins_ = e.transpose(
                                pBb[:, qs:qs + 128],
                                v_q[:, j * 2048 + hd * 128:
                                    j * 2048 + (hd + 1) * 128],
                                s_idn[...])
                            ins_.then_inc(SM["pe"].h, 1)
                            SM["pe"].v += 1
                            e.wait_ge(SM["qte"].h, bs["qte"] + it + 1)
                            for gi in range(ngrp):
                                half = gi % 2
                                if gi >= 2:
                                    e.wait_ge(SM["ac"].h,
                                              bs["ac"] + int(ACb[it]) + gi - 1)
                                elif it >= 1:
                                    e.wait_ge(SM["ac"].h,
                                              bs["ac"] + int(ACb[it]))
                                for c in range(gi * 2, min(nch, gi * 2 + 2)):
                                    ins_ = e.matmul(
                                        pA[:, half * 1024 + (c - gi * 2) * 512:
                                           half * 1024 +
                                           (c - gi * 2 + 1) * 512],
                                        v_qt[it % 2][...],
                                        v_kt[:, kv * 4096 + c * 512:
                                             kv * 4096 + (c + 1) * 512],
                                        start=True, stop=True)
                                ins_.then_inc(SM["pe2"].h, 1)
                                SM["pe2"].v += 1
                        if it >= 1:
                            o = it - 1
                            j, hd = iters[o]
                            nch, ngrp = nchs[o], ngrps[o]
                            kv = hd // 4
                            e.wait_ge(SM["dv3"].h, bs["dv3"] + int(DV3b[o]) + 1)
                            e.wait_ge(SM["ac3"].h, bs["ac3"] + int(AC3b[o]))
                            if o >= 1:
                                e.wait_ge(SM["dve"].h, bs["dve"] + int(DVEb[o]))
                                e.wait_ge(SM["acp"].h,
                                          bs["acp"] + int(ACPb[o]))
                            pt_emit(o, 0).then_inc(SM["pe3"].h, 1)
                            SM["pe3"].v += 1
                            for c in range(nch):
                                if c + 1 < nch:
                                    if c >= 1:
                                        evac_wait(o, c - 1)
                                    pt_emit(o, c + 1).then_inc(SM["pe3"].h, 1)
                                    SM["pe3"].v += 1
                                evac_wait(o, c)
                                for kt in range(4):
                                    kg = c * 4 + kt
                                    ins_ = e.matmul(
                                        pB[:, 1024:1152],
                                        v_v[:, kg * 512 + kv * 128:
                                            kg * 512 + (kv + 1) * 128],
                                        v_ptb2[c % 2][:, kt * 128:
                                                      (kt + 1) * 128],
                                        start=(c == 0 and kt == 0),
                                        stop=(c == nch - 1 and kt == 3))
                                ins_.then_inc(SM["pe3"].h, 1)
                                SM["pe3"].v += 1

                @blk.vector
                def _(e):
                    def ev_even(o, c):
                        e.wait_ge(SM["pe3"].h, bs["pe3"] + int(PE3b[o]) +
                                  (1 if c == 0 else 2 * c))
                        e.tensor_copy(v_ptb2[0][...], stg[0][...]
                                      ).then_inc(SM["dve"].h, 1)
                        SM["dve"].v += 1

                    for it in range(NI + 1):
                        o = it - 1
                        if it == NI:
                            for c in range(0, nchs[o], 2):
                                ev_even(o, c)
                            break
                        j, hd = iters[it]
                        nch, ngrp = nchs[it], ngrps[it]
                        e.wait_ge(SM["mk" + str(j % 2)].h,
                                  bs["mk" + str(j % 2)] + (j // 2 + 1) * 16)
                        wn = min(2, nch)
                        for gi in range(ngrp):
                            half = gi % 2
                            e.wait_ge(SM["pe2"].h,
                                      bs["pe2"] + int(PE2b[it]) + gi + 1)
                            lim = min(nch, gi * 2 + 2)
                            for c in range(gi * 2, lim):
                                if c >= nch - wn:
                                    ps = pA[:, half * 1024 +
                                            (c - gi * 2) * 512:
                                            half * 1024 +
                                            (c - gi * 2 + 1) * 512]
                                    w2 = c - (nch - wn)
                                    e.scalar_tensor_tensor(
                                        ps, ps, 1.0,
                                        v_mkb[j % 2][:, w2 * 512:
                                                     (w2 + 1) * 512],
                                        Alu.mult, Alu.add)
                            width = lim - gi * 2
                            e.reduce_max(smc[:, gi:gi + 1],
                                         pA[:, half * 1024:
                                            half * 1024 + width * 512], AXX)
                            vgap(e)
                            e.tensor_scalar(gnbt[:, gi:gi + 1],
                                            smc[:, gi:gi + 1], -ISQ, None,
                                            Alu.mult)
                            rdv(e, gnbt[:, gi:gi + 1]
                                ).then_inc(SM["dv2"].h, 1)
                            SM["dv2"].v += 1
                        e.reduce_max(sv(9), smc[:, 0:ngrp], AXX)
                        vgap(e)
                        e.tensor_scalar(sv(10), sv(9), -ISQ, None, Alu.mult)
                        rdv(e, sv(10)).then_inc(SM["dv"].h, 1)
                        SM["dv"].v += 1
                        if o >= 0:
                            ev_even(o, 0)
                        e.wait_ge(SM["ac2"].h, bs["ac2"] + int(AC2b[it]) + 1)
                        e.tensor_tensor(smc[:, 4:4 + ngrp], csum[:, 0:ngrp],
                                        nfc[:, 0:ngrp], Alu.mult)
                        vgap(e)
                        e.reduce_sum(sv(11), smc[:, 4:4 + ngrp], AXX)
                        vgap(e)
                        e.reciprocal(sv(12), sv(11))
                        vgap(e)
                        e.tensor_scalar(nfc[:, 0:ngrp], nfc[:, 0:ngrp],
                                        sv(12), None, Alu.mult)
                        vgap(e)
                        for gi in range(ngrp):
                            ins_ = e.tensor_scalar(
                                v_diag[it % 2][:, gi * 128:(gi + 1) * 128],
                                s_idn[...], nfc[:, gi:gi + 1], None, Alu.mult)
                        ins_.then_inc(SM["dv3"].h, 1)
                        SM["dv3"].v += 1
                        if o >= 0:
                            for c in range(2, nchs[o], 2):
                                ev_even(o, c)

                @blk.scalar
                def _(e):
                    for it in range(NI + 1):
                        if it < NI:
                            j, hd = iters[it]
                            nch, ngrp = nchs[it], ngrps[it]
                            e.wait_ge(SM["pe"].h, bs["pe"] + int(PEb[it]) + 1)
                            if it >= 2:
                                e.wait_ge(SM["pe2"].h,
                                          bs["pe2"] + int(PE2b[it - 1]))
                            qs = 3072 + 256 * (it % 2)
                            e.activation(v_qt[it % 2][...], pBb[:, qs:qs + 128],
                                         AF.Copy).then_inc(SM["qte"].h, 1)
                            SM["qte"].v += 1
                            for gi in range(ngrp):
                                half = gi % 2
                                lim = min(nch, gi * 2 + 2)
                                width = lim - gi * 2
                                e.wait_ge(SM["dv2"].h,
                                          bs["dv2"] + int(DV2b[it]) + gi + 1)
                                if it >= 2 and gi == 0:
                                    e.wait_ge(SM["pe3"].h,
                                              bs["pe3"] + int(PE3b[it - 1]))
                                e.activation(
                                    v_p[it % 2][:, gi * 1024:
                                                gi * 1024 + width * 512],
                                    pA[:, half * 1024:
                                       half * 1024 + width * 512],
                                    AF.Exp, scale=ISQ,
                                    bias=gnbt[:, gi:gi + 1],
                                    accum_out=csum[:, gi:gi + 1])
                                rac(e, csum[:, gi:gi + 1], 1
                                    ).then_inc(SM["ac"].h, 1)
                                SM["ac"].v += 1
                            e.wait_ge(SM["dv"].h, bs["dv"] + int(DVb[it]) + 1)
                            e.activation(nfc[:, 0:ngrp], smc[:, 0:ngrp],
                                         AF.Exp, scale=ISQ, bias=sv(10))
                            rac(e, nfc[:, 0:ngrp], ngrp
                                ).then_inc(SM["ac2"].h, 1)
                            SM["ac2"].v += 1
                        if it >= 1:
                            o = it - 1
                            j, hd = iters[o]
                            nch = nchs[o]
                            for c in range(1, nch, 2):
                                e.wait_ge(SM["pe3"].h,
                                          bs["pe3"] + int(PE3b[o]) + 2 * c)
                                e.activation(v_ptb2[1][...], stg[1][...],
                                             AF.Copy).then_inc(SM["acp"].h, 1)
                                SM["acp"].v += 1
                            e.wait_ge(SM["pe3"].h,
                                      bs["pe3"] + int(PE3b[o]) + 2 * nch)
                            if hd == 0 and j >= 1:
                                e.wait_ge(SM["spa"].h, bs["spa"] + 16 * j)
                            e.activation(v_attstg[:, hd * 128:(hd + 1) * 128],
                                         pB[:, 1024:1152], AF.Copy
                                         ).then_inc(SM["ac3"].h, 1)
                            SM["ac3"].v += 1

            bs = {k: s.v for k, s in SM.items()}
            with nc.Block() as blk:
                @blk.gpsimd
                def _(e):
                    def wob_load(cc):
                        e.dma_start(v_wob[cc % 2][...].rearrange(
                                        "p (hc n) -> p hc n", hc=16),
                                    din["wo"][:, cc * 512:(cc + 1) * 512
                                              ].rearrange(
                                        "(hc p) n -> p hc n", p=128)
                                    ).then_inc(SM["wo" + str(cc % 2)].h, 16)
                        SM["wo" + str(cc % 2)].v += 16
                    dma_fence(e, bs)
                    wob_load(0)
                    wob_load(1)
                    i = 0
                    for cc in range(4):
                        for j in range(8):
                            e.wait_ge(SM["dv"].h, bs["dv"] + i + 1)
                            r0 = h * 1024 + j * 128
                            e.dma_start(
                                out[r0:r0 + 128, cc * 512:(cc + 1) * 512],
                                v_ob2[i % 2][...]
                            ).then_inc(SM["os" + str(i % 2)].h, 16)
                            SM["os" + str(i % 2)].v += 16
                            i += 1
                        if cc + 2 < 4:
                            e.wait_ge(SM["pe"].h, bs["pe"] + (cc + 1) * 8)
                            wob_load(cc + 2)

                @blk.sync
                def _(e):
                    dma_fence(e, bs)
                    i = 0
                    for cc in range(4):
                        for j in range(8):
                            if i >= 2:
                                e.wait_ge(SM["pe"].h, bs["pe"] + i - 1)
                            e.dma_start(v_atb[i % 2][...], atsp[h, j]
                                        ).then_inc(SM["at" + str(i % 2)].h, 16)
                            SM["at" + str(i % 2)].v += 16
                            if i >= 2:
                                e.wait_ge(SM["dv"].h, bs["dv"] + i - 1)
                            r0 = h * 1024 + j * 128
                            e.dma_start(v_xo2[i % 2][...],
                                        din["xown"][r0:r0 + 128,
                                                    cc * 512:(cc + 1) * 512]
                                        ).then_inc(SM["xo" + str(i % 2)].h, 16)
                            SM["xo" + str(i % 2)].v += 16
                            i += 1

                @blk.tensor
                def _(e):
                    i = 0
                    for cc in range(4):
                        e.wait_ge(SM["wo" + str(cc % 2)].h,
                                  bs["wo" + str(cc % 2)] + (cc // 2 + 1) * 16)
                        for j in range(8):
                            pslot = pA[:, (i % 4) * 512:(i % 4 + 1) * 512]
                            e.wait_ge(SM["at" + str(i % 2)].h,
                                      bs["at" + str(i % 2)] +
                                      (i // 2 + 1) * 16)
                            if i >= 4:
                                e.wait_ge(SM["dv"].h, bs["dv"] + i - 3)
                            for ci in range(16):
                                ins_ = e.matmul(
                                    pslot,
                                    v_atb[i % 2][:, ci * 128:(ci + 1) * 128],
                                    v_wob[cc % 2][:, ci * 512:(ci + 1) * 512],
                                    start=(ci == 0), stop=(ci == 15))
                            ins_.then_inc(SM["pe"].h, 1)
                            SM["pe"].v += 1
                            i += 1

                @blk.vector
                def _(e):
                    i = 0
                    for cc in range(4):
                        for j in range(8):
                            e.wait_ge(SM["pe"].h, bs["pe"] + i + 1)
                            e.wait_ge(SM["xo" + str(i % 2)].h,
                                      bs["xo" + str(i % 2)] +
                                      (i // 2 + 1) * 16)
                            if i >= 2:
                                e.wait_ge(SM["os" + str(i % 2)].h,
                                          bs["os" + str(i % 2)] +
                                          (i // 2) * 16)
                            pslot = pA[:, (i % 4) * 512:(i % 4 + 1) * 512]
                            e.tensor_tensor(
                                v_ob2[i % 2][...], pslot,
                                v_g[:, j * 2048 + cc * 512:
                                    j * 2048 + (cc + 1) * 512], Alu.mult)
                            e.tensor_tensor(v_ob2[i % 2][...], v_ob2[i % 2][...],
                                            v_xo2[i % 2][...], Alu.add
                                            ).then_inc(SM["dv"].h, 1)
                            SM["dv"].v += 1
                            i += 1


        emit_half(0)
        emit_half(1)

        with nc.Block() as blk:
            @blk.gpsimd
            def _(e):
                e.wait_ge(SM["os0"].h, SM["os0"].v)
                e.wait_ge(SM["os1"].h, SM["os1"].v)
                e.wait_ge(SM["os"].h, SM["os"].v)
    return nc


def kernel(**inputs):
    import os as _os
    taps = bool(_os.environ.get("MK_TAPS"))
    maps = _host_stage(inputs)
    nc = build_program(taps=taps)
    res = run_bass_kernel_spmd(nc, maps, core_ids=list(range(8)))
    full = np.empty((NB, S, D), np.float32)
    for c in range(8):
        par = c % 2
        full[c // 2].reshape(32, 128, D)[par::2] = \
            np.asarray(res.results[c]["out"]).reshape(NT, 128, D)
    if taps:
        kernel.last_taps = res.results
    return full


# version 06
# speedup vs baseline: 1.7218x; 1.1758x over previous
import numpy as np
import ml_dtypes
import concourse.bass as bass
import concourse.mybir as mybir
from concourse.bass_utils import run_bass_kernel_spmd

F32 = mybir.dt.float32
BF16 = mybir.dt.bfloat16
I32 = mybir.dt.int32
U16 = mybir.dt.uint16
U8 = mybir.dt.uint8
AF = mybir.ActivationFunctionType
Alu = mybir.AluOpType
AXX = mybir.AxisListType.X
bf = ml_dtypes.bfloat16

D, S, NB = 2048, 4096, 4
NQH, NKV, HD = 16, 4, 128
MLP, NW, TOWN, NT = 16384, 64, 2048, 16
ISQ = 1.0 / float(np.sqrt(HD))
GA = 0.7978845608028654
GB = GA * 0.044715
EPS = 1e-5
_PERM = np.concatenate([np.arange(0, D, 2), np.arange(1, D, 2)])
F2_SCALAR_DMA = True


def _host_stage(inputs):
    x = np.asarray(inputs["L0_attn_x"], dtype=np.float32)
    Wq = np.asarray(inputs["L0_attn_Wq"], dtype=np.float32)
    Wk = np.asarray(inputs["L0_attn_Wk"], dtype=np.float32)
    Wv = np.asarray(inputs["L0_attn_Wv"], dtype=np.float32)
    Wo = np.asarray(inputs["L0_attn_Wo"], dtype=np.float32)
    gp = np.asarray(inputs["L0_ffn_gate_W_packed"], dtype=np.int32)
    gs = np.asarray(inputs["L0_ffn_gate_scales"], dtype=np.float32)
    up = np.asarray(inputs["L0_ffn_up_W_packed"], dtype=np.int32)
    us = np.asarray(inputs["L0_ffn_up_scales"], dtype=np.float32)
    dp = np.asarray(inputs["L0_ffn_down_W_packed"], dtype=np.int32)
    ds = np.asarray(inputs["L0_ffn_down_scales"], dtype=np.float32)

    dperm = np.concatenate([np.arange(0, HD, 2), np.arange(1, HD, 2)])
    qcol = (np.arange(NQH)[:, None] * HD + dperm[None, :]).reshape(-1)
    kcol = (np.arange(NKV)[:, None] * HD + dperm[None, :]).reshape(-1)
    wq_h = np.ascontiguousarray(Wq.T[_PERM][:, qcol])
    wk_h = np.ascontiguousarray(Wk.T[_PERM][:, kcol])
    wv_h = np.ascontiguousarray(Wv.T[_PERM, :])
    wo_h = np.ascontiguousarray(Wo.T)

    jperm = np.concatenate([np.arange(0, 256, 2), np.arange(1, 256, 2)])

    def block_packed(P):
        PT = P.T
        kp, n = PT.shape
        blk = PT.reshape(8, kp // 8, n // 256, 256).transpose(2, 1, 0, 3)
        blk = blk[:, :, :, jperm]
        return np.ascontiguousarray(blk.reshape(n // 256, kp // 8, 2048))

    def iterize(Ph):
        t = Ph.reshape(64, 128, 8, 2, 128).transpose(0, 3, 1, 2, 4)
        return np.ascontiguousarray(t.reshape(128, 128, 1024))

    pg2_h = iterize(block_packed(gp))
    pu2_h = iterize(block_packed(up))
    pd_h = np.ascontiguousarray(dp.T.reshape(64, 128, 2048))

    def block_scales(Sc):
        blk = Sc.T.reshape(8, 8, 64, 256).transpose(2, 1, 0, 3)
        blk = blk[:, :, :, jperm]
        return np.ascontiguousarray(blk.reshape(64, 8, 2048))

    def scale_iterize(Sh):
        t = Sh.reshape(64, 8, 8, 2, 128).transpose(0, 3, 1, 2, 4)
        t = np.repeat(t, 16, axis=2)
        return np.ascontiguousarray(t.reshape(128, 128, 1024))

    sg2_h = scale_iterize(block_scales(gs))
    su2_h = scale_iterize(block_scales(us))
    sd2_h = np.ascontiguousarray(
        np.repeat(ds.T.reshape(64, 8, 1, 2048), 16, axis=2
                  ).reshape(64, 128, 2048))

    idn_h = np.eye(128, dtype=bf)
    freqs = (1.0 / 10000.0 ** (np.arange(0, HD, 2) / HD)).astype(np.float64)
    ang = np.arange(S, dtype=np.float64)[:, None] * freqs[None, :]
    cosa = np.cos(ang).astype(np.float32)
    sina = np.sin(ang).astype(np.float32)
    cosk_h = np.ascontiguousarray(
        cosa.reshape(32, 128, 64).transpose(1, 0, 2).reshape(128, 2048))
    sink_h = np.ascontiguousarray(
        sina.reshape(32, 128, 64).transpose(1, 0, 2).reshape(128, 2048))

    maps = []
    for c in range(8):
        b_, par = c // 2, c % 2
        own = np.arange(par, 32, 2)
        xb = x[b_]
        xown = np.ascontiguousarray(xb.reshape(32, 128, D)[own].reshape(TOWN, D))
        cosq_h = np.ascontiguousarray(
            cosa.reshape(32, 128, 64)[own].transpose(1, 0, 2).reshape(128, 1024))
        sinq_h = np.ascontiguousarray(
            sina.reshape(32, 128, 64)[own].transpose(1, 0, 2).reshape(128, 1024))
        dmask_h = np.zeros((NT, 128, 1024), dtype=np.float32)
        for o in range(NT):
            g = 2 * o + par
            nch = (o + 2) // 2
            Wn = min(2, nch)
            qglob = 128 * g + np.arange(128)
            for w2 in range(Wn):
                cidx = nch - Wn + w2
                kglob = 512 * cidx + np.arange(512)
                dmask_h[o, :, 512 * w2:512 * (w2 + 1)] = np.where(
                    kglob[None, :] <= qglob[:, None], 0.0, -1e30)
        maps.append(dict(
            xseq=np.ascontiguousarray(xb), xown=xown,
            wq=wq_h, wk=wk_h, wv=wv_h, wo=wo_h,
            pg2=pg2_h, pu2=pu2_h, pd=pd_h, sg2=sg2_h, su2=su2_h, sd2=sd2_h,
            cosk=cosk_h, sink=sink_h, cosq=cosq_h, sinq=sinq_h,
            dmask=dmask_h, idn=idn_h))
    return maps


def nch_of(o):
    return (o + 2) // 2


class Sem:
    def __init__(self, h):
        self.h = h
        self.v = 0

    def base(self):
        return self.v


def build_program(taps=False, detect_races=True):
    nc = bass.Bass("TRN2", target_bir_lowering=False,
                   detect_race_conditions=detect_races)
    din = {}
    for nm, shp, dt_ in [
            ("xseq", [S, D], F32), ("xown", [TOWN, D], F32),
            ("wq", [D, D], F32), ("wk", [D, 512], F32), ("wv", [D, 512], F32),
            ("wo", [D, D], F32),
            ("pg2", [128, 128, 1024], I32), ("pu2", [128, 128, 1024], I32),
            ("pd", [NW, 128, 2048], I32),
            ("sg2", [128, 128, 1024], F32), ("su2", [128, 128, 1024], F32),
            ("sd2", [NW, 128, 2048], F32),
            ("cosk", [128, 2048], F32), ("sink", [128, 2048], F32),
            ("cosq", [128, 1024], F32), ("sinq", [128, 1024], F32),
            ("dmask", [NT, 128, 1024], F32),
            ("idn", [128, 128], BF16)]:
        din[nm] = nc.dram_tensor(nm, shp, dt_, kind="ExternalInput")
    out = nc.dram_tensor("out", [TOWN, D], F32, kind="ExternalOutput")
    tapd = {}
    if taps:
        for nm in ("t_xnt", "t_kt", "t_v", "t_q", "t_g", "t_at"):
            tapd[nm] = nc.dram_tensor(nm, [128, 16384], BF16,
                                      kind="ExternalOutput")
        for nm in ("t_int", "t_wd", "t_wg"):
            tapd[nm] = nc.dram_tensor(nm, [128, 2048], BF16,
                                      kind="ExternalOutput")
    kvsp = nc.dram_tensor("kvsp", [128, 32768], BF16, kind="Internal")
    intsp = nc.dram_tensor("intsp", [NW, 2, 128, 2048], BF16, kind="Internal")
    wdsp = nc.dram_tensor("wdsp", [NW, 2, 128, 2048], BF16, kind="Internal")

    from contextlib import ExitStack
    ctx = ExitStack()
    with ctx:
        s_idn = ctx.enter_context(nc.sbuf_tensor("s_idn", [128, 128], BF16))
        s_cosq = ctx.enter_context(nc.sbuf_tensor("s_cosq", [128, 1024], F32))
        s_sinq = ctx.enter_context(nc.sbuf_tensor("s_sinq", [128, 1024], F32))
        stat = ctx.enter_context(nc.sbuf_tensor("stat", [128, 32], F32))
        mxc = ctx.enter_context(nc.sbuf_tensor("mxc", [128, 8], F32))
        smc = ctx.enter_context(nc.sbuf_tensor("smc", [128, 8], F32))
        nfc = ctx.enter_context(nc.sbuf_tensor("nfc", [128, 8], F32))
        csum = ctx.enter_context(nc.sbuf_tensor("csum", [128, 8], F32))
        gnbt = ctx.enter_context(nc.sbuf_tensor("gnbt", [128, 4], F32))
        jnk = ctx.enter_context(nc.sbuf_tensor("jnk", [128, 8], F32))
        ajk = ctx.enter_context(nc.sbuf_tensor("ajk", [128, 8], F32))
        AR = ctx.enter_context(nc.sbuf_tensor("AR", [128, 188 * 1024], U8))

        def V(off_kb, size_kb, dt_):
            return AR[:, int(off_kb * 1024):int((off_kb + size_kb) * 1024)
                      ].bitcast(dt_)

        def sv(i):
            return stat[:, i:i + 1]

        v_kt = V(0, 32, BF16)
        v_xnt = V(0, 32, BF16)
        v_v = V(32, 32, BF16)
        v_pgb = [V(32, 8, I32), V(40, 8, I32)]
        v_pub = [V(48, 8, I32), V(56, 8, I32)]
        v_wkb = V(64, 16, BF16)
        v_wvb = V(80, 16, BF16)
        v_wge = [V(64, 4, BF16), V(68, 4, BF16)]
        v_wgo = [V(72, 4, BF16), V(76, 4, BF16)]
        v_wue = [V(80, 4, BF16), V(84, 4, BF16)]
        v_wuo = [V(88, 4, BF16), V(92, 4, BF16)]
        v_q = V(64, 32, BF16)
        v_wob = [V(64, 16, BF16), V(80, 16, BF16)]
        v_xs = [V(96, 8, F32), V(104, 8, F32)]
        v_xsq = V(112, 8, F32)
        v_xn = V(120, 4, BF16)
        v_krope = V(124, 1, BF16)
        v_rt1 = V(125, 1, F32)
        v_rt2 = V(126, 1, F32)
        v_scg = [V(96, 4, BF16), V(100, 4, BF16)]
        v_scu = [V(104, 4, BF16), V(108, 4, BF16)]
        v_sgb = [V(112, 4, BF16), V(116, 4, BF16)]
        v_sub = [V(120, 4, BF16), V(124, 4, BF16)]
        v_itb = [V(96, 4, BF16), V(100, 4, BF16)]
        v_wdb = [V(104, 2, BF16), V(106, 2, BF16)]
        v_wqb = [V(96, 16, BF16), V(112, 16, BF16)]
        v_p = [V(96, 8, BF16), V(104, 8, BF16)]
        v_ptb = V(112, 1, BF16)
        v_qt = [V(113, 0.25, BF16), V(113.25, 0.25, BF16)]
        v_mkb = [V(114, 4, F32), V(118, 4, F32)]
        v_art1 = V(160, 1, F32)
        v_art2 = V(161, 1, F32)
        v_xob = [V(96, 8, F32), V(104, 8, F32)]
        v_outb = [V(112, 8, F32), V(120, 8, F32)]
        v_g = V(128, 32, BF16)
        v_pdb = V(128, 8, I32)
        v_wde = V(136, 4, BF16)
        v_wdo = V(140, 4, BF16)
        v_scd = V(144, 4, BF16)
        v_sdb = V(148, 4, BF16)
        v_gsb = V(152, 2, BF16)
        v_upsb = V(154, 2, BF16)
        v_gt1 = V(156, 2, BF16)
        v_gt2 = V(158, 2, BF16)
        v_clo = V(160, 4, U16)
        v_chi = V(164, 4, U16)
        v_tm = V(168, 4, U16)
        v_tA = V(172, 4, U16)
        v_tB = V(176, 4, U16)
        v_cosk = V(160, 8, F32)
        v_sink = V(168, 8, F32)
        v_xntc = V(180, 4, BF16)
        v_intb = [V(180, 4, BF16), V(184, 4, BF16)]
        v_xnt2 = V(0, 64, BF16)
        v_pgb2 = [V(64, 4, I32), V(68, 4, I32)]
        v_pub2 = [V(72, 4, I32), V(76, 4, I32)]
        v_scg2 = [V(80, 2, BF16), V(82, 2, BF16)]
        v_scu2 = [V(84, 2, BF16), V(86, 2, BF16)]
        v_wge2 = [V(88, 2, BF16), V(90, 2, BF16)]
        v_wgo2 = [V(92, 2, BF16), V(94, 2, BF16)]
        v_wue2 = [V(96, 2, BF16), V(98, 2, BF16)]
        v_wuo2 = [V(100, 2, BF16), V(102, 2, BF16)]
        v_clo2 = V(104, 2, U16)
        v_chi2 = V(106, 2, U16)
        v_tm2 = V(108, 2, U16)
        v_tA2 = V(110, 2, U16)
        v_tB2 = V(112, 2, U16)
        v_gsb2 = V(114, 4, BF16)
        v_upsb2 = V(118, 4, BF16)
        v_intb2 = [V(122, 4, BF16), V(126, 4, BF16)]
        v_pdb2 = V(130, 8, I32)
        v_scd2 = V(138, 4, BF16)
        v_wde2 = V(142, 4, BF16)
        v_wdo2 = V(146, 4, BF16)

        pA = ctx.enter_context(nc.psum_tensor("pA", [128, 2048], F32))
        pB = ctx.enter_context(nc.psum_tensor("pB", [128, 2048], F32))
        pAb = pA[...].bitcast(BF16)
        pBb = pB[...].bitcast(BF16)

        SM = {}
        for nm in ("ld", "lp", "sp", "pe", "pe2", "pe3", "dv", "dv2", "dv3",
                   "ac", "ac2", "ac3", "os",
                   "xs0", "xs1", "pg0", "pg1", "pu0", "pu1", "pdl",
                   "sg0", "sg1", "su0", "su1", "sdl", "it0", "it1",
                   "wd0", "wd1", "wq0", "wq1", "mk0", "mk1",
                   "at0", "at1", "xo0", "xo1", "wo0", "wo1",
                   "spi0", "spi1", "spw", "spa", "os0", "os1", "ktv",
                   "acp", "qte", "dve"):
            SM[nm] = Sem(ctx.enter_context(nc.semaphore("sm_" + nm)))

        def dma_fence(e, bs):
            for nm in ("sp", "spi0", "spi1", "spw", "spa", "os0", "os1"):
                if bs[nm] > 0:
                    e.wait_ge(SM[nm].h, bs[nm])

        def decode_emit(v, pkap, w_lo, w_hi, scb):
            pv = pkap.bitcast(U16)[:, 0:4096:2]
            v.tensor_scalar(v_clo[...], pv, 15, None, Alu.bitwise_and)
            v.tensor_scalar(v_chi[...], pv, 4, 15, Alu.logical_shift_right,
                            Alu.bitwise_and)
            last = None
            for csrc, wdst in ((v_clo, w_lo), (v_chi, w_hi)):
                v.tensor_scalar(v_tm[...], csrc[...], 7, None, Alu.bitwise_and)
                v.tensor_scalar(v_tA[...], v_tm[...], 64, 16128, Alu.mult,
                                Alu.add)
                v.tensor_scalar(v_tB[...], v_tm[...], 4, 16128, Alu.min,
                                Alu.mult)
                v.tensor_tensor(v_tB[...], v_tA[...], v_tB[...], Alu.min)
                v.tensor_scalar(v_tm[...], csrc[...], 8, 12, Alu.bitwise_and,
                                Alu.logical_shift_left)
                v.tensor_tensor(v_tA[...], v_tB[...], v_tm[...], Alu.bitwise_or)
                last = v.tensor_tensor(wdst[...], v_tA[...].bitcast(BF16),
                                       scb[...], Alu.mult)
            return last

        def decode2(v, pkap, w_lo, w_hi, scb):
            pv = pkap.bitcast(U16)[:, 0:2048:2]
            v.tensor_scalar(v_clo2[...], pv, 15, None, Alu.bitwise_and)
            v.tensor_scalar(v_chi2[...], pv, 4, 15, Alu.logical_shift_right,
                            Alu.bitwise_and)
            last = None
            for csrc, wdst in ((v_clo2, w_lo), (v_chi2, w_hi)):
                v.tensor_scalar(v_tm2[...], csrc[...], 7, None,
                                Alu.bitwise_and)
                v.tensor_scalar(v_tA2[...], v_tm2[...], 64, 16128, Alu.mult,
                                Alu.add)
                v.tensor_scalar(v_tB2[...], v_tm2[...], 4, 16128, Alu.min,
                                Alu.mult)
                v.tensor_tensor(v_tB2[...], v_tA2[...], v_tB2[...], Alu.min)
                v.tensor_scalar(v_tm2[...], csrc[...], 8, 12, Alu.bitwise_and,
                                Alu.logical_shift_left)
                v.tensor_tensor(v_tA2[...], v_tB2[...], v_tm2[...],
                                Alu.bitwise_or)
                last = v.tensor_tensor(wdst, v_tA2[...].bitcast(BF16),
                                       scb, Alu.mult)
            return last

        def rope_emit(v, psum_ap, outb_ap, cs, sn, t1v, t2v):
            csb = cs.unsqueeze(1).broadcast_to([128, 4, 64])
            snb = sn.unsqueeze(1).broadcast_to([128, 4, 64])
            p3 = psum_ap.rearrange("p (h r) -> p h r", h=4)
            x1 = p3[:, :, 0:64]
            x2 = p3[:, :, 64:128]
            o3 = outb_ap.rearrange("p (h r) -> p h r", h=4)
            t1 = t1v[...].rearrange("p (h r) -> p h r", h=4)[:, :, 0:64]
            t2 = t2v[...].rearrange("p (h r) -> p h r", h=4)[:, :, 0:64]
            v.tensor_tensor(t1, x1, csb, Alu.mult)
            v.tensor_tensor(t2, x2, snb, Alu.mult)
            v.tensor_tensor(o3[:, :, 0:64], t1, t2, Alu.subtract)
            v.tensor_tensor(t1, x1, snb, Alu.mult)
            v.tensor_tensor(t2, x2, csb, Alu.mult)
            return v.tensor_tensor(o3[:, :, 64:128], t1, t2, Alu.add)

        def rdv(v, src_ap, w=1):
            v.tensor_copy(jnk[:, 6:7], jnk[:, 7:8])
            return v.tensor_copy(jnk[:, 0:w], src_ap)

        def rac(a, src_ap, w=1):
            a.activation(ajk[:, 6:7], ajk[:, 7:8], AF.Identity)
            return a.activation(ajk[:, 0:w], src_ap, AF.Identity)

        def vgap(v):
            v.tensor_copy(stat[:, 31:32], stat[:, 30:31])

        def ln_dv_stats(v, xsap):
            v.reduce_sum(sv(0), xsap, AXX)
            vgap(v)
            v.tensor_scalar(sv(1), sv(0), 1.0 / D, None, Alu.mult)
            vgap(v)
            v.tensor_scalar(sv(2), sv(1), -1.0, None, Alu.mult)

        def ln_dv_var(v):
            v.tensor_scalar(sv(4), sv(3), 1.0 / D, None, Alu.mult)
            v.tensor_tensor(sv(5), sv(1), sv(1), Alu.mult)
            vgap(v)
            v.scalar_tensor_tensor(sv(6), sv(4), EPS, sv(5), Alu.add,
                                   Alu.subtract)

        def ln_ac_finish(a, xsap):
            a.activation(sv(7), sv(6), AF.Ln)
            a.activation(stat[:, 29:30], stat[:, 28:29], AF.Identity)
            a.activation(sv(8), sv(7), AF.Exp, scale=-0.5)
            a.activation(v_xsq[...], xsap, AF.Identity, bias=sv(2))
            return a.activation(v_xn[...], v_xsq[...], AF.Identity,
                                scale=sv(8))

        def xnt_transposes(t, dst_bf_view):
            last = None
            for ci in range(16):
                off = 256 * (ci % 8) + (ci // 8)
                last = t.transpose(dst_bf_view[:, ci * 128:(ci + 1) * 128],
                                   v_xn[:, off:off + 256 - (ci // 8):2],
                                   s_idn[...])
            return last

        with nc.Block() as blk:
            @blk.sync
            def _(e):
                for dst, src in [(s_idn, din["idn"]),
                                 (s_cosq, din["cosq"]), (s_sinq, din["sinq"]),
                                 (v_cosk, din["cosk"]), (v_sink, din["sink"])]:
                    e.dma_start(dst[...], src[...]).then_inc(SM["ld"].h, 16)
                SM["ld"].v += 80

            @blk.gpsimd
            def _(e):
                e.dma_start(v_wkb[...].rearrange("p (hc n) -> p hc n", hc=16),
                            din["wk"][...].rearrange("(hc p) n -> p hc n",
                                                     p=128)
                            ).then_inc(SM["lp"].h, 16)
                e.dma_start(v_wvb[...].rearrange("p (hc n) -> p hc n", hc=16),
                            din["wv"][...].rearrange("(hc p) n -> p hc n",
                                                     p=128)
                            ).then_inc(SM["lp"].h, 16)
                SM["lp"].v += 32

            @blk.vector
            def _(e):
                for buf in (stat, mxc, smc, nfc, csum, gnbt, jnk, ajk):
                    e.memset(buf[...], 0.0)
                e.wait_ge(SM["ld"].h, SM["ld"].v)
                e.wait_ge(SM["lp"].h, SM["lp"].v)

        NTC = 32
        bs = {k: s.v for k, s in SM.items()}
        with nc.Block() as blk:
            @blk.sync
            def _(e):
                for tc in range(NTC):
                    if tc >= 2:
                        e.wait_ge(SM["ac"].h, bs["ac"] + (tc - 2) * 2 + 2)
                    xsn = "xs" + str(tc % 2)
                    e.dma_start(v_xs[tc % 2][...],
                                din["xseq"][tc * 128:(tc + 1) * 128, :]
                                ).then_inc(SM[xsn].h, 16)
                    SM[xsn].v += 16
                e.wait_ge(SM["dv2"].h, bs["dv2"] + NTC * 2)
                e.wait_ge(SM["ac2"].h, bs["ac2"] + NTC)
                e.dma_start(kvsp[:, 0:16384], v_kt[...]).then_inc(SM["sp"].h, 16)
                e.dma_start(kvsp[:, 16384:32768], v_v[...]).then_inc(
                    SM["sp"].h, 16)
                SM["sp"].v += 32
                if taps:
                    e.wait_ge(SM["sp"].h, SM["sp"].v)
                    e.dma_start(tapd["t_kt"][...], v_kt[...]).then_inc(
                        SM["sp"].h, 16)
                    e.dma_start(tapd["t_v"][...], v_v[...]).then_inc(
                        SM["sp"].h, 16)
                    SM["sp"].v += 32

            @blk.vector
            def _(e):
                for tc in range(NTC):
                    e.wait_ge(SM["xs" + str(tc % 2)].h,
                              bs["xs" + str(tc % 2)] + (tc // 2 + 1) * 16)
                    if tc >= 1:
                        e.wait_ge(SM["ac"].h, bs["ac"] + tc * 2)
                    ln_dv_stats(e, v_xs[tc % 2][...])
                    e.wait_ge(SM["ac"].h, bs["ac"] + tc * 2 + 1)
                    ln_dv_var(e)
                    rdv(e, stat[:, 0:8], 8).then_inc(SM["dv"].h, 1)
                    SM["dv"].v += 1
                    e.wait_ge(SM["pe2"].h, bs["pe2"] + tc * 2 + 1)
                    if tc >= 1:
                        e.wait_ge(SM["pe3"].h, bs["pe3"] + tc)
                    rope_emit(e, pB[:, 0:512],
                              v_krope[...],
                              v_cosk[:, tc * 64:(tc + 1) * 64],
                              v_sink[:, tc * 64:(tc + 1) * 64],
                              v_rt1, v_rt2).then_inc(SM["dv2"].h, 1)
                    SM["dv2"].v += 1
                    e.wait_ge(SM["pe3"].h, bs["pe3"] + tc + 1)
                    dst = v_kt[...].rearrange("p (h t) -> p h t", h=4)[
                        :, :, tc * 128:(tc + 1) * 128]
                    e.tensor_copy(dst, pBb[:, 3072:3584].rearrange(
                        "p (h t) -> p h t", h=4)).then_inc(SM["dv2"].h, 1)
                    SM["dv2"].v += 1

            @blk.scalar
            def _(e):
                for tc in range(NTC):
                    e.wait_ge(SM["xs" + str(tc % 2)].h,
                              bs["xs" + str(tc % 2)] + (tc // 2 + 1) * 16)
                    e.activation(v_xsq[...], v_xs[tc % 2][...], AF.Square,
                                 accum_out=sv(3))
                    rac(e, sv(3)).then_inc(SM["ac"].h, 1)
                    SM["ac"].v += 1
                    e.wait_ge(SM["dv"].h, bs["dv"] + tc + 1)
                    if tc >= 1:
                        e.wait_ge(SM["pe"].h, bs["pe"] + tc)
                    ln_ac_finish(e, v_xs[tc % 2][...]).then_inc(SM["ac"].h, 1)
                    SM["ac"].v += 1
                    e.wait_ge(SM["pe"].h, bs["pe"] + tc + 1)
                    e.activation(v_xntc[...], pAb[:, 0:2048],
                                 AF.Copy).then_inc(SM["ac3"].h, 1)
                    SM["ac3"].v += 1
                    e.wait_ge(SM["pe2"].h, bs["pe2"] + tc * 2 + 2)
                    e.activation(v_v[:, tc * 512:(tc + 1) * 512],
                                 pB[:, 512:1024], AF.Copy
                                 ).then_inc(SM["ac2"].h, 1)
                    SM["ac2"].v += 1

            @blk.tensor
            def _(e):
                for tc in range(NTC):
                    e.wait_ge(SM["ac"].h, bs["ac"] + tc * 2 + 2)
                    if tc >= 1:
                        e.wait_ge(SM["ac3"].h, bs["ac3"] + tc)
                    ins_ = xnt_transposes(e, pAb)
                    ins_.then_inc(SM["pe"].h, 1)
                    SM["pe"].v += 1
                    e.wait_ge(SM["ac3"].h, bs["ac3"] + tc + 1)
                    if tc >= 1:
                        e.wait_ge(SM["dv2"].h, bs["dv2"] + tc * 2)
                        e.wait_ge(SM["ac2"].h, bs["ac2"] + tc)
                    for ci in range(16):
                        ins_ = e.matmul(pB[:, 0:512],
                                        v_xntc[:, ci * 128:(ci + 1) * 128],
                                        v_wkb[:, ci * 512:(ci + 1) * 512],
                                        start=(ci == 0), stop=(ci == 15))
                    ins_.then_inc(SM["pe2"].h, 1)
                    for ci in range(16):
                        ins_ = e.matmul(pB[:, 512:1024],
                                        v_xntc[:, ci * 128:(ci + 1) * 128],
                                        v_wvb[:, ci * 512:(ci + 1) * 512],
                                        start=(ci == 0), stop=(ci == 15))
                    ins_.then_inc(SM["pe2"].h, 1)
                    SM["pe2"].v += 2
                    e.wait_ge(SM["dv2"].h, bs["dv2"] + tc * 2 + 1)
                    for h in range(4):
                        ins_ = e.transpose(
                            pBb[:, 3072 + h * 128:3072 + (h + 1) * 128],
                            v_krope[:, h * 128:(h + 1) * 128], s_idn[...])
                    ins_.then_inc(SM["pe3"].h, 1)
                    SM["pe3"].v += 1

        assert SM["dv2"].v == bs["dv2"] + 64 and SM["ac"].v == bs["ac"] + 64

        atsp = nc.dram_tensor("atsp", [2, 8, 128, 2048], BF16, kind="Internal")
        v_attstg = V(180, 4, BF16)
        v_atb = [V(160, 4, BF16), V(164, 4, BF16)]
        v_xo2 = [V(96, 2, F32), V(98, 2, F32)]
        v_ob2 = [V(100, 2, F32), V(102, 2, F32)]
        v_diag = [V(122, 1, BF16), V(123, 1, BF16)]
        v_ptb2 = [V(112, 1, BF16), V(124, 1, BF16)]

        bs = {k: s.v for k, s in SM.items()}
        with nc.Block() as blk:
            @blk.sync
            def _(e):
                dma_fence(e, bs)
                for tl in range(16):
                    if tl >= 2:
                        e.wait_ge(SM["ac"].h, bs["ac"] + (tl - 2) * 2 + 2)
                    xsn = "xs" + str(tl % 2)
                    e.dma_start(v_xs[tl % 2][...],
                                din["xown"][tl * 128:(tl + 1) * 128, :]
                                ).then_inc(SM[xsn].h, 16)
                    SM[xsn].v += 16

            @blk.vector
            def _(e):
                for tl in range(16):
                    e.wait_ge(SM["xs" + str(tl % 2)].h,
                              bs["xs" + str(tl % 2)] + (tl // 2 + 1) * 16)
                    if tl >= 1:
                        e.wait_ge(SM["ac"].h, bs["ac"] + tl * 2)
                    ln_dv_stats(e, v_xs[tl % 2][...])
                    e.wait_ge(SM["ac"].h, bs["ac"] + tl * 2 + 1)
                    ln_dv_var(e)
                    rdv(e, stat[:, 0:8], 8).then_inc(SM["dv"].h, 1)
                    SM["dv"].v += 1

            @blk.scalar
            def _(e):
                for tl in range(16):
                    e.wait_ge(SM["xs" + str(tl % 2)].h,
                              bs["xs" + str(tl % 2)] + (tl // 2 + 1) * 16)
                    e.activation(v_xsq[...], v_xs[tl % 2][...], AF.Square,
                                 accum_out=sv(3))
                    rac(e, sv(3)).then_inc(SM["ac"].h, 1)
                    SM["ac"].v += 1
                    e.wait_ge(SM["dv"].h, bs["dv"] + tl + 1)
                    if tl >= 1:
                        e.wait_ge(SM["pe"].h, bs["pe"] + tl)
                    ln_ac_finish(e, v_xs[tl % 2][...]).then_inc(SM["ac"].h, 1)
                    SM["ac"].v += 1
                    e.wait_ge(SM["pe"].h, bs["pe"] + tl + 1)
                    dst = v_xnt2[...].rearrange("p (ci t) -> p ci t", ci=16)[
                        :, :, tl * 128:(tl + 1) * 128]
                    e.activation(dst, pAb[:, 0:2048].rearrange(
                        "p (ci k) -> p ci k", ci=16), AF.Copy
                        ).then_inc(SM["ac3"].h, 1)
                    SM["ac3"].v += 1

            @blk.tensor
            def _(e):
                for tl in range(16):
                    e.wait_ge(SM["ac"].h, bs["ac"] + tl * 2 + 2)
                    if tl >= 1:
                        e.wait_ge(SM["ac3"].h, bs["ac3"] + tl)
                    xnt_transposes(e, pAb).then_inc(SM["pe"].h, 1)
                    SM["pe"].v += 1

        bs = {k: s.v for k, s in SM.items()}
        NI1 = 128
        dv3c = 0
        DEC_G, DEC_U, DEC_D = [], [], {}
        for i in range(NI1):
            dv3c += 1
            DEC_G.append(dv3c)
            dv3c += 1
            DEC_U.append(dv3c)
            if i % 2 == 1:
                dv3c += 1
                DEC_D[i] = dv3c

        with nc.Block() as blk:
            @blk.sync
            def _(e):
                dma_fence(e, bs)
                for j in (0, 1):
                    e.dma_start(v_pgb2[j][...], din["pg2"][j]
                                ).then_inc(SM["pg" + str(j)].h, 16)
                    SM["pg" + str(j)].v += 16
                    e.dma_start(v_pub2[j][...], din["pu2"][j]
                                ).then_inc(SM["pu" + str(j)].h, 16)
                    SM["pu" + str(j)].v += 16
                for i in range(NI1):
                    if i + 2 < NI1:
                        e.wait_ge(SM["dv3"].h, bs["dv3"] + DEC_G[i])
                        e.dma_start(v_pgb2[i % 2][...], din["pg2"][i + 2]
                                    ).then_inc(SM["pg" + str(i % 2)].h, 16)
                        SM["pg" + str(i % 2)].v += 16
                        e.wait_ge(SM["dv3"].h, bs["dv3"] + DEC_U[i])
                        e.dma_start(v_pub2[i % 2][...], din["pu2"][i + 2]
                                    ).then_inc(SM["pu" + str(i % 2)].h, 16)
                        SM["pu" + str(i % 2)].v += 16
                    if i >= 1:
                        e.wait_ge(SM["dv"].h, bs["dv"] + i)
                        spn = "spi" + str((i - 1) % 2)
                        e.dma_start(intsp[(i - 1) // 2, (i - 1) % 2],
                                    v_intb2[(i - 1) % 2][...]
                                    ).then_inc(SM[spn].h, 16)
                        SM[spn].v += 16
                    if i % 2 == 1:
                        e.wait_ge(SM["dv3"].h, bs["dv3"] + DEC_D[i])
                        e.dma_start(wdsp[i // 2, 0], v_wde2[...]
                                    ).then_inc(SM["spw"].h, 16)
                        e.dma_start(wdsp[i // 2, 1], v_wdo2[...]
                                    ).then_inc(SM["spw"].h, 16)
                        SM["spw"].v += 32
                e.wait_ge(SM["dv"].h, bs["dv"] + NI1)
                e.dma_start(intsp[63, 1], v_intb2[1][...]
                            ).then_inc(SM["spi1"].h, 16)
                SM["spi1"].v += 16

            @blk.gpsimd
            def _(e):
                dma_fence(e, bs)
                for j in (0, 1):
                    e.dma_start(v_scg2[j][...], din["sg2"][j]
                                ).then_inc(SM["sg" + str(j)].h, 16)
                    SM["sg" + str(j)].v += 16
                    e.dma_start(v_scu2[j][...], din["su2"][j]
                                ).then_inc(SM["su" + str(j)].h, 16)
                    SM["su" + str(j)].v += 16
                e.dma_start(v_pdb2[...], din["pd"][0]
                            ).then_inc(SM["pdl"].h, 16)
                SM["pdl"].v += 16
                e.dma_start(v_scd2[...], din["sd2"][0]
                            ).then_inc(SM["sdl"].h, 16)
                SM["sdl"].v += 16
                for i in range(NI1):
                    if i + 2 < NI1:
                        e.wait_ge(SM["dv3"].h, bs["dv3"] + DEC_G[i])
                        e.dma_start(v_scg2[i % 2][...], din["sg2"][i + 2]
                                    ).then_inc(SM["sg" + str(i % 2)].h, 16)
                        SM["sg" + str(i % 2)].v += 16
                        e.wait_ge(SM["dv3"].h, bs["dv3"] + DEC_U[i])
                        e.dma_start(v_scu2[i % 2][...], din["su2"][i + 2]
                                    ).then_inc(SM["su" + str(i % 2)].h, 16)
                        SM["su" + str(i % 2)].v += 16
                    if i % 2 == 1 and i // 2 + 1 < NW:
                        e.wait_ge(SM["dv3"].h, bs["dv3"] + DEC_D[i])
                        e.dma_start(v_pdb2[...], din["pd"][i // 2 + 1]
                                    ).then_inc(SM["pdl"].h, 16)
                        SM["pdl"].v += 16
                        e.dma_start(v_scd2[...], din["sd2"][i // 2 + 1]
                                    ).then_inc(SM["sdl"].h, 16)
                        SM["sdl"].v += 16

            @blk.tensor
            def _(e):
                for i in range(NI1):
                    wge, wgo = v_wge2[i % 2], v_wgo2[i % 2]
                    wue, wuo = v_wue2[i % 2], v_wuo2[i % 2]
                    e.wait_ge(SM["dv3"].h, bs["dv3"] + DEC_G[i])
                    if i >= 1:
                        e.wait_ge(SM["ac2"].h, bs["ac2"] + 2 * (i - 1) + 1)
                    for tp in range(4):
                        for ci in range(16):
                            wsrc = (wge if ci < 8 else wgo)
                            c8 = ci % 8
                            ins_ = e.matmul(
                                pB[:, tp * 512:(tp + 1) * 512],
                                wsrc[:, c8 * 128:(c8 + 1) * 128],
                                v_xnt2[:, ci * 2048 + tp * 512:
                                       ci * 2048 + (tp + 1) * 512],
                                start=(ci == 0), stop=(ci == 15))
                    ins_.then_inc(SM["pe"].h, 1)
                    SM["pe"].v += 1
                    e.wait_ge(SM["dv3"].h, bs["dv3"] + DEC_U[i])
                    if i >= 1:
                        e.wait_ge(SM["ac2"].h, bs["ac2"] + 2 * (i - 1) + 2)
                    for tp in range(4):
                        for ci in range(16):
                            wsrc = (wue if ci < 8 else wuo)
                            c8 = ci % 8
                            ins_ = e.matmul(
                                pA[:, tp * 512:(tp + 1) * 512],
                                wsrc[:, c8 * 128:(c8 + 1) * 128],
                                v_xnt2[:, ci * 2048 + tp * 512:
                                       ci * 2048 + (tp + 1) * 512],
                                start=(ci == 0), stop=(ci == 15))
                    ins_.then_inc(SM["pe"].h, 1)
                    SM["pe"].v += 1

            @blk.scalar
            def _(e):
                for i in range(NI1):
                    e.wait_ge(SM["pe"].h, bs["pe"] + 2 * i + 1)
                    if i >= 1:
                        e.wait_ge(SM["dv"].h, bs["dv"] + i)
                    e.activation(v_gsb2[...], pB[...], AF.Gelu_apprx_tanh
                                 ).then_inc(SM["ac2"].h, 1)
                    SM["ac2"].v += 1
                    e.wait_ge(SM["pe"].h, bs["pe"] + 2 * i + 2)
                    e.activation(v_upsb2[...], pA[...], AF.Copy
                                 ).then_inc(SM["ac2"].h, 1)
                    SM["ac2"].v += 1

            @blk.vector
            def _(e):
                for i in range(NI1):
                    p = i % 2
                    e.wait_ge(SM["pg" + str(p)].h,
                              bs["pg" + str(p)] + (i // 2 + 1) * 16)
                    e.wait_ge(SM["sg" + str(p)].h,
                              bs["sg" + str(p)] + (i // 2 + 1) * 16)
                    if i >= 2:
                        e.wait_ge(SM["pe"].h, bs["pe"] + 2 * (i - 2) + 1)
                    decode2(e, v_pgb2[p][...], v_wge2[p][...],
                            v_wgo2[p][...], v_scg2[p][...]
                            ).then_inc(SM["dv3"].h, 1)
                    SM["dv3"].v += 1
                    e.wait_ge(SM["pu" + str(p)].h,
                              bs["pu" + str(p)] + (i // 2 + 1) * 16)
                    e.wait_ge(SM["su" + str(p)].h,
                              bs["su" + str(p)] + (i // 2 + 1) * 16)
                    if i >= 2:
                        e.wait_ge(SM["pe"].h, bs["pe"] + 2 * (i - 2) + 2)
                    decode2(e, v_pub2[p][...], v_wue2[p][...],
                            v_wuo2[p][...], v_scu2[p][...]
                            ).then_inc(SM["dv3"].h, 1)
                    SM["dv3"].v += 1
                    if i % 2 == 1:
                        wd = i // 2
                        e.wait_ge(SM["pdl"].h, bs["pdl"] + (wd + 1) * 16)
                        e.wait_ge(SM["sdl"].h, bs["sdl"] + (wd + 1) * 16)
                        if wd >= 1:
                            e.wait_ge(SM["spw"].h, bs["spw"] + wd * 32)
                        for hf in range(2):
                            sl = slice(hf * 1024, (hf + 1) * 1024)
                            ins_ = decode2(e, v_pdb2[:, sl], v_wde2[:, sl],
                                           v_wdo2[:, sl], v_scd2[:, sl])
                        ins_.then_inc(SM["dv3"].h, 1)
                        SM["dv3"].v += 1
                    if i >= 1:
                        e.wait_ge(SM["ac2"].h, bs["ac2"] + 2 * (i - 1) + 2)
                        if i - 1 >= 2:
                            e.wait_ge(SM["spi" + str((i - 1) % 2)].h,
                                      bs["spi" + str((i - 1) % 2)] +
                                      ((i - 1) // 2) * 16)
                        e.tensor_tensor(v_intb2[(i - 1) % 2][...],
                                        v_gsb2[...], v_upsb2[...], Alu.mult
                                        ).then_inc(SM["dv"].h, 1)
                        SM["dv"].v += 1
                e.wait_ge(SM["ac2"].h, bs["ac2"] + 2 * (NI1 - 1) + 2)
                e.wait_ge(SM["spi1"].h, bs["spi1"] + ((NI1 - 1) // 2) * 16)
                e.tensor_tensor(v_intb2[(NI1 - 1) % 2][...],
                                v_gsb2[...], v_upsb2[...], Alu.mult
                                ).then_inc(SM["dv"].h, 1)
                SM["dv"].v += 1

        def emit_half(h):
            bs = {k: s.v for k, s in SM.items()}
            with nc.Block() as blk:
                @blk.sync
                def _(e):
                    dma_fence(e, bs)
                    for tl in range(8):
                        if tl >= 2:
                            e.wait_ge(SM["ac"].h, bs["ac"] + (tl - 2) * 2 + 2)
                        r0 = h * 1024 + tl * 128
                        xsn = "xs" + str(tl % 2)
                        e.dma_start(v_xs[tl % 2][...],
                                    din["xown"][r0:r0 + 128, :]
                                    ).then_inc(SM[xsn].h, 16)
                        SM[xsn].v += 16

                @blk.vector
                def _(e):
                    for tl in range(8):
                        e.wait_ge(SM["xs" + str(tl % 2)].h,
                                  bs["xs" + str(tl % 2)] + (tl // 2 + 1) * 16)
                        if tl >= 1:
                            e.wait_ge(SM["ac"].h, bs["ac"] + tl * 2)
                        ln_dv_stats(e, v_xs[tl % 2][...])
                        e.wait_ge(SM["ac"].h, bs["ac"] + tl * 2 + 1)
                        ln_dv_var(e)
                        rdv(e, stat[:, 0:8], 8).then_inc(SM["dv"].h, 1)
                        SM["dv"].v += 1

                @blk.scalar
                def _(e):
                    for tl in range(8):
                        e.wait_ge(SM["xs" + str(tl % 2)].h,
                                  bs["xs" + str(tl % 2)] + (tl // 2 + 1) * 16)
                        e.activation(v_xsq[...], v_xs[tl % 2][...], AF.Square,
                                     accum_out=sv(3))
                        rac(e, sv(3)).then_inc(SM["ac"].h, 1)
                        SM["ac"].v += 1
                        e.wait_ge(SM["dv"].h, bs["dv"] + tl + 1)
                        if tl >= 1:
                            e.wait_ge(SM["pe"].h, bs["pe"] + tl)
                        ln_ac_finish(e, v_xs[tl % 2][...]
                                     ).then_inc(SM["ac"].h, 1)
                        SM["ac"].v += 1
                        e.wait_ge(SM["pe"].h, bs["pe"] + tl + 1)
                        dst = v_xnt[...].rearrange("p (ci t) -> p ci t", ci=16)[
                            :, :, tl * 128:(tl + 1) * 128]
                        e.activation(dst, pAb[:, 0:2048].rearrange(
                            "p (ci k) -> p ci k", ci=16), AF.Copy
                            ).then_inc(SM["ac3"].h, 1)
                        SM["ac3"].v += 1

                @blk.tensor
                def _(e):
                    for tl in range(8):
                        e.wait_ge(SM["ac"].h, bs["ac"] + tl * 2 + 2)
                        if tl >= 1:
                            e.wait_ge(SM["ac3"].h, bs["ac3"] + tl)
                        ins_ = xnt_transposes(e, pAb)
                        ins_.then_inc(SM["pe"].h, 1)
                        SM["pe"].v += 1

                @blk.gpsimd
                def _(e):
                    if taps and h == 0:
                        e.wait_ge(SM["ac3"].h, bs["ac3"] + 8)
                        e.dma_start(tapd["t_xnt"][...], v_xnt[...]
                                    ).then_inc(SM["os"].h, 16)
                        SM["os"].v += 16

            bs = {k: s.v for k, s in SM.items()}

            def f2_it_load(e, i, cc, w):
                if i >= 2:
                    e.wait_ge(SM["pe"].h, bs["pe"] + i - 1)
                src = intsp[w, :, :, h * 1024:(h + 1) * 1024
                            ].transpose([1, 0, 2])
                e.dma_start(v_itb[i % 2][...].rearrange(
                    "p (eo t) -> p eo t", eo=2), src
                ).then_inc(SM["it" + str(i % 2)].h, 16)
                SM["it" + str(i % 2)].v += 16

            def f2_wd_load(e, i, cc, w):
                if i >= 2:
                    e.wait_ge(SM["pe"].h, bs["pe"] + i - 1)
                src = wdsp[w, :, :, cc * 512:(cc + 1) * 512
                           ].transpose([1, 0, 2])
                e.dma_start(v_wdb[i % 2][...].rearrange(
                    "p (eo c) -> p eo c", eo=2), src
                ).then_inc(SM["wd" + str(i % 2)].h, 16)
                SM["wd" + str(i % 2)].v += 16

            with nc.Block() as blk:
                @blk.sync
                def _(e):
                    dma_fence(e, bs)
                    i = 0
                    for cc in range(4):
                        for w in range(NW):
                            if i % 2 == 0 or not F2_SCALAR_DMA:
                                f2_it_load(e, i, cc, w)
                            i += 1

                @blk.gpsimd
                def _(e):
                    dma_fence(e, bs)
                    i = 0
                    for cc in range(4):
                        for w in range(NW):
                            f2_wd_load(e, i, cc, w)
                            i += 1

                @blk.tensor
                def _(e):
                    i = 0
                    for cc in range(4):
                        for w in range(NW):
                            e.wait_ge(SM["it" + str(i % 2)].h,
                                      bs["it" + str(i % 2)] +
                                      (i // 2 + 1) * 16)
                            e.wait_ge(SM["wd" + str(i % 2)].h,
                                      bs["wd" + str(i % 2)] +
                                      (i // 2 + 1) * 16)
                            if w == 0 and cc > 0:
                                e.wait_ge(SM["ac"].h, bs["ac"] + cc)
                            for eo in range(2):
                                for tt in range(8):
                                    pb = (pA if tt < 4 else pB)
                                    ins_ = e.matmul(
                                        pb[:, (tt % 4) * 512:
                                           (tt % 4 + 1) * 512],
                                        v_itb[i % 2][:, eo * 1024 + tt * 128:
                                                     eo * 1024 + (tt + 1) * 128],
                                        v_wdb[i % 2][:, eo * 512:
                                                     (eo + 1) * 512],
                                        start=(w == 0 and eo == 0),
                                        stop=(w == NW - 1 and eo == 1))
                            ins_.then_inc(SM["pe"].h, 1)
                            SM["pe"].v += 1
                            i += 1

                @blk.scalar
                def _(e):
                    dma_fence(e, bs)
                    i = 0
                    for cc in range(4):
                        for w in range(NW):
                            if i % 2 == 1 and F2_SCALAR_DMA:
                                f2_it_load(e, i, cc, w)
                            i += 1
                        e.wait_ge(SM["pe"].h, bs["pe"] + (cc + 1) * NW)
                        for tt in range(8):
                            pb = (pA if tt < 4 else pB)
                            ins_ = e.activation(
                                v_g[:, tt * 2048 + cc * 512:
                                    tt * 2048 + (cc + 1) * 512],
                                pb[:, (tt % 4) * 512:(tt % 4 + 1) * 512],
                                AF.Silu)
                        ins_.then_inc(SM["ac"].h, 1)
                        SM["ac"].v += 1

                @blk.gpsimd
                def _(e):
                    if taps and h == 0:
                        e.wait_ge(SM["ac"].h, bs["ac"] + 4)
                        e.dma_start(tapd["t_g"][...], v_g[...]
                                    ).then_inc(SM["os"].h, 16)
                        e.dma_start(tapd["t_int"][...], intsp[0, 0]
                                    ).then_inc(SM["os"].h, 16)
                        e.dma_start(tapd["t_wd"][...], wdsp[0, 0]
                                    ).then_inc(SM["os"].h, 16)
                        e.dma_start(tapd["t_wg"][...], intsp[0, 1]
                                    ).then_inc(SM["os"].h, 16)
                        SM["os"].v += 64

            bs = {k: s.v for k, s in SM.items()}
            with nc.Block() as blk:
                @blk.gpsimd
                def _(e):
                    dma_fence(e, bs)
                    for dq in range(4):
                        if dq >= 2:
                            e.wait_ge(SM["pe"].h, bs["pe"] + (dq - 1) * 8)
                        e.dma_start(v_wqb[dq % 2][...].rearrange(
                                        "p (hc n) -> p hc n", hc=16),
                                    din["wq"][:, dq * 512:(dq + 1) * 512
                                              ].rearrange(
                                        "(hc p) n -> p hc n", p=128)
                                    ).then_inc(SM["wq" + str(dq % 2)].h, 16)
                        SM["wq" + str(dq % 2)].v += 16

                @blk.tensor
                def _(e):
                    for dq in range(4):
                        e.wait_ge(SM["wq" + str(dq % 2)].h,
                                  bs["wq" + str(dq % 2)] + (dq // 2 + 1) * 16)
                        for tt in range(8):
                            idx = dq * 8 + tt
                            pslot = pA[:, (idx % 2) * 512:(idx % 2 + 1) * 512]
                            if idx >= 2:
                                e.wait_ge(SM["dv"].h, bs["dv"] + idx - 1)
                            for ci in range(16):
                                ins_ = e.matmul(
                                    pslot,
                                    v_xnt[:, ci * 1024 + tt * 128:
                                          ci * 1024 + (tt + 1) * 128],
                                    v_wqb[dq % 2][:, ci * 512:(ci + 1) * 512],
                                    start=(ci == 0), stop=(ci == 15))
                            ins_.then_inc(SM["pe"].h, 1)
                            SM["pe"].v += 1

                @blk.vector
                def _(e):
                    for dq in range(4):
                        for tt in range(8):
                            idx = dq * 8 + tt
                            e.wait_ge(SM["pe"].h, bs["pe"] + idx + 1)
                            o = 8 * h + tt
                            pslot = pA[:, (idx % 2) * 512:(idx % 2 + 1) * 512]
                            rope_emit(
                                e, pslot,
                                v_q[:, tt * 2048 + dq * 512:
                                    tt * 2048 + (dq + 1) * 512],
                                s_cosq[:, o * 64:(o + 1) * 64],
                                s_sinq[:, o * 64:(o + 1) * 64],
                                v_art1, v_art2).then_inc(SM["dv"].h, 1)
                            SM["dv"].v += 1

                @blk.gpsimd
                def _(e):
                    if taps and h == 0:
                        e.wait_ge(SM["dv"].h, bs["dv"] + 32)
                        e.dma_start(tapd["t_q"][...], v_q[...]
                                    ).then_inc(SM["os"].h, 16)
                        SM["os"].v += 16

            bs = {k: s.v for k, s in SM.items()}
            iters = [(j, hd) for j in range(8) for hd in range(NQH)]
            NI = len(iters)
            nchs = [nch_of(8 * h + j) for (j, hd) in iters]
            ngrps = [(n + 1) // 2 for n in nchs]
            nde = [(n + 1) // 2 for n in nchs]
            nae = [n // 2 for n in nchs]
            PEb = np.cumsum([0] + [1] * NI)
            PE2b = np.cumsum([0] + ngrps)
            PE3b = np.cumsum([0] + [2 * n for n in nchs])
            DVb = np.cumsum([0] + [1] * NI)
            DV2b = np.cumsum([0] + ngrps)
            DV3b = np.cumsum([0] + [1] * NI)
            ACb = np.cumsum([0] + ngrps)
            AC2b = np.cumsum([0] + [1] * NI)
            AC3b = np.cumsum([0] + [1] * NI)
            ACPb = np.cumsum([0] + nae)
            DVEb = np.cumsum([0] + nde)
            stg = [pB[:, 0:512], pB[:, 512:1024]]

            with nc.Block() as blk:
                @blk.sync
                def _(e):
                    dma_fence(e, bs)
                    e.dma_start(v_kt[...], kvsp[:, 0:16384]
                                ).then_inc(SM["ktv"].h, 16)
                    e.dma_start(v_v[...], kvsp[:, 16384:32768]
                                ).then_inc(SM["ktv"].h, 16)
                    SM["ktv"].v += 32
                    for j in (0, 1):
                        e.dma_start(v_mkb[j % 2][...], din["dmask"][8 * h + j]
                                    ).then_inc(SM["mk" + str(j % 2)].h, 16)
                        SM["mk" + str(j % 2)].v += 16
                    for j in range(8):
                        e.wait_ge(SM["ac3"].h,
                                  bs["ac3"] + int(AC3b[j * NQH + NQH]))
                        e.dma_start(atsp[h, j], v_attstg[...]
                                    ).then_inc(SM["spa"].h, 16)
                        SM["spa"].v += 16
                        if j + 2 < 8:
                            e.wait_ge(SM["dv3"].h,
                                      bs["dv3"] + int(DV3b[(j + 1) * NQH]))
                            e.dma_start(v_mkb[j % 2][...],
                                        din["dmask"][8 * h + j + 2]
                                        ).then_inc(SM["mk" + str(j % 2)].h, 16)
                            SM["mk" + str(j % 2)].v += 16
                    if taps and h == 0:
                        e.wait_ge(SM["spa"].h, bs["spa"] + 128)
                        e.dma_start(
                            tapd["t_at"][...].rearrange("p (j n) -> (p j) n",
                                                        j=8),
                            atsp[0]
                        ).then_inc(SM["os"].h, 16)
                        SM["os"].v += 16

                @blk.tensor
                def _(e):
                    def evac_wait(o, c):
                        if c % 2 == 0:
                            e.wait_ge(SM["dve"].h, bs["dve"] + int(DVEb[o]) +
                                      c // 2 + 1)
                        else:
                            e.wait_ge(SM["acp"].h,
                                      bs["acp"] + int(ACPb[o]) + (c + 1) // 2)

                    def pt_emit(o, c):
                        g = c // 2
                        stgb = stg[c % 2]
                        for kt in range(4):
                            ins2 = e.matmul(
                                stgb[:, kt * 128:(kt + 1) * 128],
                                v_p[o % 2][:, c * 512 + kt * 128:
                                           c * 512 + (kt + 1) * 128],
                                v_diag[o % 2][:, g * 128:(g + 1) * 128],
                                start=True, stop=True)
                        return ins2

                    e.wait_ge(SM["ktv"].h, bs["ktv"] + 32)
                    for it in range(NI + 1):
                        if it < NI:
                            j, hd = iters[it]
                            nch, ngrp = nchs[it], ngrps[it]
                            kv = hd // 4
                            if it >= 2:
                                e.wait_ge(SM["qte"].h, bs["qte"] + it - 1)
                            qs = 3072 + 256 * (it % 2)
                            ins_ = e.transpose(
                                pBb[:, qs:qs + 128],
                                v_q[:, j * 2048 + hd * 128:
                                    j * 2048 + (hd + 1) * 128],
                                s_idn[...])
                            ins_.then_inc(SM["pe"].h, 1)
                            SM["pe"].v += 1
                            e.wait_ge(SM["qte"].h, bs["qte"] + it + 1)
                            for gi in range(ngrp):
                                half = gi % 2
                                if gi >= 2:
                                    e.wait_ge(SM["ac"].h,
                                              bs["ac"] + int(ACb[it]) + gi - 1)
                                elif it >= 1:
                                    e.wait_ge(SM["ac"].h,
                                              bs["ac"] + int(ACb[it]))
                                for c in range(gi * 2, min(nch, gi * 2 + 2)):
                                    ins_ = e.matmul(
                                        pA[:, half * 1024 + (c - gi * 2) * 512:
                                           half * 1024 +
                                           (c - gi * 2 + 1) * 512],
                                        v_qt[it % 2][...],
                                        v_kt[:, kv * 4096 + c * 512:
                                             kv * 4096 + (c + 1) * 512],
                                        start=True, stop=True)
                                ins_.then_inc(SM["pe2"].h, 1)
                                SM["pe2"].v += 1
                        if it >= 1:
                            o = it - 1
                            j, hd = iters[o]
                            nch, ngrp = nchs[o], ngrps[o]
                            kv = hd // 4
                            e.wait_ge(SM["dv3"].h, bs["dv3"] + int(DV3b[o]) + 1)
                            e.wait_ge(SM["ac3"].h, bs["ac3"] + int(AC3b[o]))
                            if o >= 1:
                                e.wait_ge(SM["dve"].h, bs["dve"] + int(DVEb[o]))
                                e.wait_ge(SM["acp"].h,
                                          bs["acp"] + int(ACPb[o]))
                            pt_emit(o, 0).then_inc(SM["pe3"].h, 1)
                            SM["pe3"].v += 1
                            for c in range(nch):
                                if c + 1 < nch:
                                    if c >= 1:
                                        evac_wait(o, c - 1)
                                    pt_emit(o, c + 1).then_inc(SM["pe3"].h, 1)
                                    SM["pe3"].v += 1
                                evac_wait(o, c)
                                for kt in range(4):
                                    kg = c * 4 + kt
                                    ins_ = e.matmul(
                                        pB[:, 1024:1152],
                                        v_v[:, kg * 512 + kv * 128:
                                            kg * 512 + (kv + 1) * 128],
                                        v_ptb2[c % 2][:, kt * 128:
                                                      (kt + 1) * 128],
                                        start=(c == 0 and kt == 0),
                                        stop=(c == nch - 1 and kt == 3))
                                ins_.then_inc(SM["pe3"].h, 1)
                                SM["pe3"].v += 1

                @blk.vector
                def _(e):
                    def ev_even(o, c):
                        e.wait_ge(SM["pe3"].h, bs["pe3"] + int(PE3b[o]) +
                                  (1 if c == 0 else 2 * c))
                        e.tensor_copy(v_ptb2[0][...], stg[0][...]
                                      ).then_inc(SM["dve"].h, 1)
                        SM["dve"].v += 1

                    for it in range(NI + 1):
                        o = it - 1
                        if it == NI:
                            for c in range(0, nchs[o], 2):
                                ev_even(o, c)
                            break
                        j, hd = iters[it]
                        nch, ngrp = nchs[it], ngrps[it]
                        e.wait_ge(SM["mk" + str(j % 2)].h,
                                  bs["mk" + str(j % 2)] + (j // 2 + 1) * 16)
                        wn = min(2, nch)
                        for gi in range(ngrp):
                            half = gi % 2
                            e.wait_ge(SM["pe2"].h,
                                      bs["pe2"] + int(PE2b[it]) + gi + 1)
                            lim = min(nch, gi * 2 + 2)
                            for c in range(gi * 2, lim):
                                if c >= nch - wn:
                                    ps = pA[:, half * 1024 +
                                            (c - gi * 2) * 512:
                                            half * 1024 +
                                            (c - gi * 2 + 1) * 512]
                                    w2 = c - (nch - wn)
                                    e.scalar_tensor_tensor(
                                        ps, ps, 1.0,
                                        v_mkb[j % 2][:, w2 * 512:
                                                     (w2 + 1) * 512],
                                        Alu.mult, Alu.add)
                            width = lim - gi * 2
                            e.reduce_max(smc[:, gi:gi + 1],
                                         pA[:, half * 1024:
                                            half * 1024 + width * 512], AXX)
                            vgap(e)
                            e.tensor_scalar(gnbt[:, gi:gi + 1],
                                            smc[:, gi:gi + 1], -ISQ, None,
                                            Alu.mult)
                            rdv(e, gnbt[:, gi:gi + 1]
                                ).then_inc(SM["dv2"].h, 1)
                            SM["dv2"].v += 1
                        e.reduce_max(sv(9), smc[:, 0:ngrp], AXX)
                        vgap(e)
                        e.tensor_scalar(sv(10), sv(9), -ISQ, None, Alu.mult)
                        rdv(e, sv(10)).then_inc(SM["dv"].h, 1)
                        SM["dv"].v += 1
                        if o >= 0:
                            ev_even(o, 0)
                        e.wait_ge(SM["ac2"].h, bs["ac2"] + int(AC2b[it]) + 1)
                        e.tensor_tensor(smc[:, 4:4 + ngrp], csum[:, 0:ngrp],
                                        nfc[:, 0:ngrp], Alu.mult)
                        vgap(e)
                        e.reduce_sum(sv(11), smc[:, 4:4 + ngrp], AXX)
                        vgap(e)
                        e.reciprocal(sv(12), sv(11))
                        vgap(e)
                        e.tensor_scalar(nfc[:, 0:ngrp], nfc[:, 0:ngrp],
                                        sv(12), None, Alu.mult)
                        vgap(e)
                        for gi in range(ngrp):
                            ins_ = e.tensor_scalar(
                                v_diag[it % 2][:, gi * 128:(gi + 1) * 128],
                                s_idn[...], nfc[:, gi:gi + 1], None, Alu.mult)
                        ins_.then_inc(SM["dv3"].h, 1)
                        SM["dv3"].v += 1
                        if o >= 0:
                            for c in range(2, nchs[o], 2):
                                ev_even(o, c)

                @blk.scalar
                def _(e):
                    for it in range(NI + 1):
                        if it < NI:
                            j, hd = iters[it]
                            nch, ngrp = nchs[it], ngrps[it]
                            e.wait_ge(SM["pe"].h, bs["pe"] + int(PEb[it]) + 1)
                            if it >= 2:
                                e.wait_ge(SM["pe2"].h,
                                          bs["pe2"] + int(PE2b[it - 1]))
                            qs = 3072 + 256 * (it % 2)
                            e.activation(v_qt[it % 2][...], pBb[:, qs:qs + 128],
                                         AF.Copy).then_inc(SM["qte"].h, 1)
                            SM["qte"].v += 1
                            for gi in range(ngrp):
                                half = gi % 2
                                lim = min(nch, gi * 2 + 2)
                                width = lim - gi * 2
                                e.wait_ge(SM["dv2"].h,
                                          bs["dv2"] + int(DV2b[it]) + gi + 1)
                                if it >= 2 and gi == 0:
                                    e.wait_ge(SM["pe3"].h,
                                              bs["pe3"] + int(PE3b[it - 1]))
                                e.activation(
                                    v_p[it % 2][:, gi * 1024:
                                                gi * 1024 + width * 512],
                                    pA[:, half * 1024:
                                       half * 1024 + width * 512],
                                    AF.Exp, scale=ISQ,
                                    bias=gnbt[:, gi:gi + 1],
                                    accum_out=csum[:, gi:gi + 1])
                                rac(e, csum[:, gi:gi + 1], 1
                                    ).then_inc(SM["ac"].h, 1)
                                SM["ac"].v += 1
                            e.wait_ge(SM["dv"].h, bs["dv"] + int(DVb[it]) + 1)
                            e.activation(nfc[:, 0:ngrp], smc[:, 0:ngrp],
                                         AF.Exp, scale=ISQ, bias=sv(10))
                            rac(e, nfc[:, 0:ngrp], ngrp
                                ).then_inc(SM["ac2"].h, 1)
                            SM["ac2"].v += 1
                        if it >= 1:
                            o = it - 1
                            j, hd = iters[o]
                            nch = nchs[o]
                            for c in range(1, nch, 2):
                                e.wait_ge(SM["pe3"].h,
                                          bs["pe3"] + int(PE3b[o]) + 2 * c)
                                e.activation(v_ptb2[1][...], stg[1][...],
                                             AF.Copy).then_inc(SM["acp"].h, 1)
                                SM["acp"].v += 1
                            e.wait_ge(SM["pe3"].h,
                                      bs["pe3"] + int(PE3b[o]) + 2 * nch)
                            if hd == 0 and j >= 1:
                                e.wait_ge(SM["spa"].h, bs["spa"] + 16 * j)
                            e.activation(v_attstg[:, hd * 128:(hd + 1) * 128],
                                         pB[:, 1024:1152], AF.Copy
                                         ).then_inc(SM["ac3"].h, 1)
                            SM["ac3"].v += 1

            bs = {k: s.v for k, s in SM.items()}
            with nc.Block() as blk:
                @blk.gpsimd
                def _(e):
                    def wob_load(cc):
                        e.dma_start(v_wob[cc % 2][...].rearrange(
                                        "p (hc n) -> p hc n", hc=16),
                                    din["wo"][:, cc * 512:(cc + 1) * 512
                                              ].rearrange(
                                        "(hc p) n -> p hc n", p=128)
                                    ).then_inc(SM["wo" + str(cc % 2)].h, 16)
                        SM["wo" + str(cc % 2)].v += 16
                    dma_fence(e, bs)
                    wob_load(0)
                    wob_load(1)
                    i = 0
                    for cc in range(4):
                        for j in range(8):
                            e.wait_ge(SM["dv"].h, bs["dv"] + i + 1)
                            r0 = h * 1024 + j * 128
                            e.dma_start(
                                out[r0:r0 + 128, cc * 512:(cc + 1) * 512],
                                v_ob2[i % 2][...]
                            ).then_inc(SM["os" + str(i % 2)].h, 16)
                            SM["os" + str(i % 2)].v += 16
                            i += 1
                        if cc + 2 < 4:
                            e.wait_ge(SM["pe"].h, bs["pe"] + (cc + 1) * 8)
                            wob_load(cc + 2)

                @blk.sync
                def _(e):
                    dma_fence(e, bs)
                    i = 0
                    for cc in range(4):
                        for j in range(8):
                            if i >= 2:
                                e.wait_ge(SM["pe"].h, bs["pe"] + i - 1)
                            e.dma_start(v_atb[i % 2][...], atsp[h, j]
                                        ).then_inc(SM["at" + str(i % 2)].h, 16)
                            SM["at" + str(i % 2)].v += 16
                            if i >= 2:
                                e.wait_ge(SM["dv"].h, bs["dv"] + i - 1)
                            r0 = h * 1024 + j * 128
                            e.dma_start(v_xo2[i % 2][...],
                                        din["xown"][r0:r0 + 128,
                                                    cc * 512:(cc + 1) * 512]
                                        ).then_inc(SM["xo" + str(i % 2)].h, 16)
                            SM["xo" + str(i % 2)].v += 16
                            i += 1

                @blk.tensor
                def _(e):
                    i = 0
                    for cc in range(4):
                        e.wait_ge(SM["wo" + str(cc % 2)].h,
                                  bs["wo" + str(cc % 2)] + (cc // 2 + 1) * 16)
                        for j in range(8):
                            pslot = pA[:, (i % 4) * 512:(i % 4 + 1) * 512]
                            e.wait_ge(SM["at" + str(i % 2)].h,
                                      bs["at" + str(i % 2)] +
                                      (i // 2 + 1) * 16)
                            if i >= 4:
                                e.wait_ge(SM["dv"].h, bs["dv"] + i - 3)
                            for ci in range(16):
                                ins_ = e.matmul(
                                    pslot,
                                    v_atb[i % 2][:, ci * 128:(ci + 1) * 128],
                                    v_wob[cc % 2][:, ci * 512:(ci + 1) * 512],
                                    start=(ci == 0), stop=(ci == 15))
                            ins_.then_inc(SM["pe"].h, 1)
                            SM["pe"].v += 1
                            i += 1

                @blk.vector
                def _(e):
                    i = 0
                    for cc in range(4):
                        for j in range(8):
                            e.wait_ge(SM["pe"].h, bs["pe"] + i + 1)
                            e.wait_ge(SM["xo" + str(i % 2)].h,
                                      bs["xo" + str(i % 2)] +
                                      (i // 2 + 1) * 16)
                            if i >= 2:
                                e.wait_ge(SM["os" + str(i % 2)].h,
                                          bs["os" + str(i % 2)] +
                                          (i // 2) * 16)
                            pslot = pA[:, (i % 4) * 512:(i % 4 + 1) * 512]
                            e.tensor_tensor(
                                v_ob2[i % 2][...], pslot,
                                v_g[:, j * 2048 + cc * 512:
                                    j * 2048 + (cc + 1) * 512], Alu.mult)
                            e.tensor_tensor(v_ob2[i % 2][...], v_ob2[i % 2][...],
                                            v_xo2[i % 2][...], Alu.add
                                            ).then_inc(SM["dv"].h, 1)
                            SM["dv"].v += 1
                            i += 1


        emit_half(0)
        emit_half(1)

        with nc.Block() as blk:
            @blk.gpsimd
            def _(e):
                e.wait_ge(SM["os0"].h, SM["os0"].v)
                e.wait_ge(SM["os1"].h, SM["os1"].v)
                e.wait_ge(SM["os"].h, SM["os"].v)
    return nc


def kernel(**inputs):
    import os as _os
    taps = bool(_os.environ.get("MK_TAPS"))
    maps = _host_stage(inputs)
    nc = build_program(taps=taps)
    res = run_bass_kernel_spmd(nc, maps, core_ids=list(range(8)))
    full = np.empty((NB, S, D), np.float32)
    for c in range(8):
        par = c % 2
        full[c // 2].reshape(32, 128, D)[par::2] = \
            np.asarray(res.results[c]["out"]).reshape(NT, 128, D)
    if taps:
        kernel.last_taps = res.results
    return full
